# version 01
# baseline (speedup 1.0000x reference)
import numpy as np
import ml_dtypes
from contextlib import ExitStack

import concourse.bass as bass
import concourse.mybir as mybir
from concourse.bass_utils import run_bass_kernel_spmd

F32 = mybir.dt.float32
BF16 = mybir.dt.bfloat16
I8 = mybir.dt.int8
AF = mybir.ActivationFunctionType
ALU = mybir.AluOpType
AXX = mybir.AxisListType.X

B, T, D = 4, 4096, 2048
TOWN = 2048
NDC = 16
NH = 16
INTER = 8192
NIC = 64
E, R = 8, 16
RSQ = float(1.0 / np.sqrt(128.0))
SHIFT = -20.0
NEG = -30000.0

PHASE_HI = 99
DBG = ()
PART = "all"


def build(phase_hi=99, dbg=(), part="all", sim1=False):
    nc = bass.Bass("TRN2", target_bir_lowering=False, debug=False)
    emit_l0 = part in ("all", "l0")
    emit_l1 = part in ("all", "l1")

    def dram(name, shape, dt):
        kind = "ExternalOutput" if name in dbg else "Internal"
        return nc.dram_tensor(name, shape, dt, kind=kind).ap()

    ein = lambda n, s, d: nc.dram_tensor(n, s, d, kind="ExternalInput").ap()

    if emit_l0:
        d_x = ein("x_in", [T, D], F32)
        d_wqT = ein("wqT", [128, NDC * D], F32)
        d_wkT = ein("wkT", [128, NDC * D], F32)
        d_wvT = ein("wvT", [128, NDC * D], F32)
        d_woT = ein("woT", [128, NDC * D], F32)
        d_wg = ein("w_gate", [128, 16 * 16 * 512], I8)
        d_wu = ein("w_up", [128, 16 * 16 * 512], I8)
        d_wd = ein("w_down", [128, 16 * 64 * 128], I8)
        d_sg = ein("sgate_c", [128, NIC], F32)
        d_su = ein("sup_c", [128, NIC], F32)
        d_sd = ein("sdown_c", [128, NDC], F32)
        d_cb = ein("cb", [128, 128], F32)
    if emit_l1:
        d_wkT1 = ein("wkT1", [128, NDC * D], F32)
        d_wvT1 = ein("wvT1", [128, NDC * D], F32)
        d_wrT1 = ein("wrT1", [128, NDC * D], F32)
        d_wbT = ein("wbT", [128, NDC * D], F32)
        d_td = ein("td_c", [128, NDC], F32)
        d_bb = ein("bb_c", [128, NDC], F32)
        d_rw = ein("rwT", [128, NDC * 8], F32)
        d_Af = ein("AfT", [128, NDC * 128], F32)
        d_Bw = ein("BwF", [128, D], F32)
        d_sel = ein("selm", [8, 128], F32)
    d_flag = ein("flag", [128, 1], F32)
    d_wsel = ein("wsel", [128, 4], F32)
    d_rsel = ein("rsel", [128, 4], F32)

    if part == "all":
        x1_tm = dram("x1_tm", [TOWN, D], F32)
        xn1_tm = dram("xn1_tm", [TOWN + 1, D], BF16)
    elif part == "l0":
        x1_tm = nc.dram_tensor("x1_tm", [TOWN, D], F32, kind="ExternalOutput").ap()
        xn1_tm = nc.dram_tensor("xn1_tm", [TOWN + 1, D], BF16,
                                kind="ExternalOutput").ap()
    else:
        x1_tm = ein("x1_tm", [TOWN, D], F32)
        xn1_tm = ein("xn1_tm", [TOWN + 1, D], BF16)
    if emit_l1:
        out_x2 = nc.dram_tensor("out_x2", [TOWN, D], F32, kind="ExternalOutput").ap()

    if emit_l0:
        xn_tm = dram("xn_tm", [T, D], BF16)
        kT_sc = dram("kT_sc", [D, T], BF16)
        vT_sc = dram("vT_sc", [D, T], BF16)
        qT_sc = dram("qT_sc", [D, TOWN], BF16)
        sT_sc = dram("sT_sc", [D, TOWN], BF16)
        cc1_in = dram("cc1_in", [128, 64], F32)
        cc1_out = dram("cc1_out", [128, 64], F32)
    dT_sc = dram("dT_sc", [D, TOWN], BF16)
    if emit_l1:
        kT1_sc = dram("kT1_sc", [D, TOWN], BF16)
        vT1_sc = dram("vT1_sc", [D, TOWN], BF16)
        rT1_sc = dram("rT1_sc", [D, TOWN], BF16)
        g1T_sc = dram("g1T_sc", [D, TOWN], BF16)
        mk_sc = dram("mk_sc", [TOWN, 128], BF16)
        cc2_in = dram("cc2_in", [128, 128], F32)
        cc2_out = dram("cc2_out", [128, 128], F32)

    ctx = ExitStack()
    with ctx:
        sb = lambda n, s, d: ctx.enter_context(nc.sbuf_tensor(n, s, d))
        psm = lambda n, s, d=F32: ctx.enter_context(nc.psum_tensor(n, s, d))

        class Sem:
            def __init__(self, name):
                self.h = ctx.enter_context(nc.semaphore(name))
                self.n = 0

            def inc(self, instr, k=1):
                instr.then_inc(self.h, k)
                self.n += k
                return self.n

        A = sb("arenaA", [128, NDC * D], BF16)
        Bb = sb("arenaB", [128, NDC * D], BF16)
        SM = sb("smalls", [128, 2048], F32)
        SMB = sb("smallsb", [128, 3072], BF16)
        EV = sb("evac", [128, 2048], BF16)
        R8 = sb("row8", [8, 256], BF16)
        RWB = sb("rwb", [128, NDC * 8], BF16)

        p512 = [psm(f"ps{i}", [128, 512]) for i in range(6)]
        pD = psm("psD", [128, 512])
        pB = psm("psB", [128, 512])

        LSM = Sem("LSM")
        LX = [Sem("LX0"), Sem("LX1")]
        LR = [Sem("LR0"), Sem("LR1")]
        KH = [Sem("KH0"), Sem("KH1")]
        LS = [Sem("LS0"), Sem("LS1")]
        LMK = Sem("LMK")
        L8 = Sem("L8")
        WSM = Sem("WSM")
        WA = Sem("WA")
        WB = Sem("WB")
        WG = [Sem("WG0"), Sem("WG1")]
        WU = [Sem("WU0"), Sem("WU1")]
        SX = [Sem("SX0"), Sem("SX1")]
        SE = [Sem(f"SE{i}") for i in range(4)]
        SD = [Sem("SD0"), Sem("SD1")]
        SG = [Sem("SG0"), Sem("SG1")]
        SMK = Sem("SMK")
        SO = [Sem("SO0"), Sem("SO1")]
        OS = [Sem("OS0"), Sem("OS1")]
        STORE_SEMS = SX + SE + SD + SG + [SMK] + SO
        ALL_DMA = ([LSM, LMK, L8, WSM, WA, WB] + LX + LR + KH + LS + WG + WU +
                   STORE_SEMS + OS)

        def wait_stores(eng):
            for sm in STORE_SEMS:
                if sm.n:
                    eng.wait_ge(sm.h, sm.n)

        TC = Sem("TC")
        VC = Sem("VC")
        SC = Sem("SC")
        GC = Sem("GC")
        CCA = Sem("CCA")
        CCB = Sem("CCB")
        CCD = Sem("CCD")
        VSs = Sem("VSs")
        SSs = Sem("SSs")

        def vedge(v, instr):
            VSs.inc(instr)
            v.wait_ge(VSs.h, VSs.n)

        def sedge(s, instr):
            SSs.inc(instr)
            s.wait_ge(SSs.h, SSs.n)

        block = ctx.enter_context(nc.Block())

        (SUM_C, SQ_C, MEAN_C, NMEAN_C, VAR_C, RSTD_C, NMUR_C, T1_C, T2_C) = range(9)
        FLAG_C, WSEL_C, RSEL_C = 10, 12, 16
        EPS1_C, EPS2_C = 20, 21
        CB_C = 24
        IOTA_C = 160
        KPCOL_C = 676
        MCOL_C = 680
        ECOL_C = 690
        NECOL_C = 1970
        WCOL_C = 710
        BND_C = 730
        CCS_C = 750
        RX_C = 820
        ACC_C = 890
        DEN_C = 910
        STN_C = 1430
        STD_C = 1450
        RN_C = 1470
        RD_C = 1490
        SG_C = 1510
        SU_C = 1580
        SD_C = 1650
        BB_C = 1670
        TD_C = 1690
        CC2S_C = 1710
        RX2_C = 1840
        MK0_C = 0
        ONES_C = 2048
        ONESR_C = 2052
        RB_C = 2192

        @block.sync
        def _(sy):
            i = sy.dma_start(out=SM[:, FLAG_C:FLAG_C + 1], in_=d_flag)
            LSM.inc(i, 16)
            i = sy.dma_start(out=SM[:, WSEL_C:WSEL_C + 4], in_=d_wsel)
            LSM.inc(i, 16)
            i = sy.dma_start(out=SM[:, RSEL_C:RSEL_C + 4], in_=d_rsel)
            LSM.inc(i, 16)
            if emit_l0:
                i = sy.dma_start(out=SM[:, CB_C:CB_C + 128], in_=d_cb)
                LSM.inc(i, 16)
                i = sy.dma_start(out=SM[:, SG_C:SG_C + 64], in_=d_sg)
                LSM.inc(i, 16)
                i = sy.dma_start(out=SM[:, SU_C:SU_C + 64], in_=d_su)
                LSM.inc(i, 16)
                i = sy.dma_start(out=SM[:, SD_C:SD_C + 16], in_=d_sd)
                LSM.inc(i, 16)
            if emit_l1:
                i = sy.dma_start(out=SM[:, TD_C:TD_C + 16], in_=d_td)
                LSM.inc(i, 16)
                i = sy.dma_start(out=SM[:, BB_C:BB_C + 16], in_=d_bb)
                LSM.inc(i, 16)

        @block.gpsimd
        def _(g):
            g.iota(SM[:, IOTA_C:IOTA_C + 512], pattern=[[1, 512]], base=0,
                   channel_multiplier=0, allow_small_or_imprecise_dtypes=True)
            i = g.iota(SM[:, KPCOL_C:KPCOL_C + 1], pattern=[[1, 1]], base=0,
                       channel_multiplier=1, allow_small_or_imprecise_dtypes=True)
            GC.inc(i)
            if emit_l1:
                i = g.dma_start(out=R8[0:8, 0:128], in_=d_sel)
                WSM.inc(i, 16)
                i = g.dma_start(out=RWB[:], in_=d_rw)
                WSM.inc(i, 16)
        p0_gc, p0_ld, p0_wt = GC.n, LSM.n, WSM.n

        @block.vector
        def _(v):
            v.memset(SMB[:, ONES_C:ONES_C + 1], 1.0)
            v.memset(SMB[0:1, ONESR_C:ONESR_C + 128], 1.0)
            v.memset(SM[:, EPS1_C:EPS1_C + 1], 1e-5)
            v.memset(SM[:, EPS2_C:EPS2_C + 1], 1e-8)
            v.wait_ge(GC.h, p0_gc)
            v.wait_ge(LSM.h, p0_ld)
            i = None
            for m in range(4):
                i = v.tensor_scalar_add(SM[:, MCOL_C + m:MCOL_C + m + 1],
                                        SM[:, KPCOL_C:KPCOL_C + 1], float(128 * m))
            vedge(v, i)
            for m in range(4):
                i = v.tensor_single_scalar(
                    SMB[:, MK0_C + m * 512:MK0_C + (m + 1) * 512],
                    SM[:, IOTA_C:IOTA_C + 512],
                    SM[:, MCOL_C + m:MCOL_C + m + 1], ALU.is_ge)
            VC.inc(i)
        p0_vc = VC.n

        def load_weight(dst, src, wsem):
            tc_req = TC.n

            @block.gpsimd
            def _(g):
                if tc_req:
                    g.wait_ge(TC.h, tc_req)
                for gi in range(4):
                    i = g.dma_start(out=dst[:, gi * 4 * D:(gi + 1) * 4 * D],
                                    in_=src[:, gi * 4 * D:(gi + 1) * 4 * D])
                    wsem.inc(i, 16)
            return wsem.n

        def emit_ln(n_tiles, load_src, dst_tm, dst_row0, xf, xn16, dum16,
                    pre_vec=None, store_xrow=None):

            pv = 1 if pre_vec is not None else 0
            sc0, vc0 = SC.n, VC.n
            per_st = 16 * (2 if store_xrow is not None else 1)
            sx0 = [SX[0].n, SX[1].n]
            st_at = {tt: sx0[tt % 2] + per_st * (tt // 2 + 1)
                     for tt in range(n_tiles)}
            sc_sums = lambda tt: sc0 + 4 * tt + 1
            sc_rstd = lambda tt: sc0 + 4 * tt + 4
            vc_pre = lambda tt: vc0 + (2 + pv) * tt + 1
            vc_varA = lambda tt: vc0 + (2 + pv) * tt + pv + 1
            vc_xn = lambda tt: vc0 + (2 + pv) * tt + pv + 2
            ld_at = {}

            if load_src is not None:
                @block.sync
                def _(sy):
                    for tt in range(n_tiles + 1):
                        if tt < n_tiles:
                            if tt >= 2:
                                sy.wait_ge(VC.h, vc_xn(tt - 2))
                            i = sy.dma_start(out=xf[tt % 2][:], in_=load_src(tt))
                            ld_at[tt] = (LX[tt % 2], LX[tt % 2].inc(i, 16))
                        if tt >= 1:
                            pt = tt - 1
                            sy.wait_ge(VC.h, vc_xn(pt))
                            if store_xrow is not None:
                                i = sy.dma_start(out=store_xrow(pt),
                                                 in_=xf[pt % 2][:])
                                SX[pt % 2].inc(i, 16)
                            i = sy.dma_start(
                                out=dst_tm[dst_row0 + pt * 128:
                                           dst_row0 + (pt + 1) * 128, :],
                                in_=xn16[pt % 2][:])
                            SX[pt % 2].inc(i, 16)

            @block.scalar
            def _(s):
                for tt in range(n_tiles):
                    if load_src is not None:
                        s.wait_ge(ld_at[tt][0].h, ld_at[tt][1])
                    else:
                        s.wait_ge(VC.h, vc_pre(tt))
                    if tt >= 1:
                        s.wait_ge(VC.h, vc_varA(tt - 1))
                    i = s.activation(dum16[0][:], xf[tt % 2][:], AF.Copy,
                                     accum_out=SM[:, SUM_C:SUM_C + 1])
                    i = s.activation(dum16[1][:], xf[tt % 2][:], AF.Square,
                                     accum_out=SM[:, SQ_C:SQ_C + 1])
                    SC.inc(i)
                    s.wait_ge(VC.h, vc_varA(tt))
                    i = s.activation(SM[:, T1_C:T1_C + 1], SM[:, VAR_C:VAR_C + 1],
                                     AF.Ln, bias=SM[:, EPS1_C:EPS1_C + 1])
                    sedge(s, i)
                    i = s.activation(SM[:, RSTD_C:RSTD_C + 1],
                                     SM[:, T1_C:T1_C + 1], AF.Exp, scale=-0.5)
                    SC.inc(i, 3)

            @block.vector
            def _(v):
                for tt in range(n_tiles):
                    if pre_vec is not None:
                        if tt >= 2:
                            v.wait_ge(SX[tt % 2].h, st_at[tt - 2])
                        pre_vec(v, tt)
                    v.wait_ge(SC.h, sc_sums(tt))
                    i = v.tensor_scalar_mul(SM[:, MEAN_C:MEAN_C + 1],
                                            SM[:, SUM_C:SUM_C + 1], 1.0 / D)
                    vedge(v, i)
                    i = v.tensor_scalar_mul(SM[:, NMEAN_C:NMEAN_C + 1],
                                            SM[:, MEAN_C:MEAN_C + 1], -1.0)
                    i = v.tensor_scalar_mul(SM[:, T2_C:T2_C + 1],
                                            SM[:, SQ_C:SQ_C + 1], 1.0 / D)
                    vedge(v, i)
                    i = v.scalar_tensor_tensor(
                        SM[:, VAR_C:VAR_C + 1], SM[:, MEAN_C:MEAN_C + 1],
                        SM[:, NMEAN_C:NMEAN_C + 1], SM[:, T2_C:T2_C + 1],
                        ALU.mult, ALU.add)
                    VC.inc(i)
                    v.wait_ge(SC.h, sc_rstd(tt))
                    i = v.tensor_scalar(SM[:, NMUR_C:NMUR_C + 1],
                                        SM[:, NMEAN_C:NMEAN_C + 1],
                                        SM[:, RSTD_C:RSTD_C + 1], None, ALU.mult)
                    vedge(v, i)
                    if tt >= 2 and pre_vec is None:
                        v.wait_ge(SX[tt % 2].h, st_at[tt - 2])
                    i = v.tensor_scalar(xn16[tt % 2][:], xf[tt % 2][:],
                                        SM[:, RSTD_C:RSTD_C + 1],
                                        SM[:, NMUR_C:NMUR_C + 1],
                                        ALU.mult, ALU.add)
                    VC.inc(i)

            def store_tile(sy, tt):
                sy.wait_ge(VC.h, vc_xn(tt))
                if store_xrow is not None:
                    i = sy.dma_start(out=store_xrow(tt), in_=xf[tt % 2][:])
                    SX[tt % 2].inc(i, 16)
                i = sy.dma_start(
                    out=dst_tm[dst_row0 + tt * 128:
                               dst_row0 + (tt + 1) * 128, :],
                    in_=xn16[tt % 2][:])
                SX[tt % 2].inc(i, 16)
            return store_tile

        def emit_proj(warena, wsem, wt_ready, rhs_src, t_blocks, dstf, rbufs):
            tc0, vc0 = TC.n, VC.n
            lr0 = [LR[0].n, LR[1].n]
            ld_at = {tb: lr0[tb % 2] + 256 * (tb // 2 + 1)
                     for tb in range(t_blocks)}
            se0 = [sm.n for sm in SE]

            @block.sync
            def _(sy):
                for tb in range(t_blocks):
                    rb = rbufs[tb % 2]
                    if tb >= 2:
                        sy.wait_ge(TC.h, tc0 + 16 * (tb - 1))
                    elif tc0:
                        sy.wait_ge(TC.h, tc0)
                    for dc in range(NDC):
                        i = sy.dma_start(out=rb[:, dc * 512:(dc + 1) * 512],
                                         in_=rhs_src(tb, dc), transpose=True)
                        LR[tb % 2].inc(i, 16)

            @block.tensor
            def _(t):
                t.wait_ge(wsem.h, wt_ready)
                for tb in range(t_blocks):
                    rb = rbufs[tb % 2]
                    t.wait_ge(LR[tb % 2].h, ld_at[tb])
                    for oc in range(NDC):
                        k = tb * 16 + oc
                        pp = p512[k % 4]
                        if k >= 4:
                            t.wait_ge(VC.h, vc0 + k - 3)
                        for dc in range(NDC):
                            lh = warena[:, (dc * NDC + oc) * 128:
                                        (dc * NDC + oc) * 128 + 128]
                            t.ldweights(lh)
                            i = t.matmul(pp[:], lh, rb[:, dc * 512:(dc + 1) * 512],
                                         start=(dc == 0), stop=(dc == NDC - 1))
                        TC.inc(i)

            @block.vector
            def _(v):
                for k in range(t_blocks * 16):
                    v.wait_ge(TC.h, tc0 + k + 1)
                    tgt = se0[k % 4] + 16 * (k // 4)
                    if tgt:
                        v.wait_ge(SE[k % 4].h, tgt)
                    i = v.tensor_copy(EV[:, (k % 4) * 512:(k % 4 + 1) * 512],
                                      p512[k % 4][:])
                    VC.inc(i)

            @block.gpsimd
            def _(g):
                for tb in range(t_blocks):
                    for oc in range(NDC):
                        k = tb * 16 + oc
                        g.wait_ge(VC.h, vc0 + k + 1)
                        i = g.dma_start(out=dstf(oc, tb),
                                        in_=EV[:, (k % 4) * 512:(k % 4 + 1) * 512])
                        SE[k % 4].inc(i, 16)

        if emit_l0 and phase_hi >= 1:
            with (
                nc.sbuf_tensor("xfa", [128, D], F32) as xfa,
                nc.sbuf_tensor("xfb", [128, D], F32) as xfb,
                nc.sbuf_tensor("xna", [128, D], BF16) as xna,
                nc.sbuf_tensor("xnb", [128, D], BF16) as xnb,
                nc.sbuf_tensor("dum", [128, D], BF16) as dum,
                nc.sbuf_tensor("dumx", [128, D], BF16) as dumx,
            ):
                emit_ln(T // 128, lambda tt: d_x[tt * 128:(tt + 1) * 128, :],
                        xn_tm, 0, [xfa, xfb], [xna, xnb], (dum, dumx))

        if emit_l0 and phase_hi >= 2:
            with (
                nc.sbuf_tensor("rhsC", [128, NDC * 512], BF16) as C,
                nc.sbuf_tensor("rhsD", [128, NDC * 512], BF16) as Dd,
            ):
                rhs_src = lambda tb, dc: xn_tm[tb * 512:(tb + 1) * 512,
                                               dc * 128:(dc + 1) * 128]

                @block.sync
                def _(sy):
                    wait_stores(sy)

                wq_rdy = load_weight(A, d_wqT, WA)
                emit_proj(A, WA, wq_rdy, rhs_src, 4,
                          lambda oc, tb: qT_sc[oc * 128:(oc + 1) * 128,
                                               tb * 512:(tb + 1) * 512], (C, Dd))
                wk_rdy = load_weight(Bb, d_wkT, WB)
                emit_proj(Bb, WB, wk_rdy, rhs_src, 8,
                          lambda oc, tb: kT_sc[oc * 128:(oc + 1) * 128,
                                               tb * 512:(tb + 1) * 512], (C, Dd))
                wv_rdy = load_weight(A, d_wvT, WA)
                emit_proj(A, WA, wv_rdy, rhs_src, 8,
                          lambda oc, tb: vT_sc[oc * 128:(oc + 1) * 128,
                                               tb * 512:(tb + 1) * 512], (C, Dd))

        if emit_l0 and phase_hi >= 3:
            wo_rdy = load_weight(Bb, d_woT, WB)
            with (
                nc.sbuf_tensor("kth0", [128, T], BF16) as kth0,
                nc.sbuf_tensor("kth1", [128, T], BF16) as kth1,
                nc.sbuf_tensor("vh0", [128, T], BF16) as vh0,
                nc.sbuf_tensor("vh1", [128, T], BF16) as vh1,
                nc.sbuf_tensor("qh0", [128, TOWN], BF16) as qh0,
                nc.sbuf_tensor("qh1", [128, TOWN], BF16) as qh1,
                nc.sbuf_tensor("ptil", [128, 4 * 512], BF16) as ptil,
                nc.sbuf_tensor("bb16", [128, 512], F32) as bb16t,
                nc.sbuf_tensor("rden", [1, 512], F32) as rden,
            ):
                kth, vhh, qhh = [kth0, kth1], [vh0, vh1], [qh0, qh1]
                NCH = 32
                tc0, sc0, vc0 = TC.n, SC.n, VC.n
                kh0 = [KH[0].n, KH[1].n]
                ld_head = {h: kh0[h % 2] + 16 * 34 * (h // 2 + 1)
                           for h in range(NH)}

                def seq_totals(hq):
                    base = tc0 + hq * 65
                    qk, pv = {}, {}
                    k = 0
                    for j in range(NCH):
                        k += 1
                        qk[j] = base + k
                        if j >= 1:
                            k += 1
                            pv[j - 1] = base + k
                    k += 1
                    pv[NCH - 1] = base + k
                    k += 1
                    bc = base + k
                    return qk, pv, bc

                sc_exp = lambda hq, j: sc0 + hq * 34 + j + 1
                sc_recip = lambda hq: sc0 + hq * 34 + 33
                sc_rb = lambda hq: sc0 + hq * 34 + 34
                vc_mask = lambda hq, m: vc0 + hq * 7 + m + 1
                vc_den = lambda hq: vc0 + hq * 7 + 5
                vc_bb = lambda hq: vc0 + hq * 7 + 6
                vc_ao = lambda hq: vc0 + hq * 7 + 7

                @block.sync
                def _(sy):
                    wait_stores(sy)
                    for h in range(NH):
                        if h >= 2:
                            sy.wait_ge(TC.h, tc0 + (h - 1) * 260)
                        i = sy.dma_start(out=kth[h % 2][:],
                                         in_=kT_sc[h * 128:(h + 1) * 128, :])
                        KH[h % 2].inc(i, 16)
                        for j in range(NCH):
                            i = sy.dma_start(
                                out=vhh[h % 2][:, j * 128:(j + 1) * 128],
                                in_=vT_sc[h * 128:(h + 1) * 128,
                                          j * 128:(j + 1) * 128],
                                transpose=True)
                            KH[h % 2].inc(i, 16)
                        i = sy.dma_start(out=qhh[h % 2][:],
                                         in_=qT_sc[h * 128:(h + 1) * 128, :])
                        KH[h % 2].inc(i, 16)

                def emit_pv(t, h, qi, j, hq, qk_t, pv_t):
                    m = j - 4 * qi
                    if 0 <= m <= 3:
                        t.wait_ge(VC.h, vc_mask(hq, m))
                    else:
                        t.wait_ge(SC.h, sc_exp(hq, j))
                    if j == 0:
                        if hq >= 1:
                            t.wait_ge(VC.h, vc_den(hq - 1))
                        if hq >= 2:
                            t.wait_ge(VC.h, vc_ao(hq - 2))
                    pt = ptil[:, (j % 4) * 512:(j % 4 + 1) * 512]
                    po = p512[2 + (qi % 2)]
                    lhv = vhh[h % 2][:, j * 128:(j + 1) * 128]
                    t.ldweights(lhv)
                    t.matmul(po[:], lhv, pt, start=(j == 0), stop=(j == NCH - 1))
                    t.ldweights(SMB[:, ONES_C:ONES_C + 1])
                    i = t.matmul(pD[0:1, :], SMB[:, ONES_C:ONES_C + 1], pt,
                                 start=(j == 0), stop=(j == NCH - 1))
                    TC.inc(i)

                @block.tensor
                def _(t):
                    for h in range(NH):
                        t.wait_ge(KH[h % 2].h, ld_head[h])
                        for qi in range(4):
                            hq = h * 4 + qi
                            qk_t, pv_t, bc_t = seq_totals(hq)
                            for j in range(NCH):
                                sp = p512[j % 2]
                                if j >= 2:
                                    t.wait_ge(SC.h, sc_exp(hq, j - 2))
                                elif hq > 0:
                                    t.wait_ge(SC.h, sc_exp(hq - 1, 30 + j))
                                lhk = kth[h % 2][:, j * 128:(j + 1) * 128]
                                t.ldweights(lhk)
                                i = t.matmul(sp[:], lhk,
                                             qhh[h % 2][:, qi * 512:(qi + 1) * 512],
                                             start=True, stop=True)
                                TC.inc(i)
                                if j >= 1:
                                    emit_pv(t, h, qi, j - 1, hq, qk_t, pv_t)
                            emit_pv(t, h, qi, NCH - 1, hq, qk_t, pv_t)
                            t.wait_ge(SC.h, sc_rb(hq))
                            if hq >= 1:
                                t.wait_ge(VC.h, vc_bb(hq - 1))
                            t.ldweights(SMB[0:1, ONESR_C:ONESR_C + 128])
                            i = t.matmul(pB[:], SMB[0:1, ONESR_C:ONESR_C + 128],
                                         SMB[0:1, RB_C:RB_C + 512],
                                         start=True, stop=True)
                            TC.inc(i)
                            assert TC.n == bc_t

                @block.scalar
                def _(s):
                    for h in range(NH):
                        for qi in range(4):
                            hq = h * 4 + qi
                            qk_t, pv_t, bc_t = seq_totals(hq)
                            pqk, ppv, _ = seq_totals(hq - 1) if hq else (None, None, None)
                            for j in range(NCH):
                                s.wait_ge(TC.h, qk_t[j])
                                if j >= 4:
                                    s.wait_ge(TC.h, pv_t[j - 4])
                                elif hq > 0:
                                    s.wait_ge(TC.h, ppv[28 + j])
                                i = s.activation(
                                    ptil[:, (j % 4) * 512:(j % 4 + 1) * 512],
                                    p512[j % 2][:], AF.Exp,
                                    bias=SM[:, CB_C + qi * 32 + j:
                                            CB_C + qi * 32 + j + 1],
                                    scale=RSQ)
                                SC.inc(i)
                            s.wait_ge(VC.h, vc_den(hq))
                            i = s.activation(rden[0:1, :],
                                             SM[0:1, DEN_C:DEN_C + 512], AF.Ln)
                            sedge(s, i)
                            i = s.activation(rden[0:1, :], rden[0:1, :],
                                             AF.Exp, scale=-1.0)
                            SC.inc(i)
                            s.wait_ge(SC.h, SC.n)
                            i = s.copy(SMB[0:1, RB_C:RB_C + 512], rden[0:1, :])
                            SC.inc(i)

                @block.vector
                def _(v):
                    for h in range(NH):
                        for qi in range(4):
                            hq = h * 4 + qi
                            qk_t, pv_t, bc_t = seq_totals(hq)
                            for m in range(4):
                                j = 4 * qi + m
                                v.wait_ge(SC.h, sc_exp(hq, j))
                                pt = ptil[:, (j % 4) * 512:(j % 4 + 1) * 512]
                                i = v.tensor_mul(pt, pt,
                                                 SMB[:, MK0_C + m * 512:
                                                     MK0_C + (m + 1) * 512])
                                VC.inc(i)
                            v.wait_ge(TC.h, pv_t[NCH - 1])
                            i = v.tensor_copy(SM[0:1, DEN_C:DEN_C + 512],
                                              pD[0:1, :])
                            VC.inc(i)
                            v.wait_ge(TC.h, bc_t)
                            i = v.tensor_copy(bb16t[:], pB[:])
                            VC.inc(i)
                            v.wait_ge(VC.h, VC.n)
                            i = v.tensor_tensor(
                                A[:, h * TOWN + qi * 512:h * TOWN + (qi + 1) * 512],
                                p512[2 + (qi % 2)][:], bb16t[:], ALU.mult)
                            VC.inc(i)
        p3_vc = VC.n
        p3_wt = WB.n

        if emit_l0 and phase_hi >= 4:
            tc0, vc0 = TC.n, VC.n
            se0 = [sm.n for sm in SE]

            @block.tensor
            def _(t):
                t.wait_ge(WB.h, p3_wt)
                t.wait_ge(VC.h, p3_vc)
                for tb in range(4):
                    for oc in range(NDC):
                        k = tb * 16 + oc
                        pp = p512[k % 4]
                        if k >= 4:
                            t.wait_ge(VC.h, vc0 + k - 3)
                        for dc in range(NDC):
                            lh = Bb[:, (dc * NDC + oc) * 128:
                                    (dc * NDC + oc) * 128 + 128]
                            t.ldweights(lh)
                            i = t.matmul(pp[:], lh,
                                         A[:, dc * TOWN + tb * 512:
                                           dc * TOWN + (tb + 1) * 512],
                                         start=(dc == 0), stop=(dc == NDC - 1))
                        TC.inc(i)

            @block.vector
            def _(v):
                for k in range(64):
                    v.wait_ge(TC.h, tc0 + k + 1)
                    tgt = se0[k % 4] + 16 * (k // 4)
                    if tgt:
                        v.wait_ge(SE[k % 4].h, tgt)
                    i = v.tensor_copy(EV[:, (k % 4) * 512:(k % 4 + 1) * 512],
                                      p512[k % 4][:])
                    VC.inc(i)

            @block.gpsimd
            def _(g):
                for tb in range(4):
                    for oc in range(NDC):
                        k = tb * 16 + oc
                        g.wait_ge(VC.h, vc0 + k + 1)
                        i = g.dma_start(out=sT_sc[oc * 128:(oc + 1) * 128,
                                                  tb * 512:(tb + 1) * 512],
                                        in_=EV[:, (k % 4) * 512:(k % 4 + 1) * 512])
                        SE[k % 4].inc(i, 16)

        if emit_l0 and phase_hi >= 5:
            wg = [Bb[:, 0:8192], Bb[:, 8192:16384]]
            wu = [Bb[:, 16384:24576], Bb[:, 24576:32768]]
            wdl = [Bb[:, 0:8192], Bb[:, 8192:16384]]
            with (
                nc.sbuf_tensor("rhsC5", [128, NDC * 512], BF16) as C,
                nc.sbuf_tensor("gt5", [128, 2 * 512], BF16) as gt,
                nc.sbuf_tensor("ut5", [128, 2 * 512], BF16) as ut,
                nc.sbuf_tensor("stl5", [128, 2 * 512], BF16) as stl,
                nc.sbuf_tensor("dtl5", [128, 2 * 512], BF16) as dtl,
            ):
                for tq in range(4):
                    tc0, sc0, vc0 = TC.n, SC.n, VC.n
                    lr00 = LR[0].n
                    ld_rhs = lr00 + 16 * NDC
                    ls0 = [LS[0].n, LS[1].n]
                    ld_st = lambda oc: ls0[oc % 2] + 16 * (oc // 2 + 1)
                    wg0 = [WG[0].n, WG[1].n]
                    wu0 = [WU[0].n, WU[1].n]
                    wt_ib_g = lambda ib: wg0[ib % 2] + 16 * (ib // 2 + 1)
                    wt_ib_u = lambda ib: wu0[ib % 2] + 16 * (ib // 2 + 1)
                    wt_oc = lambda oc: wg0[oc % 2] + 16 * (8 + oc // 2 + 1)
                    tc_gu = lambda ic, gu: tc0 + ic * 2 + gu + 1
                    tc_oc = lambda oc: tc0 + 128 + oc + 1
                    sc_gu = lambda ic, gu: sc0 + ic * 2 + gu + 1
                    sc_oc = lambda oc: sc0 + 128 + oc + 1
                    vc_f = lambda ic: vc0 + ic + 1
                    vc_oc = lambda oc: vc0 + 64 + oc + 1
                    sd0 = [SD[0].n, SD[1].n]
                    st_oc = lambda oc: sd0[oc % 2] + 16 * (oc // 2 + 1)

                    @block.sync
                    def _(sy, tq=tq, tc0=tc0, vc0=vc0):
                        if tq == 0:
                            wait_stores(sy)
                        else:
                            sy.wait_ge(TC.h, tc0)
                        for dc in range(NDC):
                            i = sy.dma_start(
                                out=C[:, dc * 512:(dc + 1) * 512],
                                in_=xn_tm[tq * 512:(tq + 1) * 512,
                                          dc * 128:(dc + 1) * 128],
                                transpose=True)
                            LR[0].inc(i, 16)
                        for oc in range(NDC):
                            if oc >= 2:
                                sy.wait_ge(VC.h, vc0 + 64 + oc - 1)
                            i = sy.dma_start(
                                out=stl[:, (oc % 2) * 512:(oc % 2) * 512 + 512],
                                in_=sT_sc[oc * 128:(oc + 1) * 128,
                                          tq * 512:(tq + 1) * 512])
                            LS[oc % 2].inc(i, 16)
                            if oc >= 2:
                                po = oc - 2
                                i = sy.dma_start(
                                    out=dT_sc[po * 128:(po + 1) * 128,
                                              tq * 512:(tq + 1) * 512],
                                    in_=dtl[:, (po % 2) * 512:(po % 2) * 512 + 512])
                                SD[po % 2].inc(i, 16)
                        for po in (14, 15):
                            sy.wait_ge(VC.h, vc0 + 64 + po + 1)
                            i = sy.dma_start(
                                out=dT_sc[po * 128:(po + 1) * 128,
                                          tq * 512:(tq + 1) * 512],
                                in_=dtl[:, (po % 2) * 512:(po % 2) * 512 + 512])
                            SD[po % 2].inc(i, 16)

                    @block.gpsimd
                    def _(g, tq=tq, tc0=tc0):
                        for ib in range(16):
                            if ib >= 2:
                                g.wait_ge(TC.h, tc0 + (ib - 1) * 8)
                            elif tc0:
                                g.wait_ge(TC.h, tc0)
                            i = g.dma_start(out=wg[ib % 2],
                                            in_=d_wg[:, ib * 8192:(ib + 1) * 8192])
                            WG[ib % 2].inc(i, 16)
                            i = g.dma_start(out=wu[ib % 2],
                                            in_=d_wu[:, ib * 8192:(ib + 1) * 8192])
                            WU[ib % 2].inc(i, 16)
                        for oc in range(16):
                            if oc < 2:
                                g.wait_ge(TC.h, tc0 + 128)
                            else:
                                g.wait_ge(TC.h, tc0 + 128 + oc - 1)
                            i = g.dma_start(out=wdl[oc % 2],
                                            in_=d_wd[:, oc * 8192:(oc + 1) * 8192])
                            WG[oc % 2].inc(i, 16)

                    @block.tensor
                    def _(t, tq=tq, ld_rhs=ld_rhs, wt_ib_g=wt_ib_g,
                          wt_ib_u=wt_ib_u, wt_oc=wt_oc,
                          sc_gu=sc_gu, sc_oc=sc_oc, vc_f=vc_f):
                        t.wait_ge(LR[0].h, ld_rhs)
                        for ib in range(16):
                            t.wait_ge(WG[ib % 2].h, wt_ib_g(ib))
                            t.wait_ge(WU[ib % 2].h, wt_ib_u(ib))
                            for icc in range(4):
                                ic = ib * 4 + icc
                                pg = p512[ic % 2]
                                pu = p512[2 + ic % 2]
                                if ic >= 2:
                                    t.wait_ge(SC.h, sc_gu(ic - 2, 1))
                                for dc in range(NDC):
                                    lh = wg[ib % 2][:, dc * 512 + icc * 128:
                                                    dc * 512 + icc * 128 + 128]
                                    t.ldweights(lh)
                                    i = t.matmul(pg[:], lh,
                                                 C[:, dc * 512:(dc + 1) * 512],
                                                 start=(dc == 0),
                                                 stop=(dc == NDC - 1))
                                TC.inc(i)
                                for dc in range(NDC):
                                    lh = wu[ib % 2][:, dc * 512 + icc * 128:
                                                    dc * 512 + icc * 128 + 128]
                                    t.ldweights(lh)
                                    i = t.matmul(pu[:], lh,
                                                 C[:, dc * 512:(dc + 1) * 512],
                                                 start=(dc == 0),
                                                 stop=(dc == NDC - 1))
                                TC.inc(i)
                        for oc in range(16):
                            t.wait_ge(WG[oc % 2].h, wt_oc(oc))
                            if oc == 0:
                                t.wait_ge(VC.h, vc_f(63))
                            py = p512[4 + oc % 2]
                            if oc >= 2:
                                t.wait_ge(SC.h, sc_oc(oc - 2))
                            for ic in range(NIC):
                                lh = wdl[oc % 2][:, ic * 128:(ic + 1) * 128]
                                t.ldweights(lh)
                                i = t.matmul(py[:], lh, A[:, ic * 512:(ic + 1) * 512],
                                             start=(ic == 0), stop=(ic == NIC - 1))
                            TC.inc(i)

                    @block.scalar
                    def _(s, tq=tq, tc_gu=tc_gu, tc_oc=tc_oc, vc_f=vc_f,
                          vc_oc=vc_oc):
                        for ic in range(NIC):
                            s.wait_ge(TC.h, tc_gu(ic, 0))
                            s.wait_ge(VC.h, vc_f(ic - 2) if ic >= 2 else vc0)
                            i = s.activation(gt[:, (ic % 2) * 512:(ic % 2) * 512 + 512],
                                             p512[ic % 2][:], AF.Silu,
                                             scale=SM[:, SG_C + ic:SG_C + ic + 1])
                            SC.inc(i)
                            s.wait_ge(TC.h, tc_gu(ic, 1))
                            i = s.activation(ut[:, (ic % 2) * 512:(ic % 2) * 512 + 512],
                                             p512[2 + ic % 2][:], AF.Copy,
                                             scale=SM[:, SU_C + ic:SU_C + ic + 1])
                            SC.inc(i)
                        for oc in range(16):
                            s.wait_ge(TC.h, tc_oc(oc))
                            if oc >= 2:
                                s.wait_ge(VC.h, vc_oc(oc - 2))
                            i = s.activation(gt[:, (oc % 2) * 512:(oc % 2) * 512 + 512],
                                             p512[4 + oc % 2][:], AF.Silu,
                                             scale=SM[:, SD_C + oc:SD_C + oc + 1])
                            SC.inc(i)

                    @block.vector
                    def _(v, tq=tq, sc_gu=sc_gu, sc_oc=sc_oc, ld_st=ld_st,
                          st_oc=st_oc):
                        for ic in range(NIC):
                            v.wait_ge(SC.h, sc_gu(ic, 1))
                            i = v.tensor_mul(A[:, ic * 512:(ic + 1) * 512],
                                             gt[:, (ic % 2) * 512:(ic % 2) * 512 + 512],
                                             ut[:, (ic % 2) * 512:(ic % 2) * 512 + 512])
                            VC.inc(i)
                        for oc in range(16):
                            v.wait_ge(SC.h, sc_oc(oc))
                            v.wait_ge(LS[oc % 2].h, ld_st(oc))
                            tgt = st_oc(oc - 2) if oc >= 2 else sd0[oc % 2]
                            if tgt:
                                v.wait_ge(SD[oc % 2].h, tgt)
                            i = v.tensor_mul(dtl[:, (oc % 2) * 512:(oc % 2) * 512 + 512],
                                             gt[:, (oc % 2) * 512:(oc % 2) * 512 + 512],
                                             stl[:, (oc % 2) * 512:(oc % 2) * 512 + 512])
                            VC.inc(i)


        if emit_l0 and phase_hi >= 6:
            with (
                nc.sbuf_tensor("xf6a", [128, D], F32) as xf6a,
                nc.sbuf_tensor("xf6b", [128, D], F32) as xf6b,
                nc.sbuf_tensor("xr6a", [128, D], F32) as xr6a,
                nc.sbuf_tensor("xr6b", [128, D], F32) as xr6b,
                nc.sbuf_tensor("dl6a", [128, D], BF16) as dl6a,
                nc.sbuf_tensor("dl6b", [128, D], BF16) as dl6b,
                nc.sbuf_tensor("xn6a", [128, D], BF16) as xn6a,
                nc.sbuf_tensor("xn6b", [128, D], BF16) as xn6b,
                nc.sbuf_tensor("dum6", [128, D], BF16) as dum6,
                nc.sbuf_tensor("dum6x", [128, D], BF16) as dum6x,
                nc.sbuf_tensor("bnd", [128, 16], BF16) as bnd,
            ):
                xf6, xr6, dl6 = [xf6a, xf6b], [xr6a, xr6b], [dl6a, dl6b]
                vc0 = VC.n
                lx0 = [LX[0].n, LX[1].n]
                ld_at6 = {tt: lx0[tt % 2] + 80 * (tt // 2 + 1)
                          for tt in range(16)}
                vc_pre6 = lambda tt: vc0 + 3 * tt + 1

                def pre_vec(v, tt):
                    v.wait_ge(LX[tt % 2].h, ld_at6[tt])
                    i = v.tensor_tensor(xf6[tt % 2][:], xr6[tt % 2][:],
                                        dl6[tt % 2][:], ALU.add)
                    VC.inc(i)

                store_tile = emit_ln(
                    16, None, xn1_tm, 1, xf6, [xn6a, xn6b], (dum6, dum6x),
                    pre_vec=pre_vec,
                    store_xrow=lambda tt: x1_tm[tt * 128:(tt + 1) * 128, :])

                @block.sync
                def _(sy):
                    wait_stores(sy)
                    for tt in range(17):
                        if tt < 16:
                            if tt >= 2:
                                sy.wait_ge(VC.h, vc_pre6(tt - 2))
                            i = sy.dma_start(out=xr6[tt % 2][:],
                                             in_=d_x[tt * 128:(tt + 1) * 128, :])
                            LX[tt % 2].inc(i, 16)
                            for gq in range(4):
                                i = sy.dma_start(
                                    out=dl6[tt % 2][:, gq * 512:(gq + 1) * 512],
                                    in_=dT_sc[gq * 512:(gq + 1) * 512,
                                              tt * 128:(tt + 1) * 128],
                                    transpose=True)
                                LX[tt % 2].inc(i, 16)
                        if tt >= 1:
                            store_tile(sy, tt - 1)

                @block.sync
                def _(sy):
                    wait_stores(sy)
                    i = sy.dma_start(
                        out=bnd[:],
                        in_=xn1_tm[TOWN:TOWN + 1, :].rearrange(
                            "one (p c) -> (one p) c", p=128))
                    LSM.inc(i, 16)
                ld_bnd = LSM.n
                vc0b = VC.n

                @block.vector
                def _(v):
                    v.wait_ge(LSM.h, ld_bnd)
                    i = v.tensor_copy(SM[:, BND_C:BND_C + 16], bnd[:])
                    vedge(v, i)
                    for bp in range(4):
                        i = v.tensor_scalar(
                            SM[:, CCS_C + bp * 16:CCS_C + (bp + 1) * 16],
                            SM[:, BND_C:BND_C + 16],
                            SM[:, WSEL_C + bp:WSEL_C + bp + 1], None, ALU.mult)
                    VC.inc(i)
                vc_ccs = VC.n

                @block.gpsimd
                def _(g):
                    g.wait_ge(VC.h, vc_ccs)
                    i = g.dma_start(out=cc1_in, in_=SM[:, CCS_C:CCS_C + 64])
                    CCD.inc(i, 16)
                    g.wait_ge(CCD.h, CCD.n)
                    i = g.collective_compute(
                        "AllReduce", ALU.add,
                        replica_groups=[[0]] if sim1 else [list(range(8))],
                        ins=[cc1_in.opt()], outs=[cc1_out.opt()])
                    CCA.inc(i, 1)
                    g.wait_ge(CCA.h, 1)
                    i = g.dma_start(out=SM[:, RX_C:RX_C + 64], in_=cc1_out)
                    CCD.inc(i, 16)
                gc_rx1 = CCD.n

                @block.vector
                def _(v):
                    v.wait_ge(CCD.h, gc_rx1)
                    i = v.tensor_scalar(SM[:, ACC_C:ACC_C + 16],
                                        SM[:, RX_C:RX_C + 16],
                                        SM[:, RSEL_C:RSEL_C + 1], None, ALU.mult)
                    vedge(v, i)
                    for bp in range(1, 4):
                        i = v.scalar_tensor_tensor(
                            SM[:, ACC_C:ACC_C + 16],
                            SM[:, RX_C + bp * 16:RX_C + (bp + 1) * 16],
                            SM[:, RSEL_C + bp:RSEL_C + bp + 1],
                            SM[:, ACC_C:ACC_C + 16], ALU.mult, ALU.add)
                        vedge(v, i)
                    i = v.tensor_scalar(bnd[:], SM[:, ACC_C:ACC_C + 16],
                                        SM[:, FLAG_C:FLAG_C + 1], None, ALU.mult)
                    VC.inc(i)
                vc_bnd = VC.n

                @block.sync
                def _(sy):
                    sy.wait_ge(VC.h, vc_bnd)
                    i = sy.dma_start(
                        out=xn1_tm[0:1, :].rearrange("one (p c) -> (one p) c",
                                                     p=128),
                        in_=bnd[:])
                    SX[0].inc(i, 16)

        if emit_l1 and phase_hi >= 7:
            with (
                nc.sbuf_tensor("rhsC7", [128, NDC * 512], BF16) as C,
                nc.sbuf_tensor("rhsD7", [128, NDC * 512], BF16) as Dd,
                nc.sbuf_tensor("abuf", [128, NDC * 128], BF16) as ABUF,
                nc.sbuf_tensor("bwb", [128, D], BF16) as BWB,
                nc.sbuf_tensor("hT", [128, TOWN], BF16) as hT,
                nc.sbuf_tensor("lgt", [128, 8], F32) as lgt,
                nc.sbuf_tensor("mxc", [128, 1], F32) as mxc,
                nc.sbuf_tensor("mk8", [128, 8], BF16) as mk8,
                nc.sbuf_tensor("mkT", [128, 512], BF16) as mkT,
                nc.sbuf_tensor("g1t", [128, 2 * 512], BF16) as g1t,
            ):
                @block.sync
                def _(sy):
                    wait_stores(sy)

                @block.gpsimd
                def _(g):
                    tcn = TC.n
                    if tcn:
                        g.wait_ge(TC.h, tcn)
                    i = g.dma_start(out=ABUF[:], in_=d_Af)
                    WSM.inc(i, 16)
                    i = g.dma_start(out=BWB[:], in_=d_Bw)
                    WSM.inc(i, 16)
                wsm_rdy = WSM.n

                rhs_shift = lambda tb, dc: xn1_tm[tb * 512:(tb + 1) * 512,
                                                  dc * 128:(dc + 1) * 128]
                rhs_plain = lambda tb, dc: xn1_tm[1 + tb * 512:1 + (tb + 1) * 512,
                                                  dc * 128:(dc + 1) * 128]

                wk1_rdy = load_weight(A, d_wkT1, WA)
                emit_proj(A, WA, wk1_rdy, rhs_shift, 4,
                          lambda oc, tb: kT1_sc[oc * 128:(oc + 1) * 128,
                                                tb * 512:(tb + 1) * 512], (C, Dd))
                wv1_rdy = load_weight(Bb, d_wvT1, WB)
                emit_proj(Bb, WB, wv1_rdy, rhs_shift, 4,
                          lambda oc, tb: vT1_sc[oc * 128:(oc + 1) * 128,
                                                tb * 512:(tb + 1) * 512], (C, Dd))
                wr1_rdy = load_weight(A, d_wrT1, WA)
                emit_proj(A, WA, wr1_rdy, rhs_plain, 4,
                          lambda oc, tb: rT1_sc[oc * 128:(oc + 1) * 128,
                                                tb * 512:(tb + 1) * 512], (C, Dd))
                wb_rdy = load_weight(Bb, d_wbT, WB)

                @block.vector
                def _(v):
                    for sm in SE:
                        if sm.n:
                            v.wait_ge(sm.h, sm.n)
                    i = v.memset(EV[:], 0.0)
                    VC.inc(i)
                vc_z = VC.n

                @block.gpsimd
                def _(g):
                    g.wait_ge(VC.h, vc_z)
                    i = g.dma_start(
                        out=mk_sc.rearrange("(a p) c -> p a c", p=128),
                        in_=EV.rearrange("p (a c) -> p a c", c=128))
                    SMK.inc(i, 16)

                prev_vh = None
                prev_vhm = None
                prev_tse = None
                for tb in range(4):
                    tc0, vc0, sc0 = TC.n, VC.n, SC.n
                    rb = C if tb % 2 == 0 else Dd
                    t_h = tc0 + 1
                    t_lg = lambda tc4: tc0 + 2 + tc4
                    t_se = tc0 + 6
                    t_base = lambda oc: tc0 + 6 + oc + 1
                    v_h = vc0 + 1
                    v_mk = lambda tc4: vc0 + 2 + tc4
                    v_hm = vc0 + 6
                    smk0 = SMK.n
                    st_mk = lambda tc4: smk0 + 16 * (tc4 + 1)
                    sg0 = [SG[0].n, SG[1].n]
                    st_g1 = lambda oc: sg0[oc % 2] + 16 * (oc // 2 + 1)
                    sc_g1 = lambda oc: sc0 + oc + 1

                    @block.sync
                    def _(sy, tb=tb, rb=rb, tc0=tc0):
                        sy.wait_ge(TC.h, tc0)
                        for dc in range(NDC):
                            i = sy.dma_start(out=rb[:, dc * 512:(dc + 1) * 512],
                                             in_=rhs_plain(tb, dc), transpose=True)
                            LR[tb % 2].inc(i, 16)
                    ld_rhs = LR[tb % 2].n

                    @block.sync
                    def _(sy, tb=tb, st_mk=st_mk, prev_tse=prev_tse):
                        sy.wait_ge(SMK.h, st_mk(3))
                        if prev_tse is not None:
                            sy.wait_ge(TC.h, prev_tse)
                        i = sy.dma_start(out=mkT[:],
                                         in_=mk_sc[tb * 512:(tb + 1) * 512, :],
                                         transpose=True)
                        LMK.inc(i, 16)
                    ld_mk = LMK.n

                    @block.tensor
                    def _(t, tb=tb, rb=rb, ld_rhs=ld_rhs, ld_mk=ld_mk, tc0=tc0,
                          v_mk=v_mk, v_hm=v_hm, sc_g1=sc_g1,
                          prev_vh=prev_vh, prev_vhm=prev_vhm):
                        t.wait_ge(LR[tb % 2].h, ld_rhs)
                        t.wait_ge(WB.h, wb_rdy)
                        t.wait_ge(WSM.h, wsm_rdy)
                        if prev_vh is not None:
                            t.wait_ge(VC.h, prev_vh)
                        for dc in range(NDC):
                            lh = ABUF[:, dc * 128:(dc + 1) * 128]
                            t.ldweights(lh)
                            i = t.matmul(p512[0][:], lh,
                                         rb[:, dc * 512:(dc + 1) * 512],
                                         start=(dc == 0), stop=(dc == NDC - 1))
                        TC.inc(i)
                        for tc4 in range(4):
                            if tc4 >= 1:
                                t.wait_ge(VC.h, v_mk(tc4 - 1))
                            for dc in range(NDC):
                                lh = rb[:, dc * 512 + tc4 * 128:
                                        dc * 512 + tc4 * 128 + 128]
                                t.ldweights(lh)
                                i = t.matmul(p512[1][0:128, 0:8], lh,
                                             RWB[:, dc * 8:(dc + 1) * 8],
                                             start=(dc == 0), stop=(dc == NDC - 1))
                            TC.inc(i)
                        t.wait_ge(LMK.h, ld_mk)
                        if prev_vhm is not None:
                            t.wait_ge(VC.h, prev_vhm)
                        t.ldweights(R8[0:8, 0:128])
                        i = t.matmul(p512[2][:], R8[0:8, 0:128], mkT[0:8, :],
                                     start=True, stop=True)
                        TC.inc(i)
                        t.wait_ge(VC.h, v_hm)
                        for oc in range(NDC):
                            po = p512[3 + oc % 2]
                            if oc >= 2:
                                t.wait_ge(SC.h, sc_g1(oc - 2))
                            for dc in range(NDC):
                                lh = Bb[:, (dc * NDC + oc) * 128:
                                        (dc * NDC + oc) * 128 + 128]
                                t.ldweights(lh)
                                t.matmul(po[:], lh, rb[:, dc * 512:(dc + 1) * 512],
                                         start=(dc == 0), stop=False)
                            lh = BWB[:, oc * 128:(oc + 1) * 128]
                            t.ldweights(lh)
                            i = t.matmul(po[:], lh, hT[:, tb * 512:(tb + 1) * 512],
                                         start=False, stop=True)
                            TC.inc(i)

                    @block.vector
                    def _(v, tb=tb, tc0=tc0, t_h=t_h, t_lg=t_lg, t_se=t_se,
                          st_mk=st_mk):
                        v.wait_ge(TC.h, t_h)
                        i = v.tensor_copy(hT[:, tb * 512:(tb + 1) * 512],
                                          p512[0][:])
                        VC.inc(i)
                        for tc4 in range(4):
                            v.wait_ge(TC.h, t_lg(tc4))
                            tgt = st_mk(tc4 - 1) if tc4 >= 1 else smk0
                            if tgt:
                                v.wait_ge(SMK.h, tgt)
                            i = v.tensor_copy(lgt[:], p512[1][0:128, 0:8])
                            vedge(v, i)
                            i = v.reduce_max(mxc[:], lgt[:], axis=AXX)
                            vedge(v, i)
                            i = v.tensor_single_scalar(mk8[:], lgt[:],
                                                       mxc[:, 0:1], ALU.is_ge)
                            VC.inc(i)
                        v.wait_ge(TC.h, t_se)
                        i = v.tensor_tensor(hT[:, tb * 512:(tb + 1) * 512],
                                            hT[:, tb * 512:(tb + 1) * 512],
                                            p512[2][:], ALU.mult)
                        VC.inc(i)

                    @block.scalar
                    def _(s, tb=tb, t_base=t_base, st_g1=st_g1):
                        for oc in range(NDC):
                            s.wait_ge(TC.h, t_base(oc))
                            tgt = st_g1(oc - 2) if oc >= 2 else sg0[oc % 2]
                            if tgt:
                                s.wait_ge(SG[oc % 2].h, tgt)
                            i = s.activation(
                                g1t[:, (oc % 2) * 512:(oc % 2) * 512 + 512],
                                p512[3 + oc % 2][:], AF.Silu,
                                bias=SM[:, BB_C + oc:BB_C + oc + 1])
                            SC.inc(i)

                    @block.gpsimd
                    def _(g, tb=tb, v_mk=v_mk, sc_g1=sc_g1):
                        for tc4 in range(4):
                            g.wait_ge(VC.h, v_mk(tc4))
                            i = g.dma_start(
                                out=mk_sc[tb * 512 + tc4 * 128:
                                          tb * 512 + (tc4 + 1) * 128, 0:8],
                                in_=mk8[:])
                            SMK.inc(i, 16)
                        for oc in range(NDC):
                            g.wait_ge(SC.h, sc_g1(oc))
                            i = g.dma_start(
                                out=g1T_sc[oc * 128:(oc + 1) * 128,
                                           tb * 512:(tb + 1) * 512],
                                in_=g1t[:, (oc % 2) * 512:(oc % 2) * 512 + 512])
                            SG[oc % 2].inc(i, 16)

                    prev_vh = v_h
                    prev_vhm = v_hm
                    prev_tse = t_se

        if emit_l1 and phase_hi >= 8:
            HT = TOWN // 2
            with (
                nc.sbuf_tensor("ktl", [128, TOWN], BF16) as ktl,
                nc.sbuf_tensor("vtl", [128, TOWN], BF16) as vtl,
                nc.sbuf_tensor("rtl", [128, TOWN], BF16) as rtl,
                nc.sbuf_tensor("gtl", [128, TOWN], BF16) as gtl,
                nc.sbuf_tensor("ekf", [128, HT], F32) as ekf,
                nc.sbuf_tensor("avf", [128, HT], F32) as avf,
                nc.sbuf_tensor("n0f", [128, HT], F32) as n0f,
                nc.sbuf_tensor("d0f", [128, HT], F32) as d0f,
                nc.sbuf_tensor("wrp", [128, HT], F32) as wrp,
                nc.sbuf_tensor("sth", [128, 4], F32) as sth,
                nc.sbuf_tensor("dta", [128, TOWN], BF16) as dta,
                nc.sbuf_tensor("dtb", [128, TOWN], BF16) as dtb,
            ):
                douts = [dta, dtb]

                @block.scalar
                def _(s):
                    s.wait_ge(LSM.h, p0_ld)
                    i = s.activation(SM[:, ECOL_C:ECOL_C + 16],
                                      SM[:, TD_C:TD_C + 16], AF.Exp)
                    sedge(s, i)
                    i = s.activation(SM[:, NECOL_C:NECOL_C + 16],
                                     SM[:, ECOL_C:ECOL_C + 16], AF.Copy,
                                     scale=-1.0)
                    sedge(s, i)
                    i = s.activation(SM[:, WCOL_C:WCOL_C + 16],
                                     SM[:, ECOL_C:ECOL_C + 16], AF.Exp, scale=-1.0)
                    SC.inc(i)
                sc_w = SC.n

                ld0, sc0, vc0 = L8.n, SC.n, VC.n
                ldA = lambda dc: ld0 + 32 * (dc + 1)
                scA = lambda dc, k: sc0 + 3 * dc + k
                vcA = lambda dc, k: vc0 + 6 * dc + k

                @block.sync
                def _(sy):
                    wait_stores(sy)
                    for dc in range(NDC):
                        if dc >= 1:
                            sy.wait_ge(VC.h, vcA(dc - 1, 6))
                        i = sy.dma_start(out=ktl[:],
                                         in_=kT1_sc[dc * 128:(dc + 1) * 128, :])
                        L8.inc(i, 16)
                        i = sy.dma_start(out=vtl[:],
                                         in_=vT1_sc[dc * 128:(dc + 1) * 128, :])
                        L8.inc(i, 16)

                @block.scalar
                def _(s):
                    s.wait_ge(SC.h, sc_w)
                    for dc in range(NDC):
                        s.wait_ge(L8.h, ldA(dc))
                        if dc >= 1:
                            s.wait_ge(VC.h, vcA(dc - 1, 3))
                        i = s.activation(ekf[:], ktl[:, 0:HT], AF.Exp)
                        SC.inc(i)
                        i = s.activation(wrp[:], ktl[:, 0:HT], AF.Exp, scale=0.0,
                                         bias=SM[:, NECOL_C + dc:NECOL_C + dc + 1])
                        SC.inc(i)
                        s.wait_ge(VC.h, vcA(dc, 3))
                        i = s.activation(ekf[:], ktl[:, HT:TOWN], AF.Exp)
                        SC.inc(i)

                @block.vector
                def _(v):
                    for dc in range(NDC):
                        v.wait_ge(SC.h, scA(dc, 1))
                        v.wait_ge(L8.h, ldA(dc))
                        i = v.tensor_tensor(avf[:], ekf[:], vtl[:, 0:HT], ALU.mult)
                        VC.inc(i)
                        v.wait_ge(VC.h, VC.n)
                        v.wait_ge(SC.h, scA(dc, 2))
                        i = v.tensor_tensor_scan(n0f[:], wrp[:], avf[:], 0.0,
                                                 ALU.mult, ALU.add)
                        VC.inc(i)
                        i = v.tensor_tensor_scan(d0f[:], wrp[:], ekf[:], 0.0,
                                                 ALU.mult, ALU.add)
                        vedge(v, i)
                        v.wait_ge(VC.h, VC.n)
                        v.tensor_copy(sth[:, 0:1], n0f[:, HT - 1:HT])
                        i = v.tensor_copy(sth[:, 1:2], d0f[:, HT - 1:HT])
                        VC.inc(i)
                        v.wait_ge(VC.h, VC.n)
                        v.wait_ge(SC.h, scA(dc, 3))
                        i = v.tensor_tensor(avf[:], ekf[:], vtl[:, HT:TOWN],
                                            ALU.mult)
                        VC.inc(i)
                        v.wait_ge(VC.h, VC.n)
                        i = v.tensor_tensor_scan(n0f[:], wrp[:], avf[:],
                                                 sth[:, 0:1], ALU.mult, ALU.add)
                        VC.inc(i)
                        v.wait_ge(VC.h, VC.n)
                        i = v.tensor_tensor_scan(d0f[:], wrp[:], ekf[:],
                                                 sth[:, 1:2], ALU.mult, ALU.add)
                        vedge(v, i)
                        v.tensor_copy(SM[:, STN_C + dc:STN_C + dc + 1],
                                      n0f[:, HT - 1:HT])
                        i = v.tensor_copy(SM[:, STD_C + dc:STD_C + dc + 1],
                                          d0f[:, HT - 1:HT])
                        VC.inc(i)

                @block.vector
                def _(v):
                    v.wait_ge(VC.h, VC.n)
                    i = None
                    for bp in range(4):
                        i = v.tensor_scalar(
                            SM[:, CC2S_C + bp * 16:CC2S_C + (bp + 1) * 16],
                            SM[:, STN_C:STN_C + 16],
                            SM[:, WSEL_C + bp:WSEL_C + bp + 1], None, ALU.mult)
                        i = v.tensor_scalar(
                            SM[:, CC2S_C + 64 + bp * 16:CC2S_C + 64 + (bp + 1) * 16],
                            SM[:, STD_C:STD_C + 16],
                            SM[:, WSEL_C + bp:WSEL_C + bp + 1], None, ALU.mult)
                    VC.inc(i)
                vc_cc2 = VC.n

                @block.gpsimd
                def _(g):
                    g.wait_ge(VC.h, vc_cc2)
                    i = g.dma_start(out=cc2_in, in_=SM[:, CC2S_C:CC2S_C + 128])
                    CCD.inc(i, 16)
                    g.wait_ge(CCD.h, CCD.n)
                    i = g.collective_compute(
                        "AllReduce", ALU.add,
                        replica_groups=[[0]] if sim1 else [list(range(8))],
                        ins=[cc2_in.opt()], outs=[cc2_out.opt()])
                    CCB.inc(i, 1)
                    g.wait_ge(CCB.h, 1)
                    i = g.dma_start(out=SM[:, RX2_C:RX2_C + 128], in_=cc2_out)
                    CCD.inc(i, 16)
                gc_rx2 = CCD.n

                @block.vector
                def _(v):
                    v.wait_ge(CCD.h, gc_rx2)
                    for (dstc, base) in ((RN_C, RX2_C), (RD_C, RX2_C + 64)):
                        i = v.tensor_scalar(SM[:, dstc:dstc + 16],
                                            SM[:, base:base + 16],
                                            SM[:, RSEL_C:RSEL_C + 1], None,
                                            ALU.mult)
                        vedge(v, i)
                        for bp in range(1, 4):
                            i = v.scalar_tensor_tensor(
                                SM[:, dstc:dstc + 16],
                                SM[:, base + bp * 16:base + (bp + 1) * 16],
                                SM[:, RSEL_C + bp:RSEL_C + bp + 1],
                                SM[:, dstc:dstc + 16], ALU.mult, ALU.add)
                            vedge(v, i)
                        i = v.tensor_scalar(SM[:, dstc:dstc + 16],
                                            SM[:, dstc:dstc + 16],
                                            SM[:, FLAG_C:FLAG_C + 1], None,
                                            ALU.mult)
                        VC.inc(i)
                        v.wait_ge(VC.h, VC.n)

                ld1, sc1, vc1 = L8.n, SC.n, VC.n
                so0 = [SO[0].n, SO[1].n]
                ldB = lambda dc: ld1 + 64 * (dc + 1)
                scB = lambda dc, hf, k: sc1 + 6 * dc + 3 * hf + k
                vcB = lambda dc, hf, k: vc1 + 8 * dc + 4 * hf + k
                stB = lambda dc: so0[dc % 2] + 16 * (dc // 2 + 1)

                @block.sync
                def _(sy):
                    sy.wait_ge(VC.h, vc1)
                    for dc in range(NDC):
                        if dc >= 1:
                            sy.wait_ge(VC.h, vcB(dc - 1, 1, 4))
                        i = sy.dma_start(out=ktl[:],
                                         in_=kT1_sc[dc * 128:(dc + 1) * 128, :])
                        L8.inc(i, 16)
                        i = sy.dma_start(out=vtl[:],
                                         in_=vT1_sc[dc * 128:(dc + 1) * 128, :])
                        L8.inc(i, 16)
                        i = sy.dma_start(out=rtl[:],
                                         in_=rT1_sc[dc * 128:(dc + 1) * 128, :])
                        L8.inc(i, 16)
                        i = sy.dma_start(out=gtl[:],
                                         in_=g1T_sc[dc * 128:(dc + 1) * 128, :])
                        L8.inc(i, 16)

                @block.scalar
                def _(s):
                    for dc in range(NDC):
                        s.wait_ge(L8.h, ldB(dc))
                        for hf in range(2):
                            sl = slice(hf * HT, (hf + 1) * HT)
                            if dc + hf > 0:
                                prev = vcB(dc, 0, 3) if hf else vcB(dc - 1, 1, 3)
                                s.wait_ge(VC.h, prev)
                            i = s.activation(ekf[:], ktl[:, sl], AF.Exp)
                            SC.inc(i)
                            i = s.activation(wrp[:], ktl[:, sl], AF.Exp,
                                             scale=0.0,
                                             bias=SM[:, NECOL_C + dc:NECOL_C + dc + 1])
                            SC.inc(i)
                            s.wait_ge(VC.h, vcB(dc, hf, 3))
                            i = s.activation(d0f[:], d0f[:], AF.Ln,
                                             bias=SM[0:128, EPS2_C:EPS2_C + 1])
                            sedge(s, i)
                            i = s.activation(d0f[:], d0f[:], AF.Exp, scale=-1.0)
                            SC.inc(i)

                @block.vector
                def _(v):
                    for dc in range(NDC):
                        for hf in range(2):
                            sl = slice(hf * HT, (hf + 1) * HT)
                            v.wait_ge(SC.h, scB(dc, hf, 1))
                            i = v.tensor_tensor(avf[:], ekf[:], vtl[:, sl],
                                                ALU.mult)
                            VC.inc(i)
                            v.wait_ge(VC.h, VC.n)
                            v.wait_ge(SC.h, scB(dc, hf, 2))
                            ini_n = (SM[:, RN_C + dc:RN_C + dc + 1] if hf == 0
                                     else sth[:, 2:3])
                            ini_d = (SM[:, RD_C + dc:RD_C + dc + 1] if hf == 0
                                     else sth[:, 3:4])
                            i = v.tensor_tensor_scan(n0f[:], wrp[:], avf[:],
                                                     ini_n, ALU.mult, ALU.add)
                            VC.inc(i)
                            v.wait_ge(VC.h, VC.n)
                            if hf == 0:
                                i = v.tensor_copy(sth[:, 2:3], n0f[:, HT - 1:HT])
                                vedge(v, i)
                            i = v.tensor_tensor_scan(d0f[:], wrp[:], ekf[:],
                                                     ini_d, ALU.mult, ALU.add)
                            if hf == 0:
                                vedge(v, i)
                                i = v.tensor_copy(sth[:, 3:4], d0f[:, HT - 1:HT])
                            VC.inc(i)
                            v.wait_ge(VC.h, VC.n)
                            v.wait_ge(SC.h, scB(dc, hf, 3))
                            i = v.tensor_tensor(n0f[:], n0f[:], d0f[:], ALU.mult)
                            vedge(v, i)
                            i = v.tensor_tensor(n0f[:], n0f[:], rtl[:, sl],
                                                ALU.mult)
                            vedge(v, i)
                            if dc >= 2 and hf == 0:
                                v.wait_ge(SO[dc % 2].h, stB(dc - 2))
                            i = v.tensor_tensor(douts[dc % 2][:, sl], n0f[:],
                                                gtl[:, sl], ALU.mult)
                            VC.inc(i)

                @block.gpsimd
                def _(g):
                    for dc in range(NDC):
                        g.wait_ge(VC.h, vcB(dc, 1, 4))
                        i = g.dma_start(out=dT_sc[dc * 128:(dc + 1) * 128, :],
                                        in_=douts[dc % 2][:])
                        SO[dc % 2].inc(i, 16)

        if emit_l1 and phase_hi >= 9:
            with (
                nc.sbuf_tensor("xr9a", [128, D], F32) as xr9a,
                nc.sbuf_tensor("xr9b", [128, D], F32) as xr9b,
                nc.sbuf_tensor("dl9a", [128, D], BF16) as dl9a,
                nc.sbuf_tensor("dl9b", [128, D], BF16) as dl9b,
                nc.sbuf_tensor("o9a", [128, D], F32) as o9a,
                nc.sbuf_tensor("o9b", [128, D], F32) as o9b,
            ):
                xr9, dl9, o9 = [xr9a, xr9b], [dl9a, dl9b], [o9a, o9b]
                vc0 = VC.n
                lx0 = [LX[0].n, LX[1].n]
                ld9 = lambda tt: lx0[tt % 2] + 80 * (tt // 2 + 1)
                os0 = [OS[0].n, OS[1].n]
                os_at = lambda tt: os0[tt % 2] + 16 * (tt // 2 + 1)

                @block.sync
                def _(sy):
                    wait_stores(sy)
                    for tt in range(16):
                        if tt >= 2:
                            sy.wait_ge(VC.h, vc0 + tt - 1)
                        i = sy.dma_start(out=xr9[tt % 2][:],
                                         in_=x1_tm[tt * 128:(tt + 1) * 128, :])
                        LX[tt % 2].inc(i, 16)
                        for gq in range(4):
                            i = sy.dma_start(
                                out=dl9[tt % 2][:, gq * 512:(gq + 1) * 512],
                                in_=dT_sc[gq * 512:(gq + 1) * 512,
                                          tt * 128:(tt + 1) * 128],
                                transpose=True)
                            LX[tt % 2].inc(i, 16)

                @block.vector
                def _(v):
                    for tt in range(16):
                        v.wait_ge(LX[tt % 2].h, ld9(tt))
                        if tt >= 2:
                            v.wait_ge(OS[tt % 2].h, os_at(tt - 2))
                        i = v.tensor_tensor(o9[tt % 2][:], xr9[tt % 2][:],
                                            dl9[tt % 2][:], ALU.add)
                        VC.inc(i)

                @block.gpsimd
                def _(g):
                    for tt in range(16):
                        g.wait_ge(VC.h, vc0 + tt + 1)
                        i = g.dma_start(out=out_x2[tt * 128:(tt + 1) * 128, :],
                                        in_=o9[tt % 2][:])
                        OS[tt % 2].inc(i, 16)

        @block.sync
        def _(sy):
            for sm in ALL_DMA + [TC, VC, SC, GC, CCA, CCB, CCD, VSs, SSs]:
                if sm.n:
                    sy.wait_ge(sm.h, sm.n)

    return nc


def _prep_weights(inp):
    t = lambda a: np.ascontiguousarray(np.asarray(a))
    w = {}

    def chunk_T(m):
        Dm, O = m.shape
        return t(m.reshape(Dm // 128, 128, O).transpose(1, 0, 2).reshape(128, -1))

    w["wqT"] = chunk_T(t(np.asarray(inp["L0_attn_wq"]).T))
    w["wkT"] = chunk_T(t(np.asarray(inp["L0_attn_wk"]).T))
    w["wvT"] = chunk_T(t(np.asarray(inp["L0_attn_wv"]).T))
    w["woT"] = chunk_T(t(np.asarray(inp["L0_attn_wo"]).T))
    for nm, key in (("w_gate", "L0_ffn_w_gate"), ("w_up", "L0_ffn_w_up")):
        m = np.asarray(inp[key])
        w[nm] = t(m.reshape(16, 128, 16, 512).transpose(1, 2, 0, 3).reshape(128, -1))
    md = np.asarray(inp["L0_ffn_w_down"])
    w["w_down"] = t(md.reshape(64, 128, 16, 128).transpose(1, 2, 0, 3).reshape(128, -1))
    w["sgate_c"] = t(np.asarray(inp["L0_ffn_s_gate"]).reshape(64, 128).T)
    w["sup_c"] = t(np.asarray(inp["L0_ffn_s_up"]).reshape(64, 128).T)
    w["sdown_c"] = t(np.asarray(inp["L0_ffn_s_down"]).reshape(16, 128).T)
    w["wkT1"] = chunk_T(t(np.asarray(inp["L1_attn_Wk"]).T))
    w["wvT1"] = chunk_T(t(np.asarray(inp["L1_attn_Wv"]).T))
    w["wrT1"] = chunk_T(t(np.asarray(inp["L1_attn_Wr"]).T))
    w["wbT"] = chunk_T(t(np.asarray(inp["L1_ffn_W_base"]).T))
    w["td_c"] = t(np.asarray(inp["L1_attn_time_decay"]).reshape(16, 128).T)
    w["bb_c"] = t(np.asarray(inp["L1_ffn_b_base"]).reshape(16, 128).T)
    w["rwT"] = chunk_T(t(np.asarray(inp["L1_ffn_router_W"]).T))
    w["AfT"] = chunk_T(t(np.asarray(inp["L1_ffn_A"]).reshape(E * R, D).T))
    w["BwF"] = t(np.asarray(inp["L1_ffn_Bw"]).transpose(0, 2, 1).reshape(E * R, D))
    sel = np.zeros((8, 128), np.float32)
    for e in range(E):
        sel[e, e * R:(e + 1) * R] = 2.0
    w["selm"] = sel
    return w


def _per_core(inp, b, h):
    x = np.asarray(inp["L0_attn_x"])
    own = x[b, h * TOWN:(h + 1) * TOWN]
    oth = x[b, (1 - h) * TOWN:(2 - h) * TOWN]
    m = {"x_in": np.ascontiguousarray(np.concatenate([own, oth], axis=0))}
    cb = np.zeros((128, 128), np.float32)
    for qi in range(4):
        for j in range(32):
            if j < 16:
                v = SHIFT if j <= 4 * qi + 3 else NEG
            else:
                v = SHIFT if h == 1 else NEG
            cb[:, qi * 32 + j] = v
    m["cb"] = cb
    m["flag"] = np.full((128, 1), float(h), np.float32)
    ws = np.zeros((128, 4), np.float32)
    ws[:, b] = 1.0 - h
    m["wsel"] = ws
    rs = np.zeros((128, 4), np.float32)
    rs[:, b] = 1.0
    m["rsel"] = rs
    return m


def kernel(**inputs):
    w = _prep_weights(inputs)
    in_maps = []
    for c in range(8):
        b, h = c // 2, c % 2
        m = dict(w)
        m.update(_per_core(inputs, b, h))
        in_maps.append(m)
    nc = build(phase_hi=PHASE_HI, dbg=DBG, part=PART)
    res = run_bass_kernel_spmd(nc, in_maps, core_ids=list(range(8)))
    out = np.empty((B, T, D), np.float32)
    for c in range(8):
        b, h = c // 2, c % 2
        out[b, h * TOWN:(h + 1) * TOWN] = np.asarray(res.results[c]["out_x2"])
    return out


# version 02
# speedup vs baseline: 1.0000x; 1.0000x over previous
import numpy as np
import ml_dtypes
from contextlib import ExitStack

import concourse.bass as bass
import concourse.mybir as mybir
from concourse.bass_utils import run_bass_kernel_spmd

F32 = mybir.dt.float32
BF16 = mybir.dt.bfloat16
I8 = mybir.dt.int8
AF = mybir.ActivationFunctionType
ALU = mybir.AluOpType
AXX = mybir.AxisListType.X

B, T, D = 4, 4096, 2048
TOWN = 2048
NDC = 16
NH = 16
INTER = 8192
NIC = 64
E, R = 8, 16
RSQ = float(1.0 / np.sqrt(128.0))
SHIFT = -20.0
NEG = -30000.0

PHASE_HI = 99
DBG = ()
PART = "all"


def build(phase_hi=99, dbg=(), part="all", sim1=False):
    nc = bass.Bass("TRN2", target_bir_lowering=False, debug=False)
    emit_l0 = part in ("all", "l0")
    emit_l1 = part in ("all", "l1")

    def dram(name, shape, dt):
        kind = "ExternalOutput" if name in dbg else "Internal"
        return nc.dram_tensor(name, shape, dt, kind=kind).ap()

    ein = lambda n, s, d: nc.dram_tensor(n, s, d, kind="ExternalInput").ap()

    if emit_l0:
        d_x = ein("x_in", [T, D], F32)
        d_wqT = ein("wqT", [128, NDC * D], F32)
        d_wkT = ein("wkT", [128, NDC * D], F32)
        d_wvT = ein("wvT", [128, NDC * D], F32)
        d_woT = ein("woT", [128, NDC * D], F32)
        d_wg = ein("w_gate", [128, 16 * 16 * 512], I8)
        d_wu = ein("w_up", [128, 16 * 16 * 512], I8)
        d_wd = ein("w_down", [128, 16 * 64 * 128], I8)
        d_sg = ein("sgate_c", [128, NIC], F32)
        d_su = ein("sup_c", [128, NIC], F32)
        d_sd = ein("sdown_c", [128, NDC], F32)
        d_cb = ein("cb", [128, 128], F32)
    if emit_l1:
        d_wkT1 = ein("wkT1", [128, NDC * D], F32)
        d_wvT1 = ein("wvT1", [128, NDC * D], F32)
        d_wrT1 = ein("wrT1", [128, NDC * D], F32)
        d_wbT = ein("wbT", [128, NDC * D], F32)
        d_td = ein("td_c", [128, NDC], F32)
        d_bb = ein("bb_c", [128, NDC], F32)
        d_rw = ein("rwT", [128, NDC * 8], F32)
        d_Af = ein("AfT", [128, NDC * 128], F32)
        d_Bw = ein("BwF", [128, D], F32)
        d_sel = ein("selm", [8, 128], F32)
    d_flag = ein("flag", [128, 1], F32)
    d_wsel = ein("wsel", [128, 4], F32)
    d_rsel = ein("rsel", [128, 4], F32)

    if part == "all":
        x1_tm = dram("x1_tm", [TOWN, D], F32)
        xn1_tm = dram("xn1_tm", [TOWN + 1, D], BF16)
    elif part == "l0":
        x1_tm = nc.dram_tensor("x1_tm", [TOWN, D], F32, kind="ExternalOutput").ap()
        xn1_tm = nc.dram_tensor("xn1_tm", [TOWN + 1, D], BF16,
                                kind="ExternalOutput").ap()
    else:
        x1_tm = ein("x1_tm", [TOWN, D], F32)
        xn1_tm = ein("xn1_tm", [TOWN + 1, D], BF16)
    if emit_l1:
        out_x2 = nc.dram_tensor("out_x2", [TOWN, D], F32, kind="ExternalOutput").ap()

    if emit_l0:
        xn_tm = dram("xn_tm", [T, D], BF16)
        kT_sc = dram("kT_sc", [D, T], BF16)
        vT_sc = dram("vT_sc", [D, T], BF16)
        qT_sc = dram("qT_sc", [D, TOWN], BF16)
        sT_sc = dram("sT_sc", [D, TOWN], BF16)
        cc1_in = dram("cc1_in", [128, 64], F32)
        cc1_out = dram("cc1_out", [128, 64], F32)
    dT_sc = dram("dT_sc", [D, TOWN], BF16)
    if emit_l1:
        kT1_sc = dram("kT1_sc", [D, TOWN], BF16)
        vT1_sc = dram("vT1_sc", [D, TOWN], BF16)
        rT1_sc = dram("rT1_sc", [D, TOWN], BF16)
        g1T_sc = dram("g1T_sc", [D, TOWN], BF16)
        mk_sc = dram("mk_sc", [TOWN, 128], BF16)
        cc2_in = dram("cc2_in", [128, 128], F32)
        cc2_out = dram("cc2_out", [128, 128], F32)

    ctx = ExitStack()
    with ctx:
        sb = lambda n, s, d: ctx.enter_context(nc.sbuf_tensor(n, s, d))
        psm = lambda n, s, d=F32: ctx.enter_context(nc.psum_tensor(n, s, d))

        class Sem:
            def __init__(self, name):
                self.h = ctx.enter_context(nc.semaphore(name))
                self.n = 0

            def inc(self, instr, k=1):
                instr.then_inc(self.h, k)
                self.n += k
                return self.n

        A = sb("arenaA", [128, NDC * D], BF16)
        Bb = sb("arenaB", [128, NDC * D], BF16)
        SM = sb("smalls", [128, 2048], F32)
        SMB = sb("smallsb", [128, 2708], BF16)
        EV = sb("evac", [128, 2048], BF16)
        R8 = sb("row8", [8, 256], BF16)
        RWB = sb("rwb", [128, NDC * 8], BF16)

        p512 = [psm(f"ps{i}", [128, 512]) for i in range(6)]
        pD = psm("psD", [128, 512])
        pB = psm("psB", [128, 512])

        LSM = Sem("LSM")
        LX = [Sem("LX0"), Sem("LX1")]
        GX = [Sem("GX0"), Sem("GX1")]
        GS = [Sem("GS0"), Sem("GS1")]
        LR = [Sem("LR0"), Sem("LR1")]
        KH = [Sem("KH0"), Sem("KH1")]
        LS = [Sem("LS0"), Sem("LS1")]
        LMK = Sem("LMK")
        L8 = Sem("L8")
        WSM = Sem("WSM")
        WA = Sem("WA")
        WB = Sem("WB")
        WG = [Sem("WG0"), Sem("WG1")]
        WU = [Sem("WU0"), Sem("WU1")]
        SX = [Sem("SX0"), Sem("SX1")]
        SE = [Sem(f"SE{i}") for i in range(4)]
        SD = [Sem("SD0"), Sem("SD1")]
        SG = [Sem("SG0"), Sem("SG1")]
        SMK = Sem("SMK")
        SO = [Sem("SO0"), Sem("SO1")]
        OS = [Sem("OS0"), Sem("OS1")]
        STORE_SEMS = SX + SE + SD + SG + [SMK] + SO
        ALL_DMA = ([LSM, LMK, L8, WSM, WA, WB] + LX + GX + LR + KH + LS + WG + WU +
                   STORE_SEMS + GS + OS)

        def wait_stores(eng):
            for sm in STORE_SEMS + GS:
                if sm.n:
                    eng.wait_ge(sm.h, sm.n)

        TC = Sem("TC")
        VC = Sem("VC")
        SC = Sem("SC")
        GC = Sem("GC")
        CCA = Sem("CCA")
        CCB = Sem("CCB")
        CCD = Sem("CCD")
        VSs = Sem("VSs")
        SSs = Sem("SSs")

        def vedge(v, instr):
            VSs.inc(instr)
            v.wait_ge(VSs.h, VSs.n)

        def sedge(s, instr):
            SSs.inc(instr)
            s.wait_ge(SSs.h, SSs.n)

        block = ctx.enter_context(nc.Block())

        (SUM_C, SQ_C, MEAN_C, NMEAN_C, VAR_C, RSTD_C, NMUR_C, T1_C, T2_C) = range(9)
        FLAG_C, WSEL_C, RSEL_C = 10, 12, 16
        EPS1_C, EPS2_C = 20, 21
        CB_C = 24
        IOTA_C = 160
        KPCOL_C = 676
        MCOL_C = 680
        ECOL_C = 690
        NECOL_C = 1970
        WCOL_C = 710
        BND_C = 730
        CCS_C = 750
        RX_C = 820
        ACC_C = 890
        DEN_C = 910
        STN_C = 1430
        STD_C = 1450
        RN_C = 1470
        RD_C = 1490
        SG_C = 1510
        SU_C = 1580
        SD_C = 1650
        BB_C = 1670
        TD_C = 1690
        CC2S_C = 1710
        RX2_C = 1840
        MK0_C = 0
        ONES_C = 2048
        ONESR_C = 2052
        RB_C = 2192

        @block.sync
        def _(sy):
            i = sy.dma_start(out=SM[:, FLAG_C:FLAG_C + 1], in_=d_flag)
            LSM.inc(i, 16)
            i = sy.dma_start(out=SM[:, WSEL_C:WSEL_C + 4], in_=d_wsel)
            LSM.inc(i, 16)
            i = sy.dma_start(out=SM[:, RSEL_C:RSEL_C + 4], in_=d_rsel)
            LSM.inc(i, 16)
            if emit_l0:
                i = sy.dma_start(out=SM[:, CB_C:CB_C + 128], in_=d_cb)
                LSM.inc(i, 16)
                i = sy.dma_start(out=SM[:, SG_C:SG_C + 64], in_=d_sg)
                LSM.inc(i, 16)
                i = sy.dma_start(out=SM[:, SU_C:SU_C + 64], in_=d_su)
                LSM.inc(i, 16)
                i = sy.dma_start(out=SM[:, SD_C:SD_C + 16], in_=d_sd)
                LSM.inc(i, 16)
            if emit_l1:
                i = sy.dma_start(out=SM[:, TD_C:TD_C + 16], in_=d_td)
                LSM.inc(i, 16)
                i = sy.dma_start(out=SM[:, BB_C:BB_C + 16], in_=d_bb)
                LSM.inc(i, 16)

        @block.gpsimd
        def _(g):
            g.iota(SM[:, IOTA_C:IOTA_C + 512], pattern=[[1, 512]], base=0,
                   channel_multiplier=0, allow_small_or_imprecise_dtypes=True)
            i = g.iota(SM[:, KPCOL_C:KPCOL_C + 1], pattern=[[1, 1]], base=0,
                       channel_multiplier=1, allow_small_or_imprecise_dtypes=True)
            GC.inc(i)
            if emit_l1:
                i = g.dma_start(out=R8[0:8, 0:128], in_=d_sel)
                WSM.inc(i, 16)
                i = g.dma_start(out=RWB[:], in_=d_rw)
                WSM.inc(i, 16)
        p0_gc, p0_ld, p0_wt = GC.n, LSM.n, WSM.n

        @block.vector
        def _(v):
            v.memset(SMB[:, ONES_C:ONES_C + 1], 1.0)
            v.memset(SMB[0:1, ONESR_C:ONESR_C + 128], 1.0)
            v.memset(SM[:, EPS1_C:EPS1_C + 1], 1e-5)
            v.memset(SM[:, EPS2_C:EPS2_C + 1], 1e-8)
            v.wait_ge(GC.h, p0_gc)
            v.wait_ge(LSM.h, p0_ld)
            i = None
            for m in range(4):
                i = v.tensor_scalar_add(SM[:, MCOL_C + m:MCOL_C + m + 1],
                                        SM[:, KPCOL_C:KPCOL_C + 1], float(128 * m))
            vedge(v, i)
            for m in range(4):
                i = v.tensor_single_scalar(
                    SMB[:, MK0_C + m * 512:MK0_C + (m + 1) * 512],
                    SM[:, IOTA_C:IOTA_C + 512],
                    SM[:, MCOL_C + m:MCOL_C + m + 1], ALU.is_ge)
            VC.inc(i)
        p0_vc = VC.n

        def load_weight(dst, src, wsem):
            tc_req = TC.n

            @block.gpsimd
            def _(g):
                if tc_req:
                    g.wait_ge(TC.h, tc_req)
                for gi in range(4):
                    i = g.dma_start(out=dst[:, gi * 4 * D:(gi + 1) * 4 * D],
                                    in_=src[:, gi * 4 * D:(gi + 1) * 4 * D])
                    wsem.inc(i, 16)
            return wsem.n

        def emit_ln(n_tiles, load_src, dst_tm, dst_row0, xf, xn16, dum16,
                    pre_vec=None, store_xrow=None, dma_dec=None):
            LXq = LX if dma_dec is None else GX
            SXq = SX if dma_dec is None else GS
            if dma_dec is None:
                dma_dec = block.sync

            pv = 1 if pre_vec is not None else 0
            sc0, vc0 = SC.n, VC.n
            per_st = 16 * (2 if store_xrow is not None else 1)
            sx0 = [SXq[0].n, SXq[1].n]
            st_at = {tt: sx0[tt % 2] + per_st * (tt // 2 + 1)
                     for tt in range(n_tiles)}
            sc_sums = lambda tt: sc0 + 4 * tt + 1
            sc_rstd = lambda tt: sc0 + 4 * tt + 4
            vc_pre = lambda tt: vc0 + (2 + pv) * tt + 1
            vc_varA = lambda tt: vc0 + (2 + pv) * tt + pv + 1
            vc_xn = lambda tt: vc0 + (2 + pv) * tt + pv + 2
            ld_at = {}

            if load_src is not None:
                @dma_dec
                def _(sy):
                    for tt in range(n_tiles + 1):
                        if tt < n_tiles:
                            if tt >= 2:
                                sy.wait_ge(VC.h, vc_xn(tt - 2))
                            i = sy.dma_start(out=xf[tt % 2][:], in_=load_src(tt))
                            ld_at[tt] = (LXq[tt % 2], LXq[tt % 2].inc(i, 16))
                        if tt >= 1:
                            pt = tt - 1
                            sy.wait_ge(VC.h, vc_xn(pt))
                            if store_xrow is not None:
                                i = sy.dma_start(out=store_xrow(pt),
                                                 in_=xf[pt % 2][:])
                                SXq[pt % 2].inc(i, 16)
                            i = sy.dma_start(
                                out=dst_tm[dst_row0 + pt * 128:
                                           dst_row0 + (pt + 1) * 128, :],
                                in_=xn16[pt % 2][:])
                            SXq[pt % 2].inc(i, 16)

            @block.scalar
            def _(s):
                for tt in range(n_tiles):
                    if load_src is not None:
                        s.wait_ge(ld_at[tt][0].h, ld_at[tt][1])
                    else:
                        s.wait_ge(VC.h, vc_pre(tt))
                    if tt >= 1:
                        s.wait_ge(VC.h, vc_varA(tt - 1))
                    i = s.activation(dum16[0][:], xf[tt % 2][:], AF.Copy,
                                     accum_out=SM[:, SUM_C:SUM_C + 1])
                    sedge(s, i)
                    i = s.activation(dum16[0][:], xf[tt % 2][:], AF.Square,
                                     accum_out=SM[:, SQ_C:SQ_C + 1])
                    SC.inc(i)
                    s.wait_ge(VC.h, vc_varA(tt))
                    i = s.activation(SM[:, T1_C:T1_C + 1], SM[:, VAR_C:VAR_C + 1],
                                     AF.Ln, bias=SM[:, EPS1_C:EPS1_C + 1])
                    sedge(s, i)
                    i = s.activation(SM[:, RSTD_C:RSTD_C + 1],
                                     SM[:, T1_C:T1_C + 1], AF.Exp, scale=-0.5)
                    SC.inc(i, 3)

            @block.vector
            def _(v):
                for tt in range(n_tiles):
                    if pre_vec is not None:
                        if tt >= 2:
                            v.wait_ge(SXq[tt % 2].h, st_at[tt - 2])
                        pre_vec(v, tt)
                    v.wait_ge(SC.h, sc_sums(tt))
                    i = v.tensor_scalar_mul(SM[:, MEAN_C:MEAN_C + 1],
                                            SM[:, SUM_C:SUM_C + 1], 1.0 / D)
                    vedge(v, i)
                    i = v.tensor_scalar_mul(SM[:, NMEAN_C:NMEAN_C + 1],
                                            SM[:, MEAN_C:MEAN_C + 1], -1.0)
                    i = v.tensor_scalar_mul(SM[:, T2_C:T2_C + 1],
                                            SM[:, SQ_C:SQ_C + 1], 1.0 / D)
                    vedge(v, i)
                    i = v.scalar_tensor_tensor(
                        SM[:, VAR_C:VAR_C + 1], SM[:, MEAN_C:MEAN_C + 1],
                        SM[:, NMEAN_C:NMEAN_C + 1], SM[:, T2_C:T2_C + 1],
                        ALU.mult, ALU.add)
                    VC.inc(i)
                    v.wait_ge(SC.h, sc_rstd(tt))
                    i = v.tensor_scalar(SM[:, NMUR_C:NMUR_C + 1],
                                        SM[:, NMEAN_C:NMEAN_C + 1],
                                        SM[:, RSTD_C:RSTD_C + 1], None, ALU.mult)
                    vedge(v, i)
                    if tt >= 2 and pre_vec is None:
                        v.wait_ge(SXq[tt % 2].h, st_at[tt - 2])
                    i = v.tensor_scalar(xn16[tt % 2][:], xf[tt % 2][:],
                                        SM[:, RSTD_C:RSTD_C + 1],
                                        SM[:, NMUR_C:NMUR_C + 1],
                                        ALU.mult, ALU.add)
                    VC.inc(i)

            emit_ln_last_st_at = st_at

            def store_tile(sy, tt):
                sy.wait_ge(VC.h, vc_xn(tt))
                if store_xrow is not None:
                    i = sy.dma_start(out=store_xrow(tt), in_=xf[tt % 2][:])
                    SXq[tt % 2].inc(i, 16)
                i = sy.dma_start(
                    out=dst_tm[dst_row0 + tt * 128:
                               dst_row0 + (tt + 1) * 128, :],
                    in_=xn16[tt % 2][:])
                SXq[tt % 2].inc(i, 16)
            store_tile.st_at = st_at
            store_tile.slots = SXq
            return store_tile

        def emit_proj(warena, wsem, wt_ready, rhs_src, t_blocks, dstf, rbufs,
                      rhs_gate=None):
            tc0, vc0 = TC.n, VC.n
            lr0 = [LR[0].n, LR[1].n]
            ld_at = {tb: lr0[tb % 2] + 256 * (tb // 2 + 1)
                     for tb in range(t_blocks)}
            se0 = [sm.n for sm in SE]

            @block.sync
            def _(sy):
                for tb in range(t_blocks):
                    rb = rbufs[tb % 2]
                    if tb >= 2:
                        sy.wait_ge(TC.h, tc0 + 16 * (tb - 1))
                    elif tc0:
                        sy.wait_ge(TC.h, tc0)
                    if rhs_gate is not None:
                        for (gs, gt) in rhs_gate(tb):
                            sy.wait_ge(gs.h, gt)
                    for dc in range(NDC):
                        i = sy.dma_start(out=rb[:, dc * 512:(dc + 1) * 512],
                                         in_=rhs_src(tb, dc), transpose=True)
                        LR[tb % 2].inc(i, 16)

            @block.tensor
            def _(t):
                t.wait_ge(wsem.h, wt_ready)
                for tb in range(t_blocks):
                    rb = rbufs[tb % 2]
                    t.wait_ge(LR[tb % 2].h, ld_at[tb])
                    for oc in range(NDC):
                        k = tb * 16 + oc
                        pp = p512[k % 4]
                        if k >= 4:
                            t.wait_ge(VC.h, vc0 + k - 3)
                        for dc in range(NDC):
                            lh = warena[:, (dc * NDC + oc) * 128:
                                        (dc * NDC + oc) * 128 + 128]
                            t.ldweights(lh)
                            i = t.matmul(pp[:], lh, rb[:, dc * 512:(dc + 1) * 512],
                                         start=(dc == 0), stop=(dc == NDC - 1))
                        TC.inc(i)

            @block.vector
            def _(v):
                for k in range(t_blocks * 16):
                    v.wait_ge(TC.h, tc0 + k + 1)
                    tgt = se0[k % 4] + 16 * (k // 4)
                    if tgt:
                        v.wait_ge(SE[k % 4].h, tgt)
                    i = v.tensor_copy(EV[:, (k % 4) * 512:(k % 4 + 1) * 512],
                                      p512[k % 4][:])
                    VC.inc(i)

            @block.gpsimd
            def _(g):
                for tb in range(t_blocks):
                    for oc in range(NDC):
                        k = tb * 16 + oc
                        g.wait_ge(VC.h, vc0 + k + 1)
                        i = g.dma_start(out=dstf(oc, tb),
                                        in_=EV[:, (k % 4) * 512:(k % 4 + 1) * 512])
                        SE[k % 4].inc(i, 16)

        wq_rdy = load_weight(A, d_wqT, WA) if (emit_l0 and phase_hi >= 2) else 0
        if emit_l0 and phase_hi >= 1:
            with (
                nc.sbuf_tensor("xfa", [128, D], F32) as xfa,
                nc.sbuf_tensor("xfb", [128, D], F32) as xfb,
                nc.sbuf_tensor("xna", [128, D], BF16) as xna,
                nc.sbuf_tensor("xnb", [128, D], BF16) as xnb,
                nc.sbuf_tensor("dum", [128, D], BF16) as dum,
                nc.sbuf_tensor("rhsC", [128, NDC * 512], BF16) as C,
                nc.sbuf_tensor("rhsD", [128, NDC * 512], BF16) as Dd,
            ):
                p1_store = emit_ln(
                    T // 128, lambda tt: d_x[tt * 128:(tt + 1) * 128, :],
                    xn_tm, 0, [xfa, xfb], [xna, xnb], (dum, dum),
                    dma_dec=block.gpsimd)
                p1_st_at = p1_store.st_at

                rhs_src = lambda tb, dc: xn_tm[tb * 512:(tb + 1) * 512,
                                               dc * 128:(dc + 1) * 128]

                p1_slots = p1_store.slots

                def p1_gate(tb):
                    tt = tb * 4 + 3
                    return [(p1_slots[tt % 2], p1_st_at[tt]),
                            (p1_slots[(tt - 1) % 2], p1_st_at[tt - 1])]

                if phase_hi >= 2:
                    emit_proj(A, WA, wq_rdy, rhs_src, 4,
                              lambda oc, tb: qT_sc[oc * 128:(oc + 1) * 128,
                                                   tb * 512:(tb + 1) * 512],
                              (C, Dd), rhs_gate=p1_gate)
                    wk_rdy = load_weight(Bb, d_wkT, WB)
                    emit_proj(Bb, WB, wk_rdy, rhs_src, 8,
                              lambda oc, tb: kT_sc[oc * 128:(oc + 1) * 128,
                                                   tb * 512:(tb + 1) * 512],
                              (C, Dd), rhs_gate=p1_gate)
                    wv_rdy = load_weight(A, d_wvT, WA)
                    emit_proj(A, WA, wv_rdy, rhs_src, 8,
                              lambda oc, tb: vT_sc[oc * 128:(oc + 1) * 128,
                                                   tb * 512:(tb + 1) * 512],
                              (C, Dd), rhs_gate=p1_gate)

        if emit_l0 and phase_hi >= 3:
            wo_rdy = load_weight(Bb, d_woT, WB)
            with (
                nc.sbuf_tensor("kth0", [128, T], BF16) as kth0,
                nc.sbuf_tensor("kth1", [128, T], BF16) as kth1,
                nc.sbuf_tensor("vh0", [128, T], BF16) as vh0,
                nc.sbuf_tensor("vh1", [128, T], BF16) as vh1,
                nc.sbuf_tensor("qh0", [128, TOWN], BF16) as qh0,
                nc.sbuf_tensor("qh1", [128, TOWN], BF16) as qh1,
                nc.sbuf_tensor("ptil", [128, 4 * 512], BF16) as ptil,
                nc.sbuf_tensor("bb16", [128, 512], F32) as bb16t,
                nc.sbuf_tensor("rden", [1, 512], F32) as rden,
            ):
                kth, vhh, qhh = [kth0, kth1], [vh0, vh1], [qh0, qh1]
                NCH = 32
                tc0, sc0, vc0 = TC.n, SC.n, VC.n
                kh0 = [KH[0].n, KH[1].n]
                ld_head = {h: kh0[h % 2] + 16 * 34 * (h // 2 + 1)
                           for h in range(NH)}

                def seq_totals(hq):
                    base = tc0 + hq * 65
                    qk, pv = {}, {}
                    k = 0
                    for j in range(NCH):
                        k += 1
                        qk[j] = base + k
                        if j >= 1:
                            k += 1
                            pv[j - 1] = base + k
                    k += 1
                    pv[NCH - 1] = base + k
                    k += 1
                    bc = base + k
                    return qk, pv, bc

                sc_exp = lambda hq, j: sc0 + hq * 34 + j + 1
                sc_recip = lambda hq: sc0 + hq * 34 + 33
                sc_rb = lambda hq: sc0 + hq * 34 + 34
                vc_mask = lambda hq, m: vc0 + hq * 7 + m + 1
                vc_den = lambda hq: vc0 + hq * 7 + 5
                vc_bb = lambda hq: vc0 + hq * 7 + 6
                vc_ao = lambda hq: vc0 + hq * 7 + 7

                @block.sync
                def _(sy):
                    wait_stores(sy)
                    for h in range(NH):
                        if h >= 2:
                            sy.wait_ge(TC.h, tc0 + (h - 1) * 260)
                        i = sy.dma_start(out=kth[h % 2][:],
                                         in_=kT_sc[h * 128:(h + 1) * 128, :])
                        KH[h % 2].inc(i, 16)
                        for j in range(NCH):
                            i = sy.dma_start(
                                out=vhh[h % 2][:, j * 128:(j + 1) * 128],
                                in_=vT_sc[h * 128:(h + 1) * 128,
                                          j * 128:(j + 1) * 128],
                                transpose=True)
                            KH[h % 2].inc(i, 16)
                        i = sy.dma_start(out=qhh[h % 2][:],
                                         in_=qT_sc[h * 128:(h + 1) * 128, :])
                        KH[h % 2].inc(i, 16)

                def emit_pv(t, h, qi, j, hq, qk_t, pv_t):
                    m = j - 4 * qi
                    if 0 <= m <= 3:
                        t.wait_ge(VC.h, vc_mask(hq, m))
                    else:
                        t.wait_ge(SC.h, sc_exp(hq, j))
                    if j == 0:
                        if hq >= 1:
                            t.wait_ge(VC.h, vc_den(hq - 1))
                        if hq >= 2:
                            t.wait_ge(VC.h, vc_ao(hq - 2))
                    pt = ptil[:, (j % 4) * 512:(j % 4 + 1) * 512]
                    po = p512[2 + (qi % 2)]
                    lhv = vhh[h % 2][:, j * 128:(j + 1) * 128]
                    t.ldweights(lhv)
                    t.matmul(po[:], lhv, pt, start=(j == 0), stop=(j == NCH - 1))
                    t.ldweights(SMB[:, ONES_C:ONES_C + 1])
                    i = t.matmul(pD[0:1, :], SMB[:, ONES_C:ONES_C + 1], pt,
                                 start=(j == 0), stop=(j == NCH - 1))
                    TC.inc(i)

                @block.tensor
                def _(t):
                    for h in range(NH):
                        t.wait_ge(KH[h % 2].h, ld_head[h])
                        for qi in range(4):
                            hq = h * 4 + qi
                            qk_t, pv_t, bc_t = seq_totals(hq)
                            for j in range(NCH):
                                sp = p512[j % 2]
                                if j >= 2:
                                    t.wait_ge(SC.h, sc_exp(hq, j - 2))
                                elif hq > 0:
                                    t.wait_ge(SC.h, sc_exp(hq - 1, 30 + j))
                                lhk = kth[h % 2][:, j * 128:(j + 1) * 128]
                                t.ldweights(lhk)
                                i = t.matmul(sp[:], lhk,
                                             qhh[h % 2][:, qi * 512:(qi + 1) * 512],
                                             start=True, stop=True)
                                TC.inc(i)
                                if j >= 1:
                                    emit_pv(t, h, qi, j - 1, hq, qk_t, pv_t)
                            emit_pv(t, h, qi, NCH - 1, hq, qk_t, pv_t)
                            t.wait_ge(SC.h, sc_rb(hq))
                            if hq >= 1:
                                t.wait_ge(VC.h, vc_bb(hq - 1))
                            t.ldweights(SMB[0:1, ONESR_C:ONESR_C + 128])
                            i = t.matmul(pB[:], SMB[0:1, ONESR_C:ONESR_C + 128],
                                         SMB[0:1, RB_C:RB_C + 512],
                                         start=True, stop=True)
                            TC.inc(i)
                            assert TC.n == bc_t

                @block.scalar
                def _(s):
                    for h in range(NH):
                        for qi in range(4):
                            hq = h * 4 + qi
                            qk_t, pv_t, bc_t = seq_totals(hq)
                            pqk, ppv, _ = seq_totals(hq - 1) if hq else (None, None, None)
                            for j in range(NCH):
                                s.wait_ge(TC.h, qk_t[j])
                                if j >= 4:
                                    s.wait_ge(TC.h, pv_t[j - 4])
                                elif hq > 0:
                                    s.wait_ge(TC.h, ppv[28 + j])
                                i = s.activation(
                                    ptil[:, (j % 4) * 512:(j % 4 + 1) * 512],
                                    p512[j % 2][:], AF.Exp,
                                    bias=SM[:, CB_C + qi * 32 + j:
                                            CB_C + qi * 32 + j + 1],
                                    scale=RSQ)
                                SC.inc(i)
                            s.wait_ge(VC.h, vc_den(hq))
                            i = s.activation(rden[0:1, :],
                                             SM[0:1, DEN_C:DEN_C + 512], AF.Ln)
                            sedge(s, i)
                            i = s.activation(rden[0:1, :], rden[0:1, :],
                                             AF.Exp, scale=-1.0)
                            SC.inc(i)
                            s.wait_ge(SC.h, SC.n)
                            i = s.copy(SMB[0:1, RB_C:RB_C + 512], rden[0:1, :])
                            SC.inc(i)

                @block.vector
                def _(v):
                    for h in range(NH):
                        for qi in range(4):
                            hq = h * 4 + qi
                            qk_t, pv_t, bc_t = seq_totals(hq)
                            for m in range(4):
                                j = 4 * qi + m
                                v.wait_ge(SC.h, sc_exp(hq, j))
                                pt = ptil[:, (j % 4) * 512:(j % 4 + 1) * 512]
                                i = v.tensor_mul(pt, pt,
                                                 SMB[:, MK0_C + m * 512:
                                                     MK0_C + (m + 1) * 512])
                                VC.inc(i)
                            v.wait_ge(TC.h, pv_t[NCH - 1])
                            i = v.tensor_copy(SM[0:1, DEN_C:DEN_C + 512],
                                              pD[0:1, :])
                            VC.inc(i)
                            v.wait_ge(TC.h, bc_t)
                            i = v.tensor_copy(bb16t[:], pB[:])
                            VC.inc(i)
                            v.wait_ge(VC.h, VC.n)
                            i = v.tensor_tensor(
                                A[:, h * TOWN + qi * 512:h * TOWN + (qi + 1) * 512],
                                p512[2 + (qi % 2)][:], bb16t[:], ALU.mult)
                            VC.inc(i)
        p3_vc = VC.n
        p3_wt = WB.n

        if emit_l0 and phase_hi >= 4:
            tc0, vc0 = TC.n, VC.n
            se0 = [sm.n for sm in SE]

            @block.tensor
            def _(t):
                t.wait_ge(WB.h, p3_wt)
                t.wait_ge(VC.h, p3_vc)
                for tb in range(4):
                    for oc in range(NDC):
                        k = tb * 16 + oc
                        pp = p512[k % 4]
                        if k >= 4:
                            t.wait_ge(VC.h, vc0 + k - 3)
                        for dc in range(NDC):
                            lh = Bb[:, (dc * NDC + oc) * 128:
                                    (dc * NDC + oc) * 128 + 128]
                            t.ldweights(lh)
                            i = t.matmul(pp[:], lh,
                                         A[:, dc * TOWN + tb * 512:
                                           dc * TOWN + (tb + 1) * 512],
                                         start=(dc == 0), stop=(dc == NDC - 1))
                        TC.inc(i)

            @block.vector
            def _(v):
                for k in range(64):
                    v.wait_ge(TC.h, tc0 + k + 1)
                    tgt = se0[k % 4] + 16 * (k // 4)
                    if tgt:
                        v.wait_ge(SE[k % 4].h, tgt)
                    i = v.tensor_copy(EV[:, (k % 4) * 512:(k % 4 + 1) * 512],
                                      p512[k % 4][:])
                    VC.inc(i)

            @block.gpsimd
            def _(g):
                for tb in range(4):
                    for oc in range(NDC):
                        k = tb * 16 + oc
                        g.wait_ge(VC.h, vc0 + k + 1)
                        i = g.dma_start(out=sT_sc[oc * 128:(oc + 1) * 128,
                                                  tb * 512:(tb + 1) * 512],
                                        in_=EV[:, (k % 4) * 512:(k % 4 + 1) * 512])
                        SE[k % 4].inc(i, 16)

        if emit_l0 and phase_hi >= 5:
            wg = [Bb[:, 0:8192], Bb[:, 8192:16384]]
            wu = [Bb[:, 16384:24576], Bb[:, 24576:32768]]
            wdl = [Bb[:, 0:8192], Bb[:, 8192:16384]]
            with (
                nc.sbuf_tensor("rhsC5", [128, NDC * 512], BF16) as C,
                nc.sbuf_tensor("gt5", [128, 2 * 512], BF16) as gt,
                nc.sbuf_tensor("ut5", [128, 2 * 512], BF16) as ut,
                nc.sbuf_tensor("stl5", [128, 2 * 512], BF16) as stl,
                nc.sbuf_tensor("dtl5", [128, 2 * 512], BF16) as dtl,
            ):
                for tq in range(4):
                    tc0, sc0, vc0 = TC.n, SC.n, VC.n
                    lr00 = LR[0].n
                    ld_rhs = lr00 + 16 * NDC
                    ls0 = [LS[0].n, LS[1].n]
                    ld_st = lambda oc: ls0[oc % 2] + 16 * (oc // 2 + 1)
                    wg0 = [WG[0].n, WG[1].n]
                    wu0 = [WU[0].n, WU[1].n]
                    wt_ib_g = lambda ib: wg0[ib % 2] + 16 * (ib // 2 + 1)
                    wt_ib_u = lambda ib: wu0[ib % 2] + 16 * (ib // 2 + 1)
                    wt_oc = lambda oc: wg0[oc % 2] + 16 * (8 + oc // 2 + 1)
                    tc_gu = lambda ic, gu: tc0 + ic * 2 + gu + 1
                    tc_oc = lambda oc: tc0 + 128 + oc + 1
                    sc_gu = lambda ic, gu: sc0 + ic * 2 + gu + 1
                    sc_oc = lambda oc: sc0 + 128 + oc + 1
                    vc_f = lambda ic: vc0 + ic + 1
                    vc_oc = lambda oc: vc0 + 64 + oc + 1
                    sd0 = [SD[0].n, SD[1].n]
                    st_oc = lambda oc: sd0[oc % 2] + 16 * (oc // 2 + 1)

                    @block.sync
                    def _(sy, tq=tq, tc0=tc0, vc0=vc0):
                        if tq == 0:
                            wait_stores(sy)
                        else:
                            sy.wait_ge(TC.h, tc0)
                        for dc in range(NDC):
                            i = sy.dma_start(
                                out=C[:, dc * 512:(dc + 1) * 512],
                                in_=xn_tm[tq * 512:(tq + 1) * 512,
                                          dc * 128:(dc + 1) * 128],
                                transpose=True)
                            LR[0].inc(i, 16)
                        for oc in range(NDC):
                            if oc >= 2:
                                sy.wait_ge(VC.h, vc0 + 64 + oc - 1)
                            i = sy.dma_start(
                                out=stl[:, (oc % 2) * 512:(oc % 2) * 512 + 512],
                                in_=sT_sc[oc * 128:(oc + 1) * 128,
                                          tq * 512:(tq + 1) * 512])
                            LS[oc % 2].inc(i, 16)
                            if oc >= 2:
                                po = oc - 2
                                i = sy.dma_start(
                                    out=dT_sc[po * 128:(po + 1) * 128,
                                              tq * 512:(tq + 1) * 512],
                                    in_=dtl[:, (po % 2) * 512:(po % 2) * 512 + 512])
                                SD[po % 2].inc(i, 16)
                        for po in (14, 15):
                            sy.wait_ge(VC.h, vc0 + 64 + po + 1)
                            i = sy.dma_start(
                                out=dT_sc[po * 128:(po + 1) * 128,
                                          tq * 512:(tq + 1) * 512],
                                in_=dtl[:, (po % 2) * 512:(po % 2) * 512 + 512])
                            SD[po % 2].inc(i, 16)

                    @block.gpsimd
                    def _(g, tq=tq, tc0=tc0):
                        for ib in range(16):
                            if ib >= 2:
                                g.wait_ge(TC.h, tc0 + (ib - 1) * 8)
                            elif tc0:
                                g.wait_ge(TC.h, tc0)
                            i = g.dma_start(out=wg[ib % 2],
                                            in_=d_wg[:, ib * 8192:(ib + 1) * 8192])
                            WG[ib % 2].inc(i, 16)
                            i = g.dma_start(out=wu[ib % 2],
                                            in_=d_wu[:, ib * 8192:(ib + 1) * 8192])
                            WU[ib % 2].inc(i, 16)
                        for oc in range(16):
                            if oc < 2:
                                g.wait_ge(TC.h, tc0 + 128)
                            else:
                                g.wait_ge(TC.h, tc0 + 128 + oc - 1)
                            i = g.dma_start(out=wdl[oc % 2],
                                            in_=d_wd[:, oc * 8192:(oc + 1) * 8192])
                            WG[oc % 2].inc(i, 16)

                    @block.tensor
                    def _(t, tq=tq, ld_rhs=ld_rhs, wt_ib_g=wt_ib_g,
                          wt_ib_u=wt_ib_u, wt_oc=wt_oc,
                          sc_gu=sc_gu, sc_oc=sc_oc, vc_f=vc_f):
                        t.wait_ge(LR[0].h, ld_rhs)
                        for ib in range(16):
                            t.wait_ge(WG[ib % 2].h, wt_ib_g(ib))
                            t.wait_ge(WU[ib % 2].h, wt_ib_u(ib))
                            for icc in range(4):
                                ic = ib * 4 + icc
                                pg = p512[ic % 2]
                                pu = p512[2 + ic % 2]
                                if ic >= 2:
                                    t.wait_ge(SC.h, sc_gu(ic - 2, 1))
                                for dc in range(NDC):
                                    lh = wg[ib % 2][:, dc * 512 + icc * 128:
                                                    dc * 512 + icc * 128 + 128]
                                    t.ldweights(lh)
                                    i = t.matmul(pg[:], lh,
                                                 C[:, dc * 512:(dc + 1) * 512],
                                                 start=(dc == 0),
                                                 stop=(dc == NDC - 1))
                                TC.inc(i)
                                for dc in range(NDC):
                                    lh = wu[ib % 2][:, dc * 512 + icc * 128:
                                                    dc * 512 + icc * 128 + 128]
                                    t.ldweights(lh)
                                    i = t.matmul(pu[:], lh,
                                                 C[:, dc * 512:(dc + 1) * 512],
                                                 start=(dc == 0),
                                                 stop=(dc == NDC - 1))
                                TC.inc(i)
                        for oc in range(16):
                            t.wait_ge(WG[oc % 2].h, wt_oc(oc))
                            if oc == 0:
                                t.wait_ge(VC.h, vc_f(63))
                            py = p512[4 + oc % 2]
                            if oc >= 2:
                                t.wait_ge(SC.h, sc_oc(oc - 2))
                            for ic in range(NIC):
                                lh = wdl[oc % 2][:, ic * 128:(ic + 1) * 128]
                                t.ldweights(lh)
                                i = t.matmul(py[:], lh, A[:, ic * 512:(ic + 1) * 512],
                                             start=(ic == 0), stop=(ic == NIC - 1))
                            TC.inc(i)

                    @block.scalar
                    def _(s, tq=tq, tc_gu=tc_gu, tc_oc=tc_oc, vc_f=vc_f,
                          vc_oc=vc_oc):
                        for ic in range(NIC):
                            s.wait_ge(TC.h, tc_gu(ic, 0))
                            s.wait_ge(VC.h, vc_f(ic - 2) if ic >= 2 else vc0)
                            i = s.activation(gt[:, (ic % 2) * 512:(ic % 2) * 512 + 512],
                                             p512[ic % 2][:], AF.Silu,
                                             scale=SM[:, SG_C + ic:SG_C + ic + 1])
                            SC.inc(i)
                            s.wait_ge(TC.h, tc_gu(ic, 1))
                            i = s.activation(ut[:, (ic % 2) * 512:(ic % 2) * 512 + 512],
                                             p512[2 + ic % 2][:], AF.Copy,
                                             scale=SM[:, SU_C + ic:SU_C + ic + 1])
                            SC.inc(i)
                        for oc in range(16):
                            s.wait_ge(TC.h, tc_oc(oc))
                            if oc >= 2:
                                s.wait_ge(VC.h, vc_oc(oc - 2))
                            i = s.activation(gt[:, (oc % 2) * 512:(oc % 2) * 512 + 512],
                                             p512[4 + oc % 2][:], AF.Silu,
                                             scale=SM[:, SD_C + oc:SD_C + oc + 1])
                            SC.inc(i)

                    @block.vector
                    def _(v, tq=tq, sc_gu=sc_gu, sc_oc=sc_oc, ld_st=ld_st,
                          st_oc=st_oc):
                        for ic in range(NIC):
                            v.wait_ge(SC.h, sc_gu(ic, 1))
                            i = v.tensor_mul(A[:, ic * 512:(ic + 1) * 512],
                                             gt[:, (ic % 2) * 512:(ic % 2) * 512 + 512],
                                             ut[:, (ic % 2) * 512:(ic % 2) * 512 + 512])
                            VC.inc(i)
                        for oc in range(16):
                            v.wait_ge(SC.h, sc_oc(oc))
                            v.wait_ge(LS[oc % 2].h, ld_st(oc))
                            tgt = st_oc(oc - 2) if oc >= 2 else sd0[oc % 2]
                            if tgt:
                                v.wait_ge(SD[oc % 2].h, tgt)
                            i = v.tensor_mul(dtl[:, (oc % 2) * 512:(oc % 2) * 512 + 512],
                                             gt[:, (oc % 2) * 512:(oc % 2) * 512 + 512],
                                             stl[:, (oc % 2) * 512:(oc % 2) * 512 + 512])
                            VC.inc(i)


        if emit_l0 and phase_hi >= 6:
            with (
                nc.sbuf_tensor("xf6a", [128, D], F32) as xf6a,
                nc.sbuf_tensor("xf6b", [128, D], F32) as xf6b,
                nc.sbuf_tensor("xr6a", [128, D], F32) as xr6a,
                nc.sbuf_tensor("xr6b", [128, D], F32) as xr6b,
                nc.sbuf_tensor("dl6a", [128, D], BF16) as dl6a,
                nc.sbuf_tensor("dl6b", [128, D], BF16) as dl6b,
                nc.sbuf_tensor("xn6a", [128, D], BF16) as xn6a,
                nc.sbuf_tensor("xn6b", [128, D], BF16) as xn6b,
                nc.sbuf_tensor("dum6", [128, D], BF16) as dum6,
                nc.sbuf_tensor("bnd", [128, 16], BF16) as bnd,
            ):
                xf6, xr6, dl6 = [xf6a, xf6b], [xr6a, xr6b], [dl6a, dl6b]
                vc0 = VC.n
                lx0 = [LX[0].n, LX[1].n]
                ld_at6 = {tt: lx0[tt % 2] + 80 * (tt // 2 + 1)
                          for tt in range(16)}
                vc_pre6 = lambda tt: vc0 + 3 * tt + 1

                def pre_vec(v, tt):
                    v.wait_ge(LX[tt % 2].h, ld_at6[tt])
                    i = v.tensor_tensor(xf6[tt % 2][:], xr6[tt % 2][:],
                                        dl6[tt % 2][:], ALU.add)
                    VC.inc(i)

                store_tile = emit_ln(
                    16, None, xn1_tm, 1, xf6, [xn6a, xn6b], (dum6, dum6),
                    pre_vec=pre_vec,
                    store_xrow=lambda tt: x1_tm[tt * 128:(tt + 1) * 128, :])

                @block.sync
                def _(sy):
                    wait_stores(sy)
                    for tt in range(17):
                        if tt < 16:
                            if tt >= 2:
                                sy.wait_ge(VC.h, vc_pre6(tt - 2))
                            i = sy.dma_start(out=xr6[tt % 2][:],
                                             in_=d_x[tt * 128:(tt + 1) * 128, :])
                            LX[tt % 2].inc(i, 16)
                            for gq in range(4):
                                i = sy.dma_start(
                                    out=dl6[tt % 2][:, gq * 512:(gq + 1) * 512],
                                    in_=dT_sc[gq * 512:(gq + 1) * 512,
                                              tt * 128:(tt + 1) * 128],
                                    transpose=True)
                                LX[tt % 2].inc(i, 16)
                        if tt >= 1:
                            store_tile(sy, tt - 1)

                @block.sync
                def _(sy):
                    wait_stores(sy)
                    i = sy.dma_start(
                        out=bnd[:],
                        in_=xn1_tm[TOWN:TOWN + 1, :].rearrange(
                            "one (p c) -> (one p) c", p=128))
                    LSM.inc(i, 16)
                ld_bnd = LSM.n
                vc0b = VC.n

                @block.vector
                def _(v):
                    v.wait_ge(LSM.h, ld_bnd)
                    i = v.tensor_copy(SM[:, BND_C:BND_C + 16], bnd[:])
                    vedge(v, i)
                    for bp in range(4):
                        i = v.tensor_scalar(
                            SM[:, CCS_C + bp * 16:CCS_C + (bp + 1) * 16],
                            SM[:, BND_C:BND_C + 16],
                            SM[:, WSEL_C + bp:WSEL_C + bp + 1], None, ALU.mult)
                    VC.inc(i)
                vc_ccs = VC.n

                @block.gpsimd
                def _(g):
                    g.wait_ge(VC.h, vc_ccs)
                    i = g.dma_start(out=cc1_in, in_=SM[:, CCS_C:CCS_C + 64])
                    CCD.inc(i, 16)
                    g.wait_ge(CCD.h, CCD.n)
                    i = g.collective_compute(
                        "AllReduce", ALU.add,
                        replica_groups=[[0]] if sim1 else [list(range(8))],
                        ins=[cc1_in.opt()], outs=[cc1_out.opt()])
                    CCA.inc(i, 1)
                    g.wait_ge(CCA.h, 1)
                    i = g.dma_start(out=SM[:, RX_C:RX_C + 64], in_=cc1_out)
                    CCD.inc(i, 16)
                gc_rx1 = CCD.n

                @block.vector
                def _(v):
                    v.wait_ge(CCD.h, gc_rx1)
                    i = v.tensor_scalar(SM[:, ACC_C:ACC_C + 16],
                                        SM[:, RX_C:RX_C + 16],
                                        SM[:, RSEL_C:RSEL_C + 1], None, ALU.mult)
                    vedge(v, i)
                    for bp in range(1, 4):
                        i = v.scalar_tensor_tensor(
                            SM[:, ACC_C:ACC_C + 16],
                            SM[:, RX_C + bp * 16:RX_C + (bp + 1) * 16],
                            SM[:, RSEL_C + bp:RSEL_C + bp + 1],
                            SM[:, ACC_C:ACC_C + 16], ALU.mult, ALU.add)
                        vedge(v, i)
                    i = v.tensor_scalar(bnd[:], SM[:, ACC_C:ACC_C + 16],
                                        SM[:, FLAG_C:FLAG_C + 1], None, ALU.mult)
                    VC.inc(i)
                vc_bnd = VC.n

                @block.sync
                def _(sy):
                    sy.wait_ge(VC.h, vc_bnd)
                    i = sy.dma_start(
                        out=xn1_tm[0:1, :].rearrange("one (p c) -> (one p) c",
                                                     p=128),
                        in_=bnd[:])
                    SX[0].inc(i, 16)
        p6_bnd_st = SX[0].n

        if emit_l1 and phase_hi >= 7:
            with (
                nc.sbuf_tensor("rhsC7", [128, NDC * 512], BF16) as C,
                nc.sbuf_tensor("rhsD7", [128, NDC * 512], BF16) as Dd,
                nc.sbuf_tensor("abuf", [128, NDC * 128], BF16) as ABUF,
                nc.sbuf_tensor("bwb", [128, D], BF16) as BWB,
                nc.sbuf_tensor("hT", [128, TOWN], BF16) as hT,
                nc.sbuf_tensor("lgt", [128, 8], F32) as lgt,
                nc.sbuf_tensor("mxc", [128, 1], F32) as mxc,
                nc.sbuf_tensor("mk8", [128, 8], BF16) as mk8,
                nc.sbuf_tensor("mkT", [128, 512], BF16) as mkT,
                nc.sbuf_tensor("g1t", [128, 2 * 512], BF16) as g1t,
            ):
                @block.gpsimd
                def _(g):
                    tcn = TC.n
                    if tcn:
                        g.wait_ge(TC.h, tcn)
                    i = g.dma_start(out=ABUF[:], in_=d_Af)
                    WSM.inc(i, 16)
                    i = g.dma_start(out=BWB[:], in_=d_Bw)
                    WSM.inc(i, 16)
                wsm_rdy = WSM.n

                rhs_shift = lambda tb, dc: xn1_tm[tb * 512:(tb + 1) * 512,
                                                  dc * 128:(dc + 1) * 128]
                rhs_plain = lambda tb, dc: xn1_tm[1 + tb * 512:1 + (tb + 1) * 512,
                                                  dc * 128:(dc + 1) * 128]

                p6_gate = lambda tb: [(SX[0], p6_bnd_st)]
                wk1_rdy = load_weight(A, d_wkT1, WA)
                emit_proj(A, WA, wk1_rdy, rhs_shift, 4,
                          lambda oc, tb: kT1_sc[oc * 128:(oc + 1) * 128,
                                                tb * 512:(tb + 1) * 512], (C, Dd),
                          rhs_gate=p6_gate)
                wv1_rdy = load_weight(Bb, d_wvT1, WB)
                emit_proj(Bb, WB, wv1_rdy, rhs_shift, 4,
                          lambda oc, tb: vT1_sc[oc * 128:(oc + 1) * 128,
                                                tb * 512:(tb + 1) * 512], (C, Dd),
                          rhs_gate=p6_gate)
                wr1_rdy = load_weight(A, d_wrT1, WA)
                emit_proj(A, WA, wr1_rdy, rhs_plain, 4,
                          lambda oc, tb: rT1_sc[oc * 128:(oc + 1) * 128,
                                                tb * 512:(tb + 1) * 512], (C, Dd))
                wb_rdy = load_weight(Bb, d_wbT, WB)

                @block.vector
                def _(v):
                    for sm in SE:
                        if sm.n:
                            v.wait_ge(sm.h, sm.n)
                    i = v.memset(EV[:], 0.0)
                    VC.inc(i)
                vc_z = VC.n

                @block.gpsimd
                def _(g):
                    g.wait_ge(VC.h, vc_z)
                    i = g.dma_start(
                        out=mk_sc.rearrange("(a p) c -> p a c", p=128),
                        in_=EV.rearrange("p (a c) -> p a c", c=128))
                    SMK.inc(i, 16)

                prev_vh = None
                prev_vhm = None
                prev_tse = None
                for tb in range(4):
                    tc0, vc0, sc0 = TC.n, VC.n, SC.n
                    rb = C if tb % 2 == 0 else Dd
                    t_h = tc0 + 1
                    t_lg = lambda tc4: tc0 + 2 + tc4
                    t_se = tc0 + 6
                    t_base = lambda oc: tc0 + 6 + oc + 1
                    v_h = vc0 + 1
                    v_mk = lambda tc4: vc0 + 2 + tc4
                    v_hm = vc0 + 6
                    smk0 = SMK.n
                    st_mk = lambda tc4: smk0 + 16 * (tc4 + 1)
                    sg0 = [SG[0].n, SG[1].n]
                    st_g1 = lambda oc: sg0[oc % 2] + 16 * (oc // 2 + 1)
                    sc_g1 = lambda oc: sc0 + oc + 1

                    @block.sync
                    def _(sy, tb=tb, rb=rb, tc0=tc0):
                        sy.wait_ge(TC.h, tc0)
                        for dc in range(NDC):
                            i = sy.dma_start(out=rb[:, dc * 512:(dc + 1) * 512],
                                             in_=rhs_plain(tb, dc), transpose=True)
                            LR[tb % 2].inc(i, 16)
                    ld_rhs = LR[tb % 2].n

                    @block.sync
                    def _(sy, tb=tb, st_mk=st_mk, prev_tse=prev_tse):
                        sy.wait_ge(SMK.h, st_mk(3))
                        if prev_tse is not None:
                            sy.wait_ge(TC.h, prev_tse)
                        i = sy.dma_start(out=mkT[:],
                                         in_=mk_sc[tb * 512:(tb + 1) * 512, :],
                                         transpose=True)
                        LMK.inc(i, 16)
                    ld_mk = LMK.n

                    @block.tensor
                    def _(t, tb=tb, rb=rb, ld_rhs=ld_rhs, ld_mk=ld_mk, tc0=tc0,
                          v_mk=v_mk, v_hm=v_hm, sc_g1=sc_g1,
                          prev_vh=prev_vh, prev_vhm=prev_vhm):
                        t.wait_ge(LR[tb % 2].h, ld_rhs)
                        t.wait_ge(WB.h, wb_rdy)
                        t.wait_ge(WSM.h, wsm_rdy)
                        if prev_vh is not None:
                            t.wait_ge(VC.h, prev_vh)
                        for dc in range(NDC):
                            lh = ABUF[:, dc * 128:(dc + 1) * 128]
                            t.ldweights(lh)
                            i = t.matmul(p512[0][:], lh,
                                         rb[:, dc * 512:(dc + 1) * 512],
                                         start=(dc == 0), stop=(dc == NDC - 1))
                        TC.inc(i)
                        for tc4 in range(4):
                            if tc4 >= 1:
                                t.wait_ge(VC.h, v_mk(tc4 - 1))
                            for dc in range(NDC):
                                lh = rb[:, dc * 512 + tc4 * 128:
                                        dc * 512 + tc4 * 128 + 128]
                                t.ldweights(lh)
                                i = t.matmul(p512[1][0:128, 0:8], lh,
                                             RWB[:, dc * 8:(dc + 1) * 8],
                                             start=(dc == 0), stop=(dc == NDC - 1))
                            TC.inc(i)
                        t.wait_ge(LMK.h, ld_mk)
                        if prev_vhm is not None:
                            t.wait_ge(VC.h, prev_vhm)
                        t.ldweights(R8[0:8, 0:128])
                        i = t.matmul(p512[2][:], R8[0:8, 0:128], mkT[0:8, :],
                                     start=True, stop=True)
                        TC.inc(i)
                        t.wait_ge(VC.h, v_hm)
                        for oc in range(NDC):
                            po = p512[3 + oc % 2]
                            if oc >= 2:
                                t.wait_ge(SC.h, sc_g1(oc - 2))
                            for dc in range(NDC):
                                lh = Bb[:, (dc * NDC + oc) * 128:
                                        (dc * NDC + oc) * 128 + 128]
                                t.ldweights(lh)
                                t.matmul(po[:], lh, rb[:, dc * 512:(dc + 1) * 512],
                                         start=(dc == 0), stop=False)
                            lh = BWB[:, oc * 128:(oc + 1) * 128]
                            t.ldweights(lh)
                            i = t.matmul(po[:], lh, hT[:, tb * 512:(tb + 1) * 512],
                                         start=False, stop=True)
                            TC.inc(i)

                    @block.vector
                    def _(v, tb=tb, tc0=tc0, t_h=t_h, t_lg=t_lg, t_se=t_se,
                          st_mk=st_mk):
                        v.wait_ge(TC.h, t_h)
                        i = v.tensor_copy(hT[:, tb * 512:(tb + 1) * 512],
                                          p512[0][:])
                        VC.inc(i)
                        for tc4 in range(4):
                            v.wait_ge(TC.h, t_lg(tc4))
                            tgt = st_mk(tc4 - 1) if tc4 >= 1 else smk0
                            if tgt:
                                v.wait_ge(SMK.h, tgt)
                            i = v.tensor_copy(lgt[:], p512[1][0:128, 0:8])
                            vedge(v, i)
                            i = v.reduce_max(mxc[:], lgt[:], axis=AXX)
                            vedge(v, i)
                            i = v.tensor_single_scalar(mk8[:], lgt[:],
                                                       mxc[:, 0:1], ALU.is_ge)
                            VC.inc(i)
                        v.wait_ge(TC.h, t_se)
                        i = v.tensor_tensor(hT[:, tb * 512:(tb + 1) * 512],
                                            hT[:, tb * 512:(tb + 1) * 512],
                                            p512[2][:], ALU.mult)
                        VC.inc(i)

                    @block.scalar
                    def _(s, tb=tb, t_base=t_base, st_g1=st_g1):
                        for oc in range(NDC):
                            s.wait_ge(TC.h, t_base(oc))
                            tgt = st_g1(oc - 2) if oc >= 2 else sg0[oc % 2]
                            if tgt:
                                s.wait_ge(SG[oc % 2].h, tgt)
                            i = s.activation(
                                g1t[:, (oc % 2) * 512:(oc % 2) * 512 + 512],
                                p512[3 + oc % 2][:], AF.Silu,
                                bias=SM[:, BB_C + oc:BB_C + oc + 1])
                            SC.inc(i)

                    @block.gpsimd
                    def _(g, tb=tb, v_mk=v_mk, sc_g1=sc_g1):
                        for tc4 in range(4):
                            g.wait_ge(VC.h, v_mk(tc4))
                            i = g.dma_start(
                                out=mk_sc[tb * 512 + tc4 * 128:
                                          tb * 512 + (tc4 + 1) * 128, 0:8],
                                in_=mk8[:])
                            SMK.inc(i, 16)
                        for oc in range(NDC):
                            g.wait_ge(SC.h, sc_g1(oc))
                            i = g.dma_start(
                                out=g1T_sc[oc * 128:(oc + 1) * 128,
                                           tb * 512:(tb + 1) * 512],
                                in_=g1t[:, (oc % 2) * 512:(oc % 2) * 512 + 512])
                            SG[oc % 2].inc(i, 16)

                    prev_vh = v_h
                    prev_vhm = v_hm
                    prev_tse = t_se

        if emit_l1 and phase_hi >= 8:
            HT = TOWN // 2
            with (
                nc.sbuf_tensor("ktl", [128, TOWN], BF16) as ktl,
                nc.sbuf_tensor("vtl", [128, TOWN], BF16) as vtl,
                nc.sbuf_tensor("rtl", [128, TOWN], BF16) as rtl,
                nc.sbuf_tensor("gtl", [128, TOWN], BF16) as gtl,
                nc.sbuf_tensor("ekf", [128, HT], F32) as ekf,
                nc.sbuf_tensor("avf", [128, HT], F32) as avf,
                nc.sbuf_tensor("n0f", [128, HT], F32) as n0f,
                nc.sbuf_tensor("d0f", [128, HT], F32) as d0f,
                nc.sbuf_tensor("wrp", [128, HT], F32) as wrp,
                nc.sbuf_tensor("sth", [128, 4], F32) as sth,
                nc.sbuf_tensor("dta", [128, TOWN], BF16) as dta,
                nc.sbuf_tensor("dtb", [128, TOWN], BF16) as dtb,
            ):
                douts = [dta, dtb]

                @block.scalar
                def _(s):
                    s.wait_ge(LSM.h, p0_ld)
                    i = s.activation(SM[:, ECOL_C:ECOL_C + 16],
                                      SM[:, TD_C:TD_C + 16], AF.Exp)
                    sedge(s, i)
                    i = s.activation(SM[:, NECOL_C:NECOL_C + 16],
                                     SM[:, ECOL_C:ECOL_C + 16], AF.Copy,
                                     scale=-1.0)
                    sedge(s, i)
                    i = s.activation(SM[:, WCOL_C:WCOL_C + 16],
                                     SM[:, ECOL_C:ECOL_C + 16], AF.Exp, scale=-1.0)
                    SC.inc(i)
                sc_w = SC.n

                ld0, sc0, vc0 = L8.n, SC.n, VC.n
                ldA = lambda dc: ld0 + 32 * (dc + 1)
                scA = lambda dc, k: sc0 + 3 * dc + k
                vcA = lambda dc, k: vc0 + 6 * dc + k

                @block.sync
                def _(sy):
                    wait_stores(sy)
                    for dc in range(NDC):
                        if dc >= 1:
                            sy.wait_ge(VC.h, vcA(dc - 1, 6))
                        i = sy.dma_start(out=ktl[:],
                                         in_=kT1_sc[dc * 128:(dc + 1) * 128, :])
                        L8.inc(i, 16)
                        i = sy.dma_start(out=vtl[:],
                                         in_=vT1_sc[dc * 128:(dc + 1) * 128, :])
                        L8.inc(i, 16)

                @block.scalar
                def _(s):
                    s.wait_ge(SC.h, sc_w)
                    for dc in range(NDC):
                        s.wait_ge(L8.h, ldA(dc))
                        if dc >= 1:
                            s.wait_ge(VC.h, vcA(dc - 1, 3))
                        i = s.activation(ekf[:], ktl[:, 0:HT], AF.Exp)
                        SC.inc(i)
                        i = s.activation(wrp[:], ktl[:, 0:HT], AF.Exp, scale=0.0,
                                         bias=SM[:, NECOL_C + dc:NECOL_C + dc + 1])
                        SC.inc(i)
                        s.wait_ge(VC.h, vcA(dc, 3))
                        i = s.activation(ekf[:], ktl[:, HT:TOWN], AF.Exp)
                        SC.inc(i)

                @block.vector
                def _(v):
                    for dc in range(NDC):
                        v.wait_ge(SC.h, scA(dc, 1))
                        v.wait_ge(L8.h, ldA(dc))
                        i = v.tensor_tensor(avf[:], ekf[:], vtl[:, 0:HT], ALU.mult)
                        VC.inc(i)
                        v.wait_ge(VC.h, VC.n)
                        v.wait_ge(SC.h, scA(dc, 2))
                        i = v.tensor_tensor_scan(n0f[:], wrp[:], avf[:], 0.0,
                                                 ALU.mult, ALU.add)
                        VC.inc(i)
                        i = v.tensor_tensor_scan(d0f[:], wrp[:], ekf[:], 0.0,
                                                 ALU.mult, ALU.add)
                        vedge(v, i)
                        v.wait_ge(VC.h, VC.n)
                        v.tensor_copy(sth[:, 0:1], n0f[:, HT - 1:HT])
                        i = v.tensor_copy(sth[:, 1:2], d0f[:, HT - 1:HT])
                        VC.inc(i)
                        v.wait_ge(VC.h, VC.n)
                        v.wait_ge(SC.h, scA(dc, 3))
                        i = v.tensor_tensor(avf[:], ekf[:], vtl[:, HT:TOWN],
                                            ALU.mult)
                        VC.inc(i)
                        v.wait_ge(VC.h, VC.n)
                        i = v.tensor_tensor_scan(n0f[:], wrp[:], avf[:],
                                                 sth[:, 0:1], ALU.mult, ALU.add)
                        VC.inc(i)
                        v.wait_ge(VC.h, VC.n)
                        i = v.tensor_tensor_scan(d0f[:], wrp[:], ekf[:],
                                                 sth[:, 1:2], ALU.mult, ALU.add)
                        vedge(v, i)
                        v.tensor_copy(SM[:, STN_C + dc:STN_C + dc + 1],
                                      n0f[:, HT - 1:HT])
                        i = v.tensor_copy(SM[:, STD_C + dc:STD_C + dc + 1],
                                          d0f[:, HT - 1:HT])
                        VC.inc(i)

                @block.vector
                def _(v):
                    v.wait_ge(VC.h, VC.n)
                    i = None
                    for bp in range(4):
                        i = v.tensor_scalar(
                            SM[:, CC2S_C + bp * 16:CC2S_C + (bp + 1) * 16],
                            SM[:, STN_C:STN_C + 16],
                            SM[:, WSEL_C + bp:WSEL_C + bp + 1], None, ALU.mult)
                        i = v.tensor_scalar(
                            SM[:, CC2S_C + 64 + bp * 16:CC2S_C + 64 + (bp + 1) * 16],
                            SM[:, STD_C:STD_C + 16],
                            SM[:, WSEL_C + bp:WSEL_C + bp + 1], None, ALU.mult)
                    VC.inc(i)
                vc_cc2 = VC.n

                @block.gpsimd
                def _(g):
                    g.wait_ge(VC.h, vc_cc2)
                    i = g.dma_start(out=cc2_in, in_=SM[:, CC2S_C:CC2S_C + 128])
                    CCD.inc(i, 16)
                    g.wait_ge(CCD.h, CCD.n)
                    i = g.collective_compute(
                        "AllReduce", ALU.add,
                        replica_groups=[[0]] if sim1 else [list(range(8))],
                        ins=[cc2_in.opt()], outs=[cc2_out.opt()])
                    CCB.inc(i, 1)
                    g.wait_ge(CCB.h, 1)
                    i = g.dma_start(out=SM[:, RX2_C:RX2_C + 128], in_=cc2_out)
                    CCD.inc(i, 16)
                gc_rx2 = CCD.n

                @block.vector
                def _(v):
                    v.wait_ge(CCD.h, gc_rx2)
                    for (dstc, base) in ((RN_C, RX2_C), (RD_C, RX2_C + 64)):
                        i = v.tensor_scalar(SM[:, dstc:dstc + 16],
                                            SM[:, base:base + 16],
                                            SM[:, RSEL_C:RSEL_C + 1], None,
                                            ALU.mult)
                        vedge(v, i)
                        for bp in range(1, 4):
                            i = v.scalar_tensor_tensor(
                                SM[:, dstc:dstc + 16],
                                SM[:, base + bp * 16:base + (bp + 1) * 16],
                                SM[:, RSEL_C + bp:RSEL_C + bp + 1],
                                SM[:, dstc:dstc + 16], ALU.mult, ALU.add)
                            vedge(v, i)
                        i = v.tensor_scalar(SM[:, dstc:dstc + 16],
                                            SM[:, dstc:dstc + 16],
                                            SM[:, FLAG_C:FLAG_C + 1], None,
                                            ALU.mult)
                        VC.inc(i)
                        v.wait_ge(VC.h, VC.n)

                ld1, sc1, vc1 = L8.n, SC.n, VC.n
                so0 = [SO[0].n, SO[1].n]
                ldB = lambda dc: ld1 + 64 * (dc + 1)
                scB = lambda dc, hf, k: sc1 + 6 * dc + 3 * hf + k
                vcB = lambda dc, hf, k: vc1 + 8 * dc + 4 * hf + k
                stB = lambda dc: so0[dc % 2] + 16 * (dc // 2 + 1)

                @block.sync
                def _(sy):
                    sy.wait_ge(VC.h, vc1)
                    for dc in range(NDC):
                        if dc >= 1:
                            sy.wait_ge(VC.h, vcB(dc - 1, 1, 4))
                        i = sy.dma_start(out=ktl[:],
                                         in_=kT1_sc[dc * 128:(dc + 1) * 128, :])
                        L8.inc(i, 16)
                        i = sy.dma_start(out=vtl[:],
                                         in_=vT1_sc[dc * 128:(dc + 1) * 128, :])
                        L8.inc(i, 16)
                        i = sy.dma_start(out=rtl[:],
                                         in_=rT1_sc[dc * 128:(dc + 1) * 128, :])
                        L8.inc(i, 16)
                        i = sy.dma_start(out=gtl[:],
                                         in_=g1T_sc[dc * 128:(dc + 1) * 128, :])
                        L8.inc(i, 16)

                @block.scalar
                def _(s):
                    for dc in range(NDC):
                        s.wait_ge(L8.h, ldB(dc))
                        for hf in range(2):
                            sl = slice(hf * HT, (hf + 1) * HT)
                            if dc + hf > 0:
                                prev = vcB(dc, 0, 3) if hf else vcB(dc - 1, 1, 3)
                                s.wait_ge(VC.h, prev)
                            i = s.activation(ekf[:], ktl[:, sl], AF.Exp)
                            SC.inc(i)
                            i = s.activation(wrp[:], ktl[:, sl], AF.Exp,
                                             scale=0.0,
                                             bias=SM[:, NECOL_C + dc:NECOL_C + dc + 1])
                            SC.inc(i)
                            s.wait_ge(VC.h, vcB(dc, hf, 3))
                            i = s.activation(d0f[:], d0f[:], AF.Ln,
                                             bias=SM[0:128, EPS2_C:EPS2_C + 1])
                            sedge(s, i)
                            i = s.activation(d0f[:], d0f[:], AF.Exp, scale=-1.0)
                            SC.inc(i)

                @block.vector
                def _(v):
                    for dc in range(NDC):
                        for hf in range(2):
                            sl = slice(hf * HT, (hf + 1) * HT)
                            v.wait_ge(SC.h, scB(dc, hf, 1))
                            i = v.tensor_tensor(avf[:], ekf[:], vtl[:, sl],
                                                ALU.mult)
                            VC.inc(i)
                            v.wait_ge(VC.h, VC.n)
                            v.wait_ge(SC.h, scB(dc, hf, 2))
                            ini_n = (SM[:, RN_C + dc:RN_C + dc + 1] if hf == 0
                                     else sth[:, 2:3])
                            ini_d = (SM[:, RD_C + dc:RD_C + dc + 1] if hf == 0
                                     else sth[:, 3:4])
                            i = v.tensor_tensor_scan(n0f[:], wrp[:], avf[:],
                                                     ini_n, ALU.mult, ALU.add)
                            VC.inc(i)
                            v.wait_ge(VC.h, VC.n)
                            if hf == 0:
                                i = v.tensor_copy(sth[:, 2:3], n0f[:, HT - 1:HT])
                                vedge(v, i)
                            i = v.tensor_tensor_scan(d0f[:], wrp[:], ekf[:],
                                                     ini_d, ALU.mult, ALU.add)
                            if hf == 0:
                                vedge(v, i)
                                i = v.tensor_copy(sth[:, 3:4], d0f[:, HT - 1:HT])
                            VC.inc(i)
                            v.wait_ge(VC.h, VC.n)
                            v.wait_ge(SC.h, scB(dc, hf, 3))
                            i = v.tensor_tensor(n0f[:], n0f[:], d0f[:], ALU.mult)
                            vedge(v, i)
                            i = v.tensor_tensor(n0f[:], n0f[:], rtl[:, sl],
                                                ALU.mult)
                            vedge(v, i)
                            if dc >= 2 and hf == 0:
                                v.wait_ge(SO[dc % 2].h, stB(dc - 2))
                            i = v.tensor_tensor(douts[dc % 2][:, sl], n0f[:],
                                                gtl[:, sl], ALU.mult)
                            VC.inc(i)

                @block.gpsimd
                def _(g):
                    for dc in range(NDC):
                        g.wait_ge(VC.h, vcB(dc, 1, 4))
                        i = g.dma_start(out=dT_sc[dc * 128:(dc + 1) * 128, :],
                                        in_=douts[dc % 2][:])
                        SO[dc % 2].inc(i, 16)

        if emit_l1 and phase_hi >= 9:
            with (
                nc.sbuf_tensor("xr9a", [128, D], F32) as xr9a,
                nc.sbuf_tensor("xr9b", [128, D], F32) as xr9b,
                nc.sbuf_tensor("dl9a", [128, D], BF16) as dl9a,
                nc.sbuf_tensor("dl9b", [128, D], BF16) as dl9b,
                nc.sbuf_tensor("o9a", [128, D], F32) as o9a,
                nc.sbuf_tensor("o9b", [128, D], F32) as o9b,
            ):
                xr9, dl9, o9 = [xr9a, xr9b], [dl9a, dl9b], [o9a, o9b]
                vc0 = VC.n
                lx0 = [LX[0].n, LX[1].n]
                ld9 = lambda tt: lx0[tt % 2] + 80 * (tt // 2 + 1)
                os0 = [OS[0].n, OS[1].n]
                os_at = lambda tt: os0[tt % 2] + 16 * (tt // 2 + 1)

                @block.sync
                def _(sy):
                    wait_stores(sy)
                    for tt in range(16):
                        if tt >= 2:
                            sy.wait_ge(VC.h, vc0 + tt - 1)
                        i = sy.dma_start(out=xr9[tt % 2][:],
                                         in_=x1_tm[tt * 128:(tt + 1) * 128, :])
                        LX[tt % 2].inc(i, 16)
                        for gq in range(4):
                            i = sy.dma_start(
                                out=dl9[tt % 2][:, gq * 512:(gq + 1) * 512],
                                in_=dT_sc[gq * 512:(gq + 1) * 512,
                                          tt * 128:(tt + 1) * 128],
                                transpose=True)
                            LX[tt % 2].inc(i, 16)

                @block.vector
                def _(v):
                    for tt in range(16):
                        v.wait_ge(LX[tt % 2].h, ld9(tt))
                        if tt >= 2:
                            v.wait_ge(OS[tt % 2].h, os_at(tt - 2))
                        i = v.tensor_tensor(o9[tt % 2][:], xr9[tt % 2][:],
                                            dl9[tt % 2][:], ALU.add)
                        VC.inc(i)

                @block.gpsimd
                def _(g):
                    for tt in range(16):
                        g.wait_ge(VC.h, vc0 + tt + 1)
                        i = g.dma_start(out=out_x2[tt * 128:(tt + 1) * 128, :],
                                        in_=o9[tt % 2][:])
                        OS[tt % 2].inc(i, 16)

        @block.sync
        def _(sy):
            for sm in ALL_DMA + [TC, VC, SC, GC, CCA, CCB, CCD, VSs, SSs]:
                if sm.n:
                    sy.wait_ge(sm.h, sm.n)

    return nc


def _prep_weights(inp):
    t = lambda a: np.ascontiguousarray(np.asarray(a))
    w = {}

    def chunk_T(m):
        Dm, O = m.shape
        return t(m.reshape(Dm // 128, 128, O).transpose(1, 0, 2).reshape(128, -1))

    w["wqT"] = chunk_T(t(np.asarray(inp["L0_attn_wq"]).T))
    w["wkT"] = chunk_T(t(np.asarray(inp["L0_attn_wk"]).T))
    w["wvT"] = chunk_T(t(np.asarray(inp["L0_attn_wv"]).T))
    w["woT"] = chunk_T(t(np.asarray(inp["L0_attn_wo"]).T))
    for nm, key in (("w_gate", "L0_ffn_w_gate"), ("w_up", "L0_ffn_w_up")):
        m = np.asarray(inp[key])
        w[nm] = t(m.reshape(16, 128, 16, 512).transpose(1, 2, 0, 3).reshape(128, -1))
    md = np.asarray(inp["L0_ffn_w_down"])
    w["w_down"] = t(md.reshape(64, 128, 16, 128).transpose(1, 2, 0, 3).reshape(128, -1))
    w["sgate_c"] = t(np.asarray(inp["L0_ffn_s_gate"]).reshape(64, 128).T)
    w["sup_c"] = t(np.asarray(inp["L0_ffn_s_up"]).reshape(64, 128).T)
    w["sdown_c"] = t(np.asarray(inp["L0_ffn_s_down"]).reshape(16, 128).T)
    w["wkT1"] = chunk_T(t(np.asarray(inp["L1_attn_Wk"]).T))
    w["wvT1"] = chunk_T(t(np.asarray(inp["L1_attn_Wv"]).T))
    w["wrT1"] = chunk_T(t(np.asarray(inp["L1_attn_Wr"]).T))
    w["wbT"] = chunk_T(t(np.asarray(inp["L1_ffn_W_base"]).T))
    w["td_c"] = t(np.asarray(inp["L1_attn_time_decay"]).reshape(16, 128).T)
    w["bb_c"] = t(np.asarray(inp["L1_ffn_b_base"]).reshape(16, 128).T)
    w["rwT"] = chunk_T(t(np.asarray(inp["L1_ffn_router_W"]).T))
    w["AfT"] = chunk_T(t(np.asarray(inp["L1_ffn_A"]).reshape(E * R, D).T))
    w["BwF"] = t(np.asarray(inp["L1_ffn_Bw"]).transpose(0, 2, 1).reshape(E * R, D))
    sel = np.zeros((8, 128), np.float32)
    for e in range(E):
        sel[e, e * R:(e + 1) * R] = 2.0
    w["selm"] = sel
    return w


def _per_core(inp, b, h):
    x = np.asarray(inp["L0_attn_x"])
    own = x[b, h * TOWN:(h + 1) * TOWN]
    oth = x[b, (1 - h) * TOWN:(2 - h) * TOWN]
    m = {"x_in": np.ascontiguousarray(np.concatenate([own, oth], axis=0))}
    cb = np.zeros((128, 128), np.float32)
    for qi in range(4):
        for j in range(32):
            if j < 16:
                v = SHIFT if j <= 4 * qi + 3 else NEG
            else:
                v = SHIFT if h == 1 else NEG
            cb[:, qi * 32 + j] = v
    m["cb"] = cb
    m["flag"] = np.full((128, 1), float(h), np.float32)
    ws = np.zeros((128, 4), np.float32)
    ws[:, b] = 1.0 - h
    m["wsel"] = ws
    rs = np.zeros((128, 4), np.float32)
    rs[:, b] = 1.0
    m["rsel"] = rs
    return m


def kernel(**inputs):
    w = _prep_weights(inputs)
    in_maps = []
    for c in range(8):
        b, h = c // 2, c % 2
        m = dict(w)
        m.update(_per_core(inputs, b, h))
        in_maps.append(m)
    nc = build(phase_hi=PHASE_HI, dbg=DBG, part=PART)
    res = run_bass_kernel_spmd(nc, in_maps, core_ids=list(range(8)))
    out = np.empty((B, T, D), np.float32)
    for c in range(8):
        b, h = c // 2, c % 2
        out[b, h * TOWN:(h + 1) * TOWN] = np.asarray(res.results[c]["out_x2"])
    return out


# version 03
# speedup vs baseline: 1.0023x; 1.0023x over previous
import numpy as np
import ml_dtypes
from contextlib import ExitStack

import concourse.bass as bass
import concourse.mybir as mybir
from concourse.bass_utils import run_bass_kernel_spmd

F32 = mybir.dt.float32
BF16 = mybir.dt.bfloat16
I8 = mybir.dt.int8
AF = mybir.ActivationFunctionType
ALU = mybir.AluOpType
AXX = mybir.AxisListType.X

B, T, D = 4, 4096, 2048
TOWN = 2048
NDC = 16
NH = 16
INTER = 8192
NIC = 64
E, R = 8, 16
RSQ = float(1.0 / np.sqrt(128.0))
SHIFT = -20.0
NEG = -30000.0

PHASE_HI = 99
DBG = ()
PART = "all"


class _EndL0(Exception):
    pass


def build(phase_hi=99, dbg=(), part="all", sim1=False):
    nc = bass.Bass("TRN2", target_bir_lowering=False, debug=False)
    emit_l0 = part in ("all", "l0")
    emit_l1 = part in ("all", "l1")

    def dram(name, shape, dt):
        kind = "ExternalOutput" if name in dbg else "Internal"
        return nc.dram_tensor(name, shape, dt, kind=kind).ap()

    ein = lambda n, s, d: nc.dram_tensor(n, s, d, kind="ExternalInput").ap()

    if emit_l0:
        d_x = ein("x_in", [T, D], F32)
        d_wqT = ein("wqT", [128, NDC * D], F32)
        d_wkT = ein("wkT", [128, NDC * D], F32)
        d_wvT = ein("wvT", [128, NDC * D], F32)
        d_woT = ein("woT", [128, NDC * D], F32)
        d_wg = ein("w_gate", [128, 16 * 16 * 512], I8)
        d_wu = ein("w_up", [128, 16 * 16 * 512], I8)
        d_wd = ein("w_down", [128, 16 * 64 * 128], I8)
        d_sg = ein("sgate_c", [128, NIC], F32)
        d_su = ein("sup_c", [128, NIC], F32)
        d_sd = ein("sdown_c", [128, NDC], F32)
        d_cb = ein("cb", [128, 128], F32)
    if emit_l1:
        d_wkT1 = ein("wkT1", [128, NDC * D], F32)
        d_wvT1 = ein("wvT1", [128, NDC * D], F32)
        d_wrT1 = ein("wrT1", [128, NDC * D], F32)
        d_wbT = ein("wbT", [128, NDC * D], F32)
        d_td = ein("td_c", [128, NDC], F32)
        d_bb = ein("bb_c", [128, NDC], F32)
        d_rw = ein("rwT", [128, NDC * 8], F32)
        d_Af = ein("AfT", [128, NDC * 128], F32)
        d_Bw = ein("BwF", [128, D], F32)
        d_sel = ein("selm", [8, 128], F32)
    d_flag = ein("flag", [128, 1], F32)
    d_wsel = ein("wsel", [128, 4], F32)
    d_rsel = ein("rsel", [128, 4], F32)

    if part == "all":
        x1_tm = dram("x1_tm", [TOWN, D], F32)
        xn1_tm = dram("xn1_tm", [TOWN + 1, D], BF16)
    elif part == "l0":
        x1_tm = nc.dram_tensor("x1_tm", [TOWN, D], F32, kind="ExternalOutput").ap()
        xn1_tm = nc.dram_tensor("xn1_tm", [TOWN + 1, D], BF16,
                                kind="ExternalOutput").ap()
    else:
        x1_tm = ein("x1_tm", [TOWN, D], F32)
        xn1_tm = ein("xn1_tm", [TOWN + 1, D], BF16)
    if emit_l1:
        out_x2 = nc.dram_tensor("out_x2", [TOWN, D], F32, kind="ExternalOutput").ap()

    if emit_l0:
        xn_tm = dram("xn_tm", [T, D], BF16)
        kT_sc = dram("kT_sc", [D, T], BF16)
        vT_sc = dram("vT_sc", [D, T], BF16)
        qT_sc = dram("qT_sc", [D, TOWN], BF16)
        sT_sc = dram("sT_sc", [D, TOWN], BF16)
        cc1_in = dram("cc1_in", [128, 64], F32)
        cc1_out = dram("cc1_out", [128, 64], F32)
    dT_sc = dram("dT_sc", [D, TOWN], BF16)
    if emit_l1:
        kT1_sc = dram("kT1_sc", [D, TOWN], BF16)
        vT1_sc = dram("vT1_sc", [D, TOWN], BF16)
        rT1_sc = dram("rT1_sc", [D, TOWN], BF16)
        g1T_sc = dram("g1T_sc", [D, TOWN], BF16)
        mk_sc = dram("mk_sc", [TOWN, 128], BF16)
        cc2_in = dram("cc2_in", [128, 128], F32)
        cc2_out = dram("cc2_out", [128, 128], F32)

    ctx = ExitStack()
    with ctx:
        sb = lambda n, s, d: ctx.enter_context(nc.sbuf_tensor(n, s, d))
        psm = lambda n, s, d=F32: ctx.enter_context(nc.psum_tensor(n, s, d))

        class Sem:
            def __init__(self, name):
                self.h = ctx.enter_context(nc.semaphore(name))
                self.n = 0

            def inc(self, instr, k=1):
                instr.then_inc(self.h, k)
                self.n += k
                return self.n

        A = sb("arenaA", [128, NDC * D], BF16)
        Bb = sb("arenaB", [128, NDC * D], BF16)
        SM = sb("smalls", [128, 2048], F32)
        SMB = sb("smallsb", [128, 2708], BF16)
        EV = sb("evac", [128, 2048], BF16)
        R8 = sb("row8", [8, 256], BF16)
        RWB = sb("rwb", [128, NDC * 8], BF16)

        p512 = [psm(f"ps{i}", [128, 512]) for i in range(6)]
        pD = psm("psD", [128, 512])
        pB = psm("psB", [128, 512])

        LSM = Sem("LSM")
        LX = [Sem("LX0"), Sem("LX1")]
        GX = [Sem("GX0"), Sem("GX1")]
        GS = [Sem("GS0"), Sem("GS1")]
        LR = [Sem("LR0"), Sem("LR1")]
        KH = [Sem("KH0"), Sem("KH1")]
        LS = [Sem("LS0"), Sem("LS1")]
        LMK = Sem("LMK")
        L8 = Sem("L8")
        WSM = Sem("WSM")
        WA = Sem("WA")
        WB = Sem("WB")
        WG = [Sem("WG0"), Sem("WG1")]
        WU = [Sem("WU0"), Sem("WU1")]
        SX = [Sem("SX0"), Sem("SX1")]
        SE = [Sem(f"SE{i}") for i in range(4)]
        SD = [Sem("SD0"), Sem("SD1")]
        SG = [Sem("SG0"), Sem("SG1")]
        SMK = Sem("SMK")
        SO = [Sem("SO0"), Sem("SO1")]
        OS = [Sem("OS0"), Sem("OS1")]
        STORE_SEMS = SX + SE + SD + SG + [SMK] + SO
        ALL_DMA = ([LSM, LMK, L8, WSM, WA, WB] + LX + GX + LR + KH + LS + WG + WU +
                   STORE_SEMS + GS + OS)

        def wait_stores(eng):
            for sm in STORE_SEMS + GS:
                if sm.n:
                    eng.wait_ge(sm.h, sm.n)

        TC = Sem("TC")
        VC = Sem("VC")
        SC = Sem("SC")
        GC = Sem("GC")
        CCA = Sem("CCA")
        CCB = Sem("CCB")
        CCD = Sem("CCD")
        VSs = Sem("VSs")
        SSs = Sem("SSs")

        def vedge(v, instr):
            VSs.inc(instr)
            v.wait_ge(VSs.h, VSs.n)

        def sedge(s, instr):
            SSs.inc(instr)
            s.wait_ge(SSs.h, SSs.n)

        block = ctx.enter_context(nc.Block())

        (SUM_C, SQ_C, MEAN_C, NMEAN_C, VAR_C, RSTD_C, NMUR_C, T1_C, T2_C) = range(9)
        FLAG_C, WSEL_C, RSEL_C = 10, 12, 16
        EPS1_C, EPS2_C = 20, 21
        CB_C = 24
        IOTA_C = 160
        KPCOL_C = 676
        MCOL_C = 680
        ECOL_C = 690
        NECOL_C = 1970
        WCOL_C = 710
        BND_C = 730
        CCS_C = 750
        RX_C = 820
        ACC_C = 890
        DEN_C = 910
        STN_C = 1430
        STD_C = 1450
        RN_C = 1470
        RD_C = 1490
        SG_C = 1510
        SU_C = 1580
        SD_C = 1650
        BB_C = 1670
        TD_C = 1690
        CC2S_C = 1710
        RX2_C = 1840
        MK0_C = 0
        ONES_C = 2048
        ONESR_C = 2052
        RB_C = 2192

        @block.sync
        def _(sy):
            i = sy.dma_start(out=SM[:, FLAG_C:FLAG_C + 1], in_=d_flag)
            LSM.inc(i, 16)
            i = sy.dma_start(out=SM[:, WSEL_C:WSEL_C + 4], in_=d_wsel)
            LSM.inc(i, 16)
            i = sy.dma_start(out=SM[:, RSEL_C:RSEL_C + 4], in_=d_rsel)
            LSM.inc(i, 16)
            if emit_l0:
                i = sy.dma_start(out=SM[:, CB_C:CB_C + 128], in_=d_cb)
                LSM.inc(i, 16)
                i = sy.dma_start(out=SM[:, SG_C:SG_C + 64], in_=d_sg)
                LSM.inc(i, 16)
                i = sy.dma_start(out=SM[:, SU_C:SU_C + 64], in_=d_su)
                LSM.inc(i, 16)
                i = sy.dma_start(out=SM[:, SD_C:SD_C + 16], in_=d_sd)
                LSM.inc(i, 16)
            if emit_l1:
                i = sy.dma_start(out=SM[:, TD_C:TD_C + 16], in_=d_td)
                LSM.inc(i, 16)
                i = sy.dma_start(out=SM[:, BB_C:BB_C + 16], in_=d_bb)
                LSM.inc(i, 16)

        @block.gpsimd
        def _(g):
            g.iota(SM[:, IOTA_C:IOTA_C + 512], pattern=[[1, 512]], base=0,
                   channel_multiplier=0, allow_small_or_imprecise_dtypes=True)
            i = g.iota(SM[:, KPCOL_C:KPCOL_C + 1], pattern=[[1, 1]], base=0,
                       channel_multiplier=1, allow_small_or_imprecise_dtypes=True)
            GC.inc(i)
            if emit_l1:
                i = g.dma_start(out=R8[0:8, 0:128], in_=d_sel)
                WSM.inc(i, 16)
                i = g.dma_start(out=RWB[:], in_=d_rw)
                WSM.inc(i, 16)
        p0_gc, p0_ld, p0_wt = GC.n, LSM.n, WSM.n

        @block.vector
        def _(v):
            v.memset(SMB[:, ONES_C:ONES_C + 1], 1.0)
            v.memset(SMB[0:1, ONESR_C:ONESR_C + 128], 1.0)
            v.memset(SM[:, EPS1_C:EPS1_C + 1], 1e-5)
            v.memset(SM[:, EPS2_C:EPS2_C + 1], 1e-8)
            v.wait_ge(GC.h, p0_gc)
            v.wait_ge(LSM.h, p0_ld)
            i = None
            for m in range(4):
                i = v.tensor_scalar_add(SM[:, MCOL_C + m:MCOL_C + m + 1],
                                        SM[:, KPCOL_C:KPCOL_C + 1], float(128 * m))
            vedge(v, i)
            for m in range(4):
                i = v.tensor_single_scalar(
                    SMB[:, MK0_C + m * 512:MK0_C + (m + 1) * 512],
                    SM[:, IOTA_C:IOTA_C + 512],
                    SM[:, MCOL_C + m:MCOL_C + m + 1], ALU.is_ge)
            VC.inc(i)
        p0_vc = VC.n

        def load_weight(dst, src, wsem):
            tc_req = TC.n

            @block.gpsimd
            def _(g):
                if tc_req:
                    g.wait_ge(TC.h, tc_req)
                for gi in range(4):
                    i = g.dma_start(out=dst[:, gi * 4 * D:(gi + 1) * 4 * D],
                                    in_=src[:, gi * 4 * D:(gi + 1) * 4 * D])
                    wsem.inc(i, 16)
            return wsem.n

        def emit_ln(n_tiles, load_src, dst_tm, dst_row0, xf, xn16, dum16,
                    pre_vec=None, store_xrow=None, dma_dec=None):
            LXq = LX if dma_dec is None else GX
            SXq = SX if dma_dec is None else GS
            if dma_dec is None:
                dma_dec = block.sync

            pv = 1 if pre_vec is not None else 0
            sc0, vc0 = SC.n, VC.n
            per_st = 16 * (2 if store_xrow is not None else 1)
            sx0 = [SXq[0].n, SXq[1].n]
            st_at = {tt: sx0[tt % 2] + per_st * (tt // 2 + 1)
                     for tt in range(n_tiles)}
            sc_sums = lambda tt: sc0 + 4 * tt + 1
            sc_rstd = lambda tt: sc0 + 4 * tt + 4
            vc_pre = lambda tt: vc0 + (2 + pv) * tt + 1
            vc_varA = lambda tt: vc0 + (2 + pv) * tt + pv + 1
            vc_xn = lambda tt: vc0 + (2 + pv) * tt + pv + 2
            ld_at = {}

            if load_src is not None:
                @dma_dec
                def _(sy):
                    for tt in range(n_tiles + 1):
                        if tt < n_tiles:
                            if tt >= 2:
                                sy.wait_ge(VC.h, vc_xn(tt - 2))
                            i = sy.dma_start(out=xf[tt % 2][:], in_=load_src(tt))
                            ld_at[tt] = (LXq[tt % 2], LXq[tt % 2].inc(i, 16))
                        if tt >= 1:
                            pt = tt - 1
                            sy.wait_ge(VC.h, vc_xn(pt))
                            if store_xrow is not None:
                                i = sy.dma_start(out=store_xrow(pt),
                                                 in_=xf[pt % 2][:])
                                SXq[pt % 2].inc(i, 16)
                            i = sy.dma_start(
                                out=dst_tm[dst_row0 + pt * 128:
                                           dst_row0 + (pt + 1) * 128, :],
                                in_=xn16[pt % 2][:])
                            SXq[pt % 2].inc(i, 16)

            @block.scalar
            def _(s):
                for tt in range(n_tiles):
                    if load_src is not None:
                        s.wait_ge(ld_at[tt][0].h, ld_at[tt][1])
                    else:
                        s.wait_ge(VC.h, vc_pre(tt))
                    if tt >= 1:
                        s.wait_ge(VC.h, vc_varA(tt - 1))
                    i = s.activation(dum16[0][:], xf[tt % 2][:], AF.Copy,
                                     accum_out=SM[:, SUM_C:SUM_C + 1])
                    sedge(s, i)
                    i = s.activation(dum16[0][:], xf[tt % 2][:], AF.Square,
                                     accum_out=SM[:, SQ_C:SQ_C + 1])
                    SC.inc(i)
                    s.wait_ge(VC.h, vc_varA(tt))
                    i = s.activation(SM[:, T1_C:T1_C + 1], SM[:, VAR_C:VAR_C + 1],
                                     AF.Ln, bias=SM[:, EPS1_C:EPS1_C + 1])
                    sedge(s, i)
                    i = s.activation(SM[:, RSTD_C:RSTD_C + 1],
                                     SM[:, T1_C:T1_C + 1], AF.Exp, scale=-0.5)
                    SC.inc(i, 3)

            @block.vector
            def _(v):
                for tt in range(n_tiles):
                    if pre_vec is not None:
                        if tt >= 2:
                            v.wait_ge(SXq[tt % 2].h, st_at[tt - 2])
                        pre_vec(v, tt)
                    v.wait_ge(SC.h, sc_sums(tt))
                    i = v.tensor_scalar_mul(SM[:, MEAN_C:MEAN_C + 1],
                                            SM[:, SUM_C:SUM_C + 1], 1.0 / D)
                    vedge(v, i)
                    i = v.tensor_scalar_mul(SM[:, NMEAN_C:NMEAN_C + 1],
                                            SM[:, MEAN_C:MEAN_C + 1], -1.0)
                    i = v.tensor_scalar_mul(SM[:, T2_C:T2_C + 1],
                                            SM[:, SQ_C:SQ_C + 1], 1.0 / D)
                    vedge(v, i)
                    i = v.scalar_tensor_tensor(
                        SM[:, VAR_C:VAR_C + 1], SM[:, MEAN_C:MEAN_C + 1],
                        SM[:, NMEAN_C:NMEAN_C + 1], SM[:, T2_C:T2_C + 1],
                        ALU.mult, ALU.add)
                    VC.inc(i)
                    v.wait_ge(SC.h, sc_rstd(tt))
                    i = v.tensor_scalar(SM[:, NMUR_C:NMUR_C + 1],
                                        SM[:, NMEAN_C:NMEAN_C + 1],
                                        SM[:, RSTD_C:RSTD_C + 1], None, ALU.mult)
                    vedge(v, i)
                    if tt >= 2 and pre_vec is None:
                        v.wait_ge(SXq[tt % 2].h, st_at[tt - 2])
                    i = v.tensor_scalar(xn16[tt % 2][:], xf[tt % 2][:],
                                        SM[:, RSTD_C:RSTD_C + 1],
                                        SM[:, NMUR_C:NMUR_C + 1],
                                        ALU.mult, ALU.add)
                    VC.inc(i)

            emit_ln_last_st_at = st_at

            def store_tile(sy, tt):
                sy.wait_ge(VC.h, vc_xn(tt))
                if store_xrow is not None:
                    i = sy.dma_start(out=store_xrow(tt), in_=xf[tt % 2][:])
                    SXq[tt % 2].inc(i, 16)
                i = sy.dma_start(
                    out=dst_tm[dst_row0 + tt * 128:
                               dst_row0 + (tt + 1) * 128, :],
                    in_=xn16[tt % 2][:])
                SXq[tt % 2].inc(i, 16)
            store_tile.st_at = st_at
            store_tile.slots = SXq
            return store_tile

        def emit_proj(warena, wsem, wt_ready, rhs_src, t_blocks, dstf, rbufs,
                      rhs_gate=None):
            tc0, vc0 = TC.n, VC.n
            lr0 = [LR[0].n, LR[1].n]
            ld_at = {tb: lr0[tb % 2] + 256 * (tb // 2 + 1)
                     for tb in range(t_blocks)}
            se0 = [sm.n for sm in SE]

            @block.sync
            def _(sy):
                for tb in range(t_blocks):
                    rb = rbufs[tb % 2]
                    if tb >= 2:
                        sy.wait_ge(TC.h, tc0 + 16 * (tb - 1))
                    elif tc0:
                        sy.wait_ge(TC.h, tc0)
                    if rhs_gate is not None:
                        for (gs, gt) in rhs_gate(tb):
                            sy.wait_ge(gs.h, gt)
                    for dc in range(NDC):
                        i = sy.dma_start(out=rb[:, dc * 512:(dc + 1) * 512],
                                         in_=rhs_src(tb, dc), transpose=True)
                        LR[tb % 2].inc(i, 16)

            @block.tensor
            def _(t):
                t.wait_ge(wsem.h, wt_ready)
                for tb in range(t_blocks):
                    rb = rbufs[tb % 2]
                    t.wait_ge(LR[tb % 2].h, ld_at[tb])
                    for oc in range(NDC):
                        k = tb * 16 + oc
                        pp = p512[k % 4]
                        if k >= 4:
                            t.wait_ge(VC.h, vc0 + k - 3)
                        for dc in range(NDC):
                            lh = warena[:, (dc * NDC + oc) * 128:
                                        (dc * NDC + oc) * 128 + 128]
                            t.ldweights(lh)
                            i = t.matmul(pp[:], lh, rb[:, dc * 512:(dc + 1) * 512],
                                         start=(dc == 0), stop=(dc == NDC - 1))
                        TC.inc(i)

            @block.vector
            def _(v):
                for k in range(t_blocks * 16):
                    v.wait_ge(TC.h, tc0 + k + 1)
                    tgt = se0[k % 4] + 16 * (k // 4)
                    if tgt:
                        v.wait_ge(SE[k % 4].h, tgt)
                    i = v.tensor_copy(EV[:, (k % 4) * 512:(k % 4 + 1) * 512],
                                      p512[k % 4][:])
                    VC.inc(i)

            @block.gpsimd
            def _(g):
                for tb in range(t_blocks):
                    for oc in range(NDC):
                        k = tb * 16 + oc
                        g.wait_ge(VC.h, vc0 + k + 1)
                        i = g.dma_start(out=dstf(oc, tb),
                                        in_=EV[:, (k % 4) * 512:(k % 4 + 1) * 512])
                        SE[k % 4].inc(i, 16)

        wq_rdy = load_weight(A, d_wqT, WA) if (emit_l0 and phase_hi >= 2) else 0
        if emit_l0 and phase_hi >= 1:
            with (
                nc.sbuf_tensor("xfa", [128, D], F32) as xfa,
                nc.sbuf_tensor("xfb", [128, D], F32) as xfb,
                nc.sbuf_tensor("xna", [128, D], BF16) as xna,
                nc.sbuf_tensor("xnb", [128, D], BF16) as xnb,
                nc.sbuf_tensor("dum", [128, D], BF16) as dum,
                nc.sbuf_tensor("rhsC", [128, NDC * 512], BF16) as C,
                nc.sbuf_tensor("rhsD", [128, NDC * 512], BF16) as Dd,
            ):
                p1_store = emit_ln(
                    T // 128, lambda tt: d_x[tt * 128:(tt + 1) * 128, :],
                    xn_tm, 0, [xfa, xfb], [xna, xnb], (dum, dum),
                    dma_dec=block.gpsimd)
                p1_st_at = p1_store.st_at

                rhs_src = lambda tb, dc: xn_tm[tb * 512:(tb + 1) * 512,
                                               dc * 128:(dc + 1) * 128]

                p1_slots = p1_store.slots

                def p1_gate(tb):
                    tt = tb * 4 + 3
                    return [(p1_slots[tt % 2], p1_st_at[tt]),
                            (p1_slots[(tt - 1) % 2], p1_st_at[tt - 1])]

                if phase_hi >= 2:
                    emit_proj(A, WA, wq_rdy, rhs_src, 4,
                              lambda oc, tb: qT_sc[oc * 128:(oc + 1) * 128,
                                                   tb * 512:(tb + 1) * 512],
                              (C, Dd), rhs_gate=p1_gate)
                    wk_rdy = load_weight(Bb, d_wkT, WB)
                    emit_proj(Bb, WB, wk_rdy, rhs_src, 8,
                              lambda oc, tb: kT_sc[oc * 128:(oc + 1) * 128,
                                                   tb * 512:(tb + 1) * 512],
                              (C, Dd), rhs_gate=p1_gate)
                    wv_rdy = load_weight(A, d_wvT, WA)
                    emit_proj(A, WA, wv_rdy, rhs_src, 8,
                              lambda oc, tb: vT_sc[oc * 128:(oc + 1) * 128,
                                                   tb * 512:(tb + 1) * 512],
                              (C, Dd), rhs_gate=p1_gate)

        if emit_l0 and phase_hi >= 3:
            wo_rdy = load_weight(Bb, d_woT, WB)
            with (
                nc.sbuf_tensor("kth0", [128, T], BF16) as kth0,
                nc.sbuf_tensor("kth1", [128, T], BF16) as kth1,
                nc.sbuf_tensor("vh0", [128, T], BF16) as vh0,
                nc.sbuf_tensor("vh1", [128, T], BF16) as vh1,
                nc.sbuf_tensor("qh0", [128, TOWN], BF16) as qh0,
                nc.sbuf_tensor("qh1", [128, TOWN], BF16) as qh1,
                nc.sbuf_tensor("ptil", [128, 4 * 512], BF16) as ptil,
                nc.sbuf_tensor("bb16", [128, 512], F32) as bb16t,
                nc.sbuf_tensor("rden", [1, 512], F32) as rden,
            ):
                kth, vhh, qhh = [kth0, kth1], [vh0, vh1], [qh0, qh1]
                NCH = 32
                tc0, sc0, vc0 = TC.n, SC.n, VC.n
                kh0 = [KH[0].n, KH[1].n]
                ld_head = {h: kh0[h % 2] + 16 * 34 * (h // 2 + 1)
                           for h in range(NH)}

                def seq_totals(hq):
                    base = tc0 + hq * 65
                    qk, pv = {}, {}
                    k = 0
                    for j in range(NCH):
                        k += 1
                        qk[j] = base + k
                        if j >= 1:
                            k += 1
                            pv[j - 1] = base + k
                    k += 1
                    pv[NCH - 1] = base + k
                    k += 1
                    bc = base + k
                    return qk, pv, bc

                sc_exp = lambda hq, j: sc0 + hq * 34 + j + 1
                sc_recip = lambda hq: sc0 + hq * 34 + 33
                sc_rb = lambda hq: sc0 + hq * 34 + 34
                vc_mask = lambda hq, m: vc0 + hq * 7 + m + 1
                vc_den = lambda hq: vc0 + hq * 7 + 5
                vc_bb = lambda hq: vc0 + hq * 7 + 6
                vc_ao = lambda hq: vc0 + hq * 7 + 7

                @block.sync
                def _(sy):
                    wait_stores(sy)
                    for h in range(NH):
                        if h >= 2:
                            sy.wait_ge(TC.h, tc0 + (h - 1) * 260)
                        i = sy.dma_start(out=kth[h % 2][:],
                                         in_=kT_sc[h * 128:(h + 1) * 128, :])
                        KH[h % 2].inc(i, 16)
                        for j in range(NCH):
                            i = sy.dma_start(
                                out=vhh[h % 2][:, j * 128:(j + 1) * 128],
                                in_=vT_sc[h * 128:(h + 1) * 128,
                                          j * 128:(j + 1) * 128],
                                transpose=True)
                            KH[h % 2].inc(i, 16)
                        i = sy.dma_start(out=qhh[h % 2][:],
                                         in_=qT_sc[h * 128:(h + 1) * 128, :])
                        KH[h % 2].inc(i, 16)

                def emit_pv(t, h, qi, j, hq, qk_t, pv_t):
                    m = j - 4 * qi
                    if 0 <= m <= 3:
                        t.wait_ge(VC.h, vc_mask(hq, m))
                    else:
                        t.wait_ge(SC.h, sc_exp(hq, j))
                    if j == 0:
                        if hq >= 1:
                            t.wait_ge(VC.h, vc_den(hq - 1))
                        if hq >= 2:
                            t.wait_ge(VC.h, vc_ao(hq - 2))
                    pt = ptil[:, (j % 4) * 512:(j % 4 + 1) * 512]
                    po = p512[2 + (qi % 2)]
                    lhv = vhh[h % 2][:, j * 128:(j + 1) * 128]
                    t.ldweights(lhv)
                    t.matmul(po[:], lhv, pt, start=(j == 0), stop=(j == NCH - 1))
                    t.ldweights(SMB[:, ONES_C:ONES_C + 1])
                    i = t.matmul(pD[0:1, :], SMB[:, ONES_C:ONES_C + 1], pt,
                                 start=(j == 0), stop=(j == NCH - 1))
                    TC.inc(i)

                @block.tensor
                def _(t):
                    for h in range(NH):
                        t.wait_ge(KH[h % 2].h, ld_head[h])
                        for qi in range(4):
                            hq = h * 4 + qi
                            qk_t, pv_t, bc_t = seq_totals(hq)
                            for j in range(NCH):
                                sp = p512[j % 2]
                                if j >= 2:
                                    t.wait_ge(SC.h, sc_exp(hq, j - 2))
                                elif hq > 0:
                                    t.wait_ge(SC.h, sc_exp(hq - 1, 30 + j))
                                lhk = kth[h % 2][:, j * 128:(j + 1) * 128]
                                t.ldweights(lhk)
                                i = t.matmul(sp[:], lhk,
                                             qhh[h % 2][:, qi * 512:(qi + 1) * 512],
                                             start=True, stop=True)
                                TC.inc(i)
                                if j >= 1:
                                    emit_pv(t, h, qi, j - 1, hq, qk_t, pv_t)
                            emit_pv(t, h, qi, NCH - 1, hq, qk_t, pv_t)
                            t.wait_ge(SC.h, sc_rb(hq))
                            if hq >= 1:
                                t.wait_ge(VC.h, vc_bb(hq - 1))
                            t.ldweights(SMB[0:1, ONESR_C:ONESR_C + 128])
                            i = t.matmul(pB[:], SMB[0:1, ONESR_C:ONESR_C + 128],
                                         SMB[0:1, RB_C:RB_C + 512],
                                         start=True, stop=True)
                            TC.inc(i)
                            assert TC.n == bc_t

                @block.scalar
                def _(s):
                    for h in range(NH):
                        for qi in range(4):
                            hq = h * 4 + qi
                            qk_t, pv_t, bc_t = seq_totals(hq)
                            pqk, ppv, _ = seq_totals(hq - 1) if hq else (None, None, None)
                            for j in range(NCH):
                                s.wait_ge(TC.h, qk_t[j])
                                if j >= 4:
                                    s.wait_ge(TC.h, pv_t[j - 4])
                                elif hq > 0:
                                    s.wait_ge(TC.h, ppv[28 + j])
                                i = s.activation(
                                    ptil[:, (j % 4) * 512:(j % 4 + 1) * 512],
                                    p512[j % 2][:], AF.Exp,
                                    bias=SM[:, CB_C + qi * 32 + j:
                                            CB_C + qi * 32 + j + 1],
                                    scale=RSQ)
                                SC.inc(i)
                            s.wait_ge(VC.h, vc_den(hq))
                            i = s.activation(rden[0:1, :],
                                             SM[0:1, DEN_C:DEN_C + 512], AF.Ln)
                            sedge(s, i)
                            i = s.activation(rden[0:1, :], rden[0:1, :],
                                             AF.Exp, scale=-1.0)
                            SC.inc(i)
                            s.wait_ge(SC.h, SC.n)
                            i = s.copy(SMB[0:1, RB_C:RB_C + 512], rden[0:1, :])
                            SC.inc(i)

                @block.vector
                def _(v):
                    for h in range(NH):
                        for qi in range(4):
                            hq = h * 4 + qi
                            qk_t, pv_t, bc_t = seq_totals(hq)
                            for m in range(4):
                                j = 4 * qi + m
                                v.wait_ge(SC.h, sc_exp(hq, j))
                                pt = ptil[:, (j % 4) * 512:(j % 4 + 1) * 512]
                                i = v.tensor_mul(pt, pt,
                                                 SMB[:, MK0_C + m * 512:
                                                     MK0_C + (m + 1) * 512])
                                VC.inc(i)
                            v.wait_ge(TC.h, pv_t[NCH - 1])
                            i = v.tensor_copy(SM[0:1, DEN_C:DEN_C + 512],
                                              pD[0:1, :])
                            VC.inc(i)
                            v.wait_ge(TC.h, bc_t)
                            i = v.tensor_copy(bb16t[:], pB[:])
                            VC.inc(i)
                            v.wait_ge(VC.h, VC.n)
                            i = v.tensor_tensor(
                                A[:, h * TOWN + qi * 512:h * TOWN + (qi + 1) * 512],
                                p512[2 + (qi % 2)][:], bb16t[:], ALU.mult)
                            VC.inc(i)
        p3_vc = VC.n
        p3_wt = WB.n

        if emit_l0 and phase_hi >= 4:
            tc0, vc0 = TC.n, VC.n
            se0 = [sm.n for sm in SE]

            @block.tensor
            def _(t):
                t.wait_ge(WB.h, p3_wt)
                t.wait_ge(VC.h, p3_vc)
                for tb in range(4):
                    for oc in range(NDC):
                        k = tb * 16 + oc
                        pp = p512[k % 4]
                        if k >= 4:
                            t.wait_ge(VC.h, vc0 + k - 3)
                        for dc in range(NDC):
                            lh = Bb[:, (dc * NDC + oc) * 128:
                                    (dc * NDC + oc) * 128 + 128]
                            t.ldweights(lh)
                            i = t.matmul(pp[:], lh,
                                         A[:, dc * TOWN + tb * 512:
                                           dc * TOWN + (tb + 1) * 512],
                                         start=(dc == 0), stop=(dc == NDC - 1))
                        TC.inc(i)

            @block.vector
            def _(v):
                for k in range(64):
                    v.wait_ge(TC.h, tc0 + k + 1)
                    tgt = se0[k % 4] + 16 * (k // 4)
                    if tgt:
                        v.wait_ge(SE[k % 4].h, tgt)
                    i = v.tensor_copy(EV[:, (k % 4) * 512:(k % 4 + 1) * 512],
                                      p512[k % 4][:])
                    VC.inc(i)

            @block.gpsimd
            def _(g):
                for tb in range(4):
                    for oc in range(NDC):
                        k = tb * 16 + oc
                        g.wait_ge(VC.h, vc0 + k + 1)
                        i = g.dma_start(out=sT_sc[oc * 128:(oc + 1) * 128,
                                                  tb * 512:(tb + 1) * 512],
                                        in_=EV[:, (k % 4) * 512:(k % 4 + 1) * 512])
                        SE[k % 4].inc(i, 16)

        if emit_l0 and phase_hi >= 5:
            wg = [Bb[:, 0:8192], Bb[:, 8192:16384]]
            wu = [Bb[:, 16384:24576], Bb[:, 24576:32768]]
            wdl = [Bb[:, 0:8192], Bb[:, 8192:16384]]
            _c56 = ExitStack()
            with _c56:
                C = _c56.enter_context(nc.sbuf_tensor("rhsC5", [128, NDC * 512], BF16))
                gt = _c56.enter_context(nc.sbuf_tensor("gt5", [128, 2 * 512], BF16))
                ut = _c56.enter_context(nc.sbuf_tensor("ut5", [128, 2 * 512], BF16))
                stl = _c56.enter_context(nc.sbuf_tensor("stl5", [128, 2 * 512], BF16))
                dtl = _c56.enter_context(nc.sbuf_tensor("dtl5", [128, 2 * 512], BF16))
                p6_sd_at = []
                for tq in range(4):
                    tc0, sc0, vc0 = TC.n, SC.n, VC.n
                    lr00 = LR[0].n
                    ld_rhs = lr00 + 16 * NDC
                    ls0 = [LS[0].n, LS[1].n]
                    ld_st = lambda oc: ls0[oc % 2] + 16 * (oc // 2 + 1)
                    wg0 = [WG[0].n, WG[1].n]
                    wu0 = [WU[0].n, WU[1].n]
                    wt_ib_g = lambda ib: wg0[ib % 2] + 16 * (ib // 2 + 1)
                    wt_ib_u = lambda ib: wu0[ib % 2] + 16 * (ib // 2 + 1)
                    wt_oc = lambda oc: wg0[oc % 2] + 16 * (8 + oc // 2 + 1)
                    tc_gu = lambda ic, gu: tc0 + ic * 2 + gu + 1
                    tc_oc = lambda oc: tc0 + 128 + oc + 1
                    sc_gu = lambda ic, gu: sc0 + ic * 2 + gu + 1
                    sc_oc = lambda oc: sc0 + 128 + oc + 1
                    vc_f = lambda ic: vc0 + ic + 1
                    vc_oc = lambda oc: vc0 + 64 + oc + 1
                    sd0 = [SD[0].n, SD[1].n]
                    st_oc = lambda oc: sd0[oc % 2] + 16 * (oc // 2 + 1)

                    @block.sync
                    def _(sy, tq=tq, tc0=tc0, vc0=vc0):
                        if tq == 0:
                            wait_stores(sy)
                        else:
                            sy.wait_ge(TC.h, tc0)
                        for dc in range(NDC):
                            i = sy.dma_start(
                                out=C[:, dc * 512:(dc + 1) * 512],
                                in_=xn_tm[tq * 512:(tq + 1) * 512,
                                          dc * 128:(dc + 1) * 128],
                                transpose=True)
                            LR[0].inc(i, 16)
                        for oc in range(NDC):
                            if oc >= 2:
                                sy.wait_ge(VC.h, vc0 + 64 + oc - 1)
                            i = sy.dma_start(
                                out=stl[:, (oc % 2) * 512:(oc % 2) * 512 + 512],
                                in_=sT_sc[oc * 128:(oc + 1) * 128,
                                          tq * 512:(tq + 1) * 512])
                            LS[oc % 2].inc(i, 16)
                            if oc >= 2:
                                po = oc - 2
                                i = sy.dma_start(
                                    out=dT_sc[po * 128:(po + 1) * 128,
                                              tq * 512:(tq + 1) * 512],
                                    in_=dtl[:, (po % 2) * 512:(po % 2) * 512 + 512])
                                SD[po % 2].inc(i, 16)
                        for po in (14, 15):
                            sy.wait_ge(VC.h, vc0 + 64 + po + 1)
                            i = sy.dma_start(
                                out=dT_sc[po * 128:(po + 1) * 128,
                                          tq * 512:(tq + 1) * 512],
                                in_=dtl[:, (po % 2) * 512:(po % 2) * 512 + 512])
                            SD[po % 2].inc(i, 16)
                    p6_sd_at.append((SD[0].n, SD[1].n))

                    @block.gpsimd
                    def _(g, tq=tq, tc0=tc0):
                        for ib in range(16):
                            if ib >= 2:
                                g.wait_ge(TC.h, tc0 + (ib - 1) * 8)
                            elif tc0:
                                g.wait_ge(TC.h, tc0)
                            i = g.dma_start(out=wg[ib % 2],
                                            in_=d_wg[:, ib * 8192:(ib + 1) * 8192])
                            WG[ib % 2].inc(i, 16)
                            i = g.dma_start(out=wu[ib % 2],
                                            in_=d_wu[:, ib * 8192:(ib + 1) * 8192])
                            WU[ib % 2].inc(i, 16)
                        for oc in range(16):
                            if oc < 2:
                                g.wait_ge(TC.h, tc0 + 128)
                            else:
                                g.wait_ge(TC.h, tc0 + 128 + oc - 1)
                            i = g.dma_start(out=wdl[oc % 2],
                                            in_=d_wd[:, oc * 8192:(oc + 1) * 8192])
                            WG[oc % 2].inc(i, 16)

                    @block.tensor
                    def _(t, tq=tq, ld_rhs=ld_rhs, wt_ib_g=wt_ib_g,
                          wt_ib_u=wt_ib_u, wt_oc=wt_oc,
                          sc_gu=sc_gu, sc_oc=sc_oc, vc_f=vc_f):
                        t.wait_ge(LR[0].h, ld_rhs)
                        for ib in range(16):
                            t.wait_ge(WG[ib % 2].h, wt_ib_g(ib))
                            t.wait_ge(WU[ib % 2].h, wt_ib_u(ib))
                            for icc in range(4):
                                ic = ib * 4 + icc
                                pg = p512[ic % 2]
                                pu = p512[2 + ic % 2]
                                if ic >= 2:
                                    t.wait_ge(SC.h, sc_gu(ic - 2, 1))
                                for dc in range(NDC):
                                    lh = wg[ib % 2][:, dc * 512 + icc * 128:
                                                    dc * 512 + icc * 128 + 128]
                                    t.ldweights(lh)
                                    i = t.matmul(pg[:], lh,
                                                 C[:, dc * 512:(dc + 1) * 512],
                                                 start=(dc == 0),
                                                 stop=(dc == NDC - 1))
                                TC.inc(i)
                                for dc in range(NDC):
                                    lh = wu[ib % 2][:, dc * 512 + icc * 128:
                                                    dc * 512 + icc * 128 + 128]
                                    t.ldweights(lh)
                                    i = t.matmul(pu[:], lh,
                                                 C[:, dc * 512:(dc + 1) * 512],
                                                 start=(dc == 0),
                                                 stop=(dc == NDC - 1))
                                TC.inc(i)
                        for oc in range(16):
                            t.wait_ge(WG[oc % 2].h, wt_oc(oc))
                            if oc == 0:
                                t.wait_ge(VC.h, vc_f(63))
                            py = p512[4 + oc % 2]
                            if oc >= 2:
                                t.wait_ge(SC.h, sc_oc(oc - 2))
                            for ic in range(NIC):
                                lh = wdl[oc % 2][:, ic * 128:(ic + 1) * 128]
                                t.ldweights(lh)
                                i = t.matmul(py[:], lh, A[:, ic * 512:(ic + 1) * 512],
                                             start=(ic == 0), stop=(ic == NIC - 1))
                            TC.inc(i)

                    @block.scalar
                    def _(s, tq=tq, tc_gu=tc_gu, tc_oc=tc_oc, vc_f=vc_f,
                          vc_oc=vc_oc):
                        for ic in range(NIC):
                            s.wait_ge(TC.h, tc_gu(ic, 0))
                            s.wait_ge(VC.h, vc_f(ic - 2) if ic >= 2 else vc0)
                            i = s.activation(gt[:, (ic % 2) * 512:(ic % 2) * 512 + 512],
                                             p512[ic % 2][:], AF.Silu,
                                             scale=SM[:, SG_C + ic:SG_C + ic + 1])
                            SC.inc(i)
                            s.wait_ge(TC.h, tc_gu(ic, 1))
                            i = s.activation(ut[:, (ic % 2) * 512:(ic % 2) * 512 + 512],
                                             p512[2 + ic % 2][:], AF.Copy,
                                             scale=SM[:, SU_C + ic:SU_C + ic + 1])
                            SC.inc(i)
                        for oc in range(16):
                            s.wait_ge(TC.h, tc_oc(oc))
                            if oc >= 2:
                                s.wait_ge(VC.h, vc_oc(oc - 2))
                            i = s.activation(gt[:, (oc % 2) * 512:(oc % 2) * 512 + 512],
                                             p512[4 + oc % 2][:], AF.Silu,
                                             scale=SM[:, SD_C + oc:SD_C + oc + 1])
                            SC.inc(i)

                    @block.vector
                    def _(v, tq=tq, sc_gu=sc_gu, sc_oc=sc_oc, ld_st=ld_st,
                          st_oc=st_oc):
                        for ic in range(NIC):
                            v.wait_ge(SC.h, sc_gu(ic, 1))
                            i = v.tensor_mul(A[:, ic * 512:(ic + 1) * 512],
                                             gt[:, (ic % 2) * 512:(ic % 2) * 512 + 512],
                                             ut[:, (ic % 2) * 512:(ic % 2) * 512 + 512])
                            VC.inc(i)
                        for oc in range(16):
                            v.wait_ge(SC.h, sc_oc(oc))
                            v.wait_ge(LS[oc % 2].h, ld_st(oc))
                            tgt = st_oc(oc - 2) if oc >= 2 else sd0[oc % 2]
                            if tgt:
                                v.wait_ge(SD[oc % 2].h, tgt)
                            i = v.tensor_mul(dtl[:, (oc % 2) * 512:(oc % 2) * 512 + 512],
                                             gt[:, (oc % 2) * 512:(oc % 2) * 512 + 512],
                                             stl[:, (oc % 2) * 512:(oc % 2) * 512 + 512])
                            VC.inc(i)


                if phase_hi >= 6:
                    xf6a = _c56.enter_context(nc.sbuf_tensor("xf6a", [128, D], F32))
                    xf6b = _c56.enter_context(nc.sbuf_tensor("xf6b", [128, D], F32))
                    dl6a = _c56.enter_context(nc.sbuf_tensor("dl6a", [128, D], BF16))
                    dl6b = _c56.enter_context(nc.sbuf_tensor("dl6b", [128, D], BF16))
                    xn6a = _c56.enter_context(nc.sbuf_tensor("xn6a", [128, D], BF16))
                    xn6b = _c56.enter_context(nc.sbuf_tensor("xn6b", [128, D], BF16))
                    dum6 = _c56.enter_context(nc.sbuf_tensor("dum6", [128, D], BF16))
                    bnd = _c56.enter_context(nc.sbuf_tensor("bnd", [128, 16], BF16))
                    xf6, dl6 = [xf6a, xf6b], [dl6a, dl6b]
                    vc0 = VC.n
                    lx0 = [LX[0].n, LX[1].n]
                    ld_at6 = {tt: lx0[tt % 2] + 80 * (tt // 2 + 1)
                              for tt in range(16)}
                    vc_pre6 = lambda tt: vc0 + 3 * tt + 1

                    def pre_vec(v, tt):
                        v.wait_ge(LX[tt % 2].h, ld_at6[tt])
                        i = v.tensor_tensor(xf6[tt % 2][:], xf6[tt % 2][:],
                                            dl6[tt % 2][:], ALU.add)
                        VC.inc(i)

                    store_tile = emit_ln(
                        16, None, xn1_tm, 1, xf6, [xn6a, xn6b], (dum6, dum6),
                        pre_vec=pre_vec,
                        store_xrow=lambda tt: x1_tm[tt * 128:(tt + 1) * 128, :])

                    @block.sync
                    def _(sy):
                        for tt in range(17):
                            if tt < 16:
                                tq = tt // 4
                                sy.wait_ge(SD[0].h, p6_sd_at[tq][0])
                                sy.wait_ge(SD[1].h, p6_sd_at[tq][1])
                                if tt >= 2:
                                    sy.wait_ge(VC.h, vc_pre6(tt - 2))
                                    sy.wait_ge(store_tile.slots[tt % 2].h,
                                               store_tile.st_at[tt - 2])
                                i = sy.dma_start(out=xf6[tt % 2][:],
                                                 in_=d_x[tt * 128:(tt + 1) * 128, :])
                                LX[tt % 2].inc(i, 16)
                                for gq in range(4):
                                    i = sy.dma_start(
                                        out=dl6[tt % 2][:, gq * 512:(gq + 1) * 512],
                                        in_=dT_sc[gq * 512:(gq + 1) * 512,
                                                  tt * 128:(tt + 1) * 128],
                                        transpose=True)
                                    LX[tt % 2].inc(i, 16)
                            if tt >= 1:
                                store_tile(sy, tt - 1)

                    @block.sync
                    def _(sy):
                        wait_stores(sy)
                        i = sy.dma_start(
                            out=bnd[:],
                            in_=xn1_tm[TOWN:TOWN + 1, :].rearrange(
                                "one (p c) -> (one p) c", p=128))
                        LSM.inc(i, 16)
                    ld_bnd = LSM.n
                    vc0b = VC.n

                    @block.vector
                    def _(v):
                        v.wait_ge(LSM.h, ld_bnd)
                        i = v.tensor_copy(SM[:, BND_C:BND_C + 16], bnd[:])
                        vedge(v, i)
                        for bp in range(4):
                            i = v.tensor_scalar(
                                SM[:, CCS_C + bp * 16:CCS_C + (bp + 1) * 16],
                                SM[:, BND_C:BND_C + 16],
                                SM[:, WSEL_C + bp:WSEL_C + bp + 1], None, ALU.mult)
                        VC.inc(i)
                    vc_ccs = VC.n

                    @block.gpsimd
                    def _(g):
                        g.wait_ge(VC.h, vc_ccs)
                        i = g.dma_start(out=cc1_in, in_=SM[:, CCS_C:CCS_C + 64])
                        CCD.inc(i, 16)
                        g.wait_ge(CCD.h, CCD.n)
                        i = g.collective_compute(
                            "AllReduce", ALU.add,
                            replica_groups=[[0]] if sim1 else [list(range(8))],
                            ins=[cc1_in.opt()], outs=[cc1_out.opt()])
                        CCA.inc(i, 1)
                        g.wait_ge(CCA.h, 1)
                        i = g.dma_start(out=SM[:, RX_C:RX_C + 64], in_=cc1_out)
                        CCD.inc(i, 16)
                    gc_rx1 = CCD.n

                    @block.vector
                    def _(v):
                        v.wait_ge(CCD.h, gc_rx1)
                        i = v.tensor_scalar(SM[:, ACC_C:ACC_C + 16],
                                            SM[:, RX_C:RX_C + 16],
                                            SM[:, RSEL_C:RSEL_C + 1], None, ALU.mult)
                        vedge(v, i)
                        for bp in range(1, 4):
                            i = v.scalar_tensor_tensor(
                                SM[:, ACC_C:ACC_C + 16],
                                SM[:, RX_C + bp * 16:RX_C + (bp + 1) * 16],
                                SM[:, RSEL_C + bp:RSEL_C + bp + 1],
                                SM[:, ACC_C:ACC_C + 16], ALU.mult, ALU.add)
                            vedge(v, i)
                        i = v.tensor_scalar(bnd[:], SM[:, ACC_C:ACC_C + 16],
                                            SM[:, FLAG_C:FLAG_C + 1], None, ALU.mult)
                        VC.inc(i)
                    vc_bnd = VC.n

                    @block.sync
                    def _(sy):
                        sy.wait_ge(VC.h, vc_bnd)
                        i = sy.dma_start(
                            out=xn1_tm[0:1, :].rearrange("one (p c) -> (one p) c",
                                                         p=128),
                            in_=bnd[:])
                        SX[0].inc(i, 16)
        p6_bnd_st = SX[0].n if (emit_l0 and phase_hi >= 6) else 0

        if emit_l1 and phase_hi >= 7:
            with (
                nc.sbuf_tensor("rhsC7", [128, NDC * 512], BF16) as C,
                nc.sbuf_tensor("rhsD7", [128, NDC * 512], BF16) as Dd,
                nc.sbuf_tensor("abuf", [128, NDC * 128], BF16) as ABUF,
                nc.sbuf_tensor("bwb", [128, D], BF16) as BWB,
                nc.sbuf_tensor("hT", [128, TOWN], BF16) as hT,
                nc.sbuf_tensor("lgt", [128, 8], F32) as lgt,
                nc.sbuf_tensor("mxc", [128, 1], F32) as mxc,
                nc.sbuf_tensor("mk8", [128, 8], BF16) as mk8,
                nc.sbuf_tensor("mkT", [128, 512], BF16) as mkT,
                nc.sbuf_tensor("g1t", [128, 2 * 512], BF16) as g1t,
            ):
                @block.gpsimd
                def _(g):
                    tcn = TC.n
                    if tcn:
                        g.wait_ge(TC.h, tcn)
                    i = g.dma_start(out=ABUF[:], in_=d_Af)
                    WSM.inc(i, 16)
                    i = g.dma_start(out=BWB[:], in_=d_Bw)
                    WSM.inc(i, 16)
                wsm_rdy = WSM.n

                rhs_shift = lambda tb, dc: xn1_tm[tb * 512:(tb + 1) * 512,
                                                  dc * 128:(dc + 1) * 128]
                rhs_plain = lambda tb, dc: xn1_tm[1 + tb * 512:1 + (tb + 1) * 512,
                                                  dc * 128:(dc + 1) * 128]

                p6_gate = lambda tb: [(SX[0], p6_bnd_st)]
                wk1_rdy = load_weight(A, d_wkT1, WA)
                emit_proj(A, WA, wk1_rdy, rhs_shift, 4,
                          lambda oc, tb: kT1_sc[oc * 128:(oc + 1) * 128,
                                                tb * 512:(tb + 1) * 512], (C, Dd),
                          rhs_gate=p6_gate)
                wv1_rdy = load_weight(Bb, d_wvT1, WB)
                emit_proj(Bb, WB, wv1_rdy, rhs_shift, 4,
                          lambda oc, tb: vT1_sc[oc * 128:(oc + 1) * 128,
                                                tb * 512:(tb + 1) * 512], (C, Dd),
                          rhs_gate=p6_gate)
                wr1_rdy = load_weight(A, d_wrT1, WA)
                emit_proj(A, WA, wr1_rdy, rhs_plain, 4,
                          lambda oc, tb: rT1_sc[oc * 128:(oc + 1) * 128,
                                                tb * 512:(tb + 1) * 512], (C, Dd))
                wb_rdy = load_weight(Bb, d_wbT, WB)

                @block.vector
                def _(v):
                    for sm in SE:
                        if sm.n:
                            v.wait_ge(sm.h, sm.n)
                    i = v.memset(EV[:], 0.0)
                    VC.inc(i)
                vc_z = VC.n

                @block.gpsimd
                def _(g):
                    g.wait_ge(VC.h, vc_z)
                    i = g.dma_start(
                        out=mk_sc.rearrange("(a p) c -> p a c", p=128),
                        in_=EV.rearrange("p (a c) -> p a c", c=128))
                    SMK.inc(i, 16)

                prev_vh = None
                prev_vhm = None
                prev_tse = None
                for tb in range(4):
                    tc0, vc0, sc0 = TC.n, VC.n, SC.n
                    rb = C if tb % 2 == 0 else Dd
                    t_h = tc0 + 1
                    t_lg = lambda tc4: tc0 + 2 + tc4
                    t_se = tc0 + 6
                    t_base = lambda oc: tc0 + 6 + oc + 1
                    v_h = vc0 + 1
                    v_mk = lambda tc4: vc0 + 2 + tc4
                    v_hm = vc0 + 6
                    smk0 = SMK.n
                    st_mk = lambda tc4: smk0 + 16 * (tc4 + 1)
                    sg0 = [SG[0].n, SG[1].n]
                    st_g1 = lambda oc: sg0[oc % 2] + 16 * (oc // 2 + 1)
                    sc_g1 = lambda oc: sc0 + oc + 1

                    @block.sync
                    def _(sy, tb=tb, rb=rb, tc0=tc0):
                        sy.wait_ge(TC.h, tc0)
                        for dc in range(NDC):
                            i = sy.dma_start(out=rb[:, dc * 512:(dc + 1) * 512],
                                             in_=rhs_plain(tb, dc), transpose=True)
                            LR[tb % 2].inc(i, 16)
                    ld_rhs = LR[tb % 2].n

                    @block.sync
                    def _(sy, tb=tb, st_mk=st_mk, prev_tse=prev_tse):
                        sy.wait_ge(SMK.h, st_mk(3))
                        if prev_tse is not None:
                            sy.wait_ge(TC.h, prev_tse)
                        i = sy.dma_start(out=mkT[:],
                                         in_=mk_sc[tb * 512:(tb + 1) * 512, :],
                                         transpose=True)
                        LMK.inc(i, 16)
                    ld_mk = LMK.n

                    @block.tensor
                    def _(t, tb=tb, rb=rb, ld_rhs=ld_rhs, ld_mk=ld_mk, tc0=tc0,
                          v_mk=v_mk, v_hm=v_hm, sc_g1=sc_g1,
                          prev_vh=prev_vh, prev_vhm=prev_vhm):
                        t.wait_ge(LR[tb % 2].h, ld_rhs)
                        t.wait_ge(WB.h, wb_rdy)
                        t.wait_ge(WSM.h, wsm_rdy)
                        if prev_vh is not None:
                            t.wait_ge(VC.h, prev_vh)
                        for dc in range(NDC):
                            lh = ABUF[:, dc * 128:(dc + 1) * 128]
                            t.ldweights(lh)
                            i = t.matmul(p512[0][:], lh,
                                         rb[:, dc * 512:(dc + 1) * 512],
                                         start=(dc == 0), stop=(dc == NDC - 1))
                        TC.inc(i)
                        for tc4 in range(4):
                            if tc4 >= 1:
                                t.wait_ge(VC.h, v_mk(tc4 - 1))
                            for dc in range(NDC):
                                lh = rb[:, dc * 512 + tc4 * 128:
                                        dc * 512 + tc4 * 128 + 128]
                                t.ldweights(lh)
                                i = t.matmul(p512[1][0:128, 0:8], lh,
                                             RWB[:, dc * 8:(dc + 1) * 8],
                                             start=(dc == 0), stop=(dc == NDC - 1))
                            TC.inc(i)
                        t.wait_ge(LMK.h, ld_mk)
                        if prev_vhm is not None:
                            t.wait_ge(VC.h, prev_vhm)
                        t.ldweights(R8[0:8, 0:128])
                        i = t.matmul(p512[2][:], R8[0:8, 0:128], mkT[0:8, :],
                                     start=True, stop=True)
                        TC.inc(i)
                        t.wait_ge(VC.h, v_hm)
                        for oc in range(NDC):
                            po = p512[3 + oc % 2]
                            if oc >= 2:
                                t.wait_ge(SC.h, sc_g1(oc - 2))
                            for dc in range(NDC):
                                lh = Bb[:, (dc * NDC + oc) * 128:
                                        (dc * NDC + oc) * 128 + 128]
                                t.ldweights(lh)
                                t.matmul(po[:], lh, rb[:, dc * 512:(dc + 1) * 512],
                                         start=(dc == 0), stop=False)
                            lh = BWB[:, oc * 128:(oc + 1) * 128]
                            t.ldweights(lh)
                            i = t.matmul(po[:], lh, hT[:, tb * 512:(tb + 1) * 512],
                                         start=False, stop=True)
                            TC.inc(i)

                    @block.vector
                    def _(v, tb=tb, tc0=tc0, t_h=t_h, t_lg=t_lg, t_se=t_se,
                          st_mk=st_mk):
                        v.wait_ge(TC.h, t_h)
                        i = v.tensor_copy(hT[:, tb * 512:(tb + 1) * 512],
                                          p512[0][:])
                        VC.inc(i)
                        for tc4 in range(4):
                            v.wait_ge(TC.h, t_lg(tc4))
                            tgt = st_mk(tc4 - 1) if tc4 >= 1 else smk0
                            if tgt:
                                v.wait_ge(SMK.h, tgt)
                            i = v.tensor_copy(lgt[:], p512[1][0:128, 0:8])
                            vedge(v, i)
                            i = v.reduce_max(mxc[:], lgt[:], axis=AXX)
                            vedge(v, i)
                            i = v.tensor_single_scalar(mk8[:], lgt[:],
                                                       mxc[:, 0:1], ALU.is_ge)
                            VC.inc(i)
                        v.wait_ge(TC.h, t_se)
                        i = v.tensor_tensor(hT[:, tb * 512:(tb + 1) * 512],
                                            hT[:, tb * 512:(tb + 1) * 512],
                                            p512[2][:], ALU.mult)
                        VC.inc(i)

                    @block.scalar
                    def _(s, tb=tb, t_base=t_base, st_g1=st_g1):
                        for oc in range(NDC):
                            s.wait_ge(TC.h, t_base(oc))
                            tgt = st_g1(oc - 2) if oc >= 2 else sg0[oc % 2]
                            if tgt:
                                s.wait_ge(SG[oc % 2].h, tgt)
                            i = s.activation(
                                g1t[:, (oc % 2) * 512:(oc % 2) * 512 + 512],
                                p512[3 + oc % 2][:], AF.Silu,
                                bias=SM[:, BB_C + oc:BB_C + oc + 1])
                            SC.inc(i)

                    @block.gpsimd
                    def _(g, tb=tb, v_mk=v_mk, sc_g1=sc_g1):
                        for tc4 in range(4):
                            g.wait_ge(VC.h, v_mk(tc4))
                            i = g.dma_start(
                                out=mk_sc[tb * 512 + tc4 * 128:
                                          tb * 512 + (tc4 + 1) * 128, 0:8],
                                in_=mk8[:])
                            SMK.inc(i, 16)
                        for oc in range(NDC):
                            g.wait_ge(SC.h, sc_g1(oc))
                            i = g.dma_start(
                                out=g1T_sc[oc * 128:(oc + 1) * 128,
                                           tb * 512:(tb + 1) * 512],
                                in_=g1t[:, (oc % 2) * 512:(oc % 2) * 512 + 512])
                            SG[oc % 2].inc(i, 16)

                    prev_vh = v_h
                    prev_vhm = v_hm
                    prev_tse = t_se

        if emit_l1 and phase_hi >= 8:
            HT = TOWN // 2
            with (
                nc.sbuf_tensor("ktl", [128, TOWN], BF16) as ktl,
                nc.sbuf_tensor("vtl", [128, TOWN], BF16) as vtl,
                nc.sbuf_tensor("rtl", [128, TOWN], BF16) as rtl,
                nc.sbuf_tensor("gtl", [128, TOWN], BF16) as gtl,
                nc.sbuf_tensor("ekf", [128, HT], F32) as ekf,
                nc.sbuf_tensor("avf", [128, HT], F32) as avf,
                nc.sbuf_tensor("n0f", [128, HT], F32) as n0f,
                nc.sbuf_tensor("d0f", [128, HT], F32) as d0f,
                nc.sbuf_tensor("wrp", [128, HT], F32) as wrp,
                nc.sbuf_tensor("sth", [128, 4], F32) as sth,
                nc.sbuf_tensor("dta", [128, TOWN], BF16) as dta,
                nc.sbuf_tensor("dtb", [128, TOWN], BF16) as dtb,
            ):
                douts = [dta, dtb]

                @block.scalar
                def _(s):
                    s.wait_ge(LSM.h, p0_ld)
                    i = s.activation(SM[:, ECOL_C:ECOL_C + 16],
                                      SM[:, TD_C:TD_C + 16], AF.Exp)
                    sedge(s, i)
                    i = s.activation(SM[:, NECOL_C:NECOL_C + 16],
                                     SM[:, ECOL_C:ECOL_C + 16], AF.Copy,
                                     scale=-1.0)
                    sedge(s, i)
                    i = s.activation(SM[:, WCOL_C:WCOL_C + 16],
                                     SM[:, ECOL_C:ECOL_C + 16], AF.Exp, scale=-1.0)
                    SC.inc(i)
                sc_w = SC.n

                ld0, sc0, vc0 = L8.n, SC.n, VC.n
                ldA = lambda dc: ld0 + 32 * (dc + 1)
                scA = lambda dc, k: sc0 + 3 * dc + k
                vcA = lambda dc, k: vc0 + 6 * dc + k

                @block.sync
                def _(sy):
                    wait_stores(sy)
                    for dc in range(NDC):
                        if dc >= 1:
                            sy.wait_ge(VC.h, vcA(dc - 1, 6))
                        i = sy.dma_start(out=ktl[:],
                                         in_=kT1_sc[dc * 128:(dc + 1) * 128, :])
                        L8.inc(i, 16)
                        i = sy.dma_start(out=vtl[:],
                                         in_=vT1_sc[dc * 128:(dc + 1) * 128, :])
                        L8.inc(i, 16)

                @block.scalar
                def _(s):
                    s.wait_ge(SC.h, sc_w)
                    for dc in range(NDC):
                        s.wait_ge(L8.h, ldA(dc))
                        if dc >= 1:
                            s.wait_ge(VC.h, vcA(dc - 1, 3))
                        i = s.activation(ekf[:], ktl[:, 0:HT], AF.Exp)
                        SC.inc(i)
                        i = s.activation(wrp[:], ktl[:, 0:HT], AF.Exp, scale=0.0,
                                         bias=SM[:, NECOL_C + dc:NECOL_C + dc + 1])
                        SC.inc(i)
                        s.wait_ge(VC.h, vcA(dc, 3))
                        i = s.activation(ekf[:], ktl[:, HT:TOWN], AF.Exp)
                        SC.inc(i)

                @block.vector
                def _(v):
                    for dc in range(NDC):
                        v.wait_ge(SC.h, scA(dc, 1))
                        v.wait_ge(L8.h, ldA(dc))
                        i = v.tensor_tensor(avf[:], ekf[:], vtl[:, 0:HT], ALU.mult)
                        VC.inc(i)
                        v.wait_ge(VC.h, VC.n)
                        v.wait_ge(SC.h, scA(dc, 2))
                        i = v.tensor_tensor_scan(n0f[:], wrp[:], avf[:], 0.0,
                                                 ALU.mult, ALU.add)
                        VC.inc(i)
                        i = v.tensor_tensor_scan(d0f[:], wrp[:], ekf[:], 0.0,
                                                 ALU.mult, ALU.add)
                        vedge(v, i)
                        v.wait_ge(VC.h, VC.n)
                        v.tensor_copy(sth[:, 0:1], n0f[:, HT - 1:HT])
                        i = v.tensor_copy(sth[:, 1:2], d0f[:, HT - 1:HT])
                        VC.inc(i)
                        v.wait_ge(VC.h, VC.n)
                        v.wait_ge(SC.h, scA(dc, 3))
                        i = v.tensor_tensor(avf[:], ekf[:], vtl[:, HT:TOWN],
                                            ALU.mult)
                        VC.inc(i)
                        v.wait_ge(VC.h, VC.n)
                        i = v.tensor_tensor_scan(n0f[:], wrp[:], avf[:],
                                                 sth[:, 0:1], ALU.mult, ALU.add)
                        VC.inc(i)
                        v.wait_ge(VC.h, VC.n)
                        i = v.tensor_tensor_scan(d0f[:], wrp[:], ekf[:],
                                                 sth[:, 1:2], ALU.mult, ALU.add)
                        vedge(v, i)
                        v.tensor_copy(SM[:, STN_C + dc:STN_C + dc + 1],
                                      n0f[:, HT - 1:HT])
                        i = v.tensor_copy(SM[:, STD_C + dc:STD_C + dc + 1],
                                          d0f[:, HT - 1:HT])
                        VC.inc(i)

                @block.vector
                def _(v):
                    v.wait_ge(VC.h, VC.n)
                    i = None
                    for bp in range(4):
                        i = v.tensor_scalar(
                            SM[:, CC2S_C + bp * 16:CC2S_C + (bp + 1) * 16],
                            SM[:, STN_C:STN_C + 16],
                            SM[:, WSEL_C + bp:WSEL_C + bp + 1], None, ALU.mult)
                        i = v.tensor_scalar(
                            SM[:, CC2S_C + 64 + bp * 16:CC2S_C + 64 + (bp + 1) * 16],
                            SM[:, STD_C:STD_C + 16],
                            SM[:, WSEL_C + bp:WSEL_C + bp + 1], None, ALU.mult)
                    VC.inc(i)
                vc_cc2 = VC.n

                @block.gpsimd
                def _(g):
                    g.wait_ge(VC.h, vc_cc2)
                    i = g.dma_start(out=cc2_in, in_=SM[:, CC2S_C:CC2S_C + 128])
                    CCD.inc(i, 16)
                    g.wait_ge(CCD.h, CCD.n)
                    i = g.collective_compute(
                        "AllReduce", ALU.add,
                        replica_groups=[[0]] if sim1 else [list(range(8))],
                        ins=[cc2_in.opt()], outs=[cc2_out.opt()])
                    CCB.inc(i, 1)
                    g.wait_ge(CCB.h, 1)
                    i = g.dma_start(out=SM[:, RX2_C:RX2_C + 128], in_=cc2_out)
                    CCD.inc(i, 16)
                gc_rx2 = CCD.n

                @block.vector
                def _(v):
                    v.wait_ge(CCD.h, gc_rx2)
                    for (dstc, base) in ((RN_C, RX2_C), (RD_C, RX2_C + 64)):
                        i = v.tensor_scalar(SM[:, dstc:dstc + 16],
                                            SM[:, base:base + 16],
                                            SM[:, RSEL_C:RSEL_C + 1], None,
                                            ALU.mult)
                        vedge(v, i)
                        for bp in range(1, 4):
                            i = v.scalar_tensor_tensor(
                                SM[:, dstc:dstc + 16],
                                SM[:, base + bp * 16:base + (bp + 1) * 16],
                                SM[:, RSEL_C + bp:RSEL_C + bp + 1],
                                SM[:, dstc:dstc + 16], ALU.mult, ALU.add)
                            vedge(v, i)
                        i = v.tensor_scalar(SM[:, dstc:dstc + 16],
                                            SM[:, dstc:dstc + 16],
                                            SM[:, FLAG_C:FLAG_C + 1], None,
                                            ALU.mult)
                        VC.inc(i)
                        v.wait_ge(VC.h, VC.n)

                ld1, sc1, vc1 = L8.n, SC.n, VC.n
                so0 = [SO[0].n, SO[1].n]
                ldB = lambda dc: ld1 + 64 * (dc + 1)
                scB = lambda dc, hf, k: sc1 + 6 * dc + 3 * hf + k
                vcB = lambda dc, hf, k: vc1 + 8 * dc + 4 * hf + k
                stB = lambda dc: so0[dc % 2] + 16 * (dc // 2 + 1)

                @block.sync
                def _(sy):
                    sy.wait_ge(VC.h, vc1)
                    for dc in range(NDC):
                        if dc >= 1:
                            sy.wait_ge(VC.h, vcB(dc - 1, 1, 4))
                        i = sy.dma_start(out=ktl[:],
                                         in_=kT1_sc[dc * 128:(dc + 1) * 128, :])
                        L8.inc(i, 16)
                        i = sy.dma_start(out=vtl[:],
                                         in_=vT1_sc[dc * 128:(dc + 1) * 128, :])
                        L8.inc(i, 16)
                        i = sy.dma_start(out=rtl[:],
                                         in_=rT1_sc[dc * 128:(dc + 1) * 128, :])
                        L8.inc(i, 16)
                        i = sy.dma_start(out=gtl[:],
                                         in_=g1T_sc[dc * 128:(dc + 1) * 128, :])
                        L8.inc(i, 16)

                @block.scalar
                def _(s):
                    for dc in range(NDC):
                        s.wait_ge(L8.h, ldB(dc))
                        for hf in range(2):
                            sl = slice(hf * HT, (hf + 1) * HT)
                            if dc + hf > 0:
                                prev = vcB(dc, 0, 3) if hf else vcB(dc - 1, 1, 3)
                                s.wait_ge(VC.h, prev)
                            i = s.activation(ekf[:], ktl[:, sl], AF.Exp)
                            SC.inc(i)
                            i = s.activation(wrp[:], ktl[:, sl], AF.Exp,
                                             scale=0.0,
                                             bias=SM[:, NECOL_C + dc:NECOL_C + dc + 1])
                            SC.inc(i)
                            s.wait_ge(VC.h, vcB(dc, hf, 3))
                            i = s.activation(d0f[:], d0f[:], AF.Ln,
                                             bias=SM[0:128, EPS2_C:EPS2_C + 1])
                            sedge(s, i)
                            i = s.activation(d0f[:], d0f[:], AF.Exp, scale=-1.0)
                            SC.inc(i)

                @block.vector
                def _(v):
                    for dc in range(NDC):
                        for hf in range(2):
                            sl = slice(hf * HT, (hf + 1) * HT)
                            v.wait_ge(SC.h, scB(dc, hf, 1))
                            i = v.tensor_tensor(avf[:], ekf[:], vtl[:, sl],
                                                ALU.mult)
                            VC.inc(i)
                            v.wait_ge(VC.h, VC.n)
                            v.wait_ge(SC.h, scB(dc, hf, 2))
                            ini_n = (SM[:, RN_C + dc:RN_C + dc + 1] if hf == 0
                                     else sth[:, 2:3])
                            ini_d = (SM[:, RD_C + dc:RD_C + dc + 1] if hf == 0
                                     else sth[:, 3:4])
                            i = v.tensor_tensor_scan(n0f[:], wrp[:], avf[:],
                                                     ini_n, ALU.mult, ALU.add)
                            VC.inc(i)
                            v.wait_ge(VC.h, VC.n)
                            if hf == 0:
                                i = v.tensor_copy(sth[:, 2:3], n0f[:, HT - 1:HT])
                                vedge(v, i)
                            i = v.tensor_tensor_scan(d0f[:], wrp[:], ekf[:],
                                                     ini_d, ALU.mult, ALU.add)
                            if hf == 0:
                                vedge(v, i)
                                i = v.tensor_copy(sth[:, 3:4], d0f[:, HT - 1:HT])
                            VC.inc(i)
                            v.wait_ge(VC.h, VC.n)
                            v.wait_ge(SC.h, scB(dc, hf, 3))
                            i = v.tensor_tensor(n0f[:], n0f[:], d0f[:], ALU.mult)
                            vedge(v, i)
                            i = v.tensor_tensor(n0f[:], n0f[:], rtl[:, sl],
                                                ALU.mult)
                            vedge(v, i)
                            if dc >= 2 and hf == 0:
                                v.wait_ge(SO[dc % 2].h, stB(dc - 2))
                            i = v.tensor_tensor(douts[dc % 2][:, sl], n0f[:],
                                                gtl[:, sl], ALU.mult)
                            VC.inc(i)

                @block.gpsimd
                def _(g):
                    for dc in range(NDC):
                        g.wait_ge(VC.h, vcB(dc, 1, 4))
                        i = g.dma_start(out=dT_sc[dc * 128:(dc + 1) * 128, :],
                                        in_=douts[dc % 2][:])
                        SO[dc % 2].inc(i, 16)

        if emit_l1 and phase_hi >= 9:
            with (
                nc.sbuf_tensor("xr9a", [128, D], F32) as xr9a,
                nc.sbuf_tensor("xr9b", [128, D], F32) as xr9b,
                nc.sbuf_tensor("dl9a", [128, D], BF16) as dl9a,
                nc.sbuf_tensor("dl9b", [128, D], BF16) as dl9b,
                nc.sbuf_tensor("o9a", [128, D], F32) as o9a,
                nc.sbuf_tensor("o9b", [128, D], F32) as o9b,
            ):
                xr9, dl9, o9 = [xr9a, xr9b], [dl9a, dl9b], [o9a, o9b]
                vc0 = VC.n
                lx0 = [LX[0].n, LX[1].n]
                ld9 = lambda tt: lx0[tt % 2] + 80 * (tt // 2 + 1)
                os0 = [OS[0].n, OS[1].n]
                os_at = lambda tt: os0[tt % 2] + 16 * (tt // 2 + 1)

                @block.sync
                def _(sy):
                    wait_stores(sy)
                    for tt in range(16):
                        if tt >= 2:
                            sy.wait_ge(VC.h, vc0 + tt - 1)
                        i = sy.dma_start(out=xr9[tt % 2][:],
                                         in_=x1_tm[tt * 128:(tt + 1) * 128, :])
                        LX[tt % 2].inc(i, 16)
                        for gq in range(4):
                            i = sy.dma_start(
                                out=dl9[tt % 2][:, gq * 512:(gq + 1) * 512],
                                in_=dT_sc[gq * 512:(gq + 1) * 512,
                                          tt * 128:(tt + 1) * 128],
                                transpose=True)
                            LX[tt % 2].inc(i, 16)

                @block.vector
                def _(v):
                    for tt in range(16):
                        v.wait_ge(LX[tt % 2].h, ld9(tt))
                        if tt >= 2:
                            v.wait_ge(OS[tt % 2].h, os_at(tt - 2))
                        i = v.tensor_tensor(o9[tt % 2][:], xr9[tt % 2][:],
                                            dl9[tt % 2][:], ALU.add)
                        VC.inc(i)

                @block.gpsimd
                def _(g):
                    for tt in range(16):
                        g.wait_ge(VC.h, vc0 + tt + 1)
                        i = g.dma_start(out=out_x2[tt * 128:(tt + 1) * 128, :],
                                        in_=o9[tt % 2][:])
                        OS[tt % 2].inc(i, 16)

        @block.sync
        def _(sy):
            for sm in ALL_DMA + [TC, VC, SC, GC, CCA, CCB, CCD, VSs, SSs]:
                if sm.n:
                    sy.wait_ge(sm.h, sm.n)

    return nc


def _prep_weights(inp):
    t = lambda a: np.ascontiguousarray(np.asarray(a))
    w = {}

    def chunk_T(m):
        Dm, O = m.shape
        return t(m.reshape(Dm // 128, 128, O).transpose(1, 0, 2).reshape(128, -1))

    w["wqT"] = chunk_T(t(np.asarray(inp["L0_attn_wq"]).T))
    w["wkT"] = chunk_T(t(np.asarray(inp["L0_attn_wk"]).T))
    w["wvT"] = chunk_T(t(np.asarray(inp["L0_attn_wv"]).T))
    w["woT"] = chunk_T(t(np.asarray(inp["L0_attn_wo"]).T))
    for nm, key in (("w_gate", "L0_ffn_w_gate"), ("w_up", "L0_ffn_w_up")):
        m = np.asarray(inp[key])
        w[nm] = t(m.reshape(16, 128, 16, 512).transpose(1, 2, 0, 3).reshape(128, -1))
    md = np.asarray(inp["L0_ffn_w_down"])
    w["w_down"] = t(md.reshape(64, 128, 16, 128).transpose(1, 2, 0, 3).reshape(128, -1))
    w["sgate_c"] = t(np.asarray(inp["L0_ffn_s_gate"]).reshape(64, 128).T)
    w["sup_c"] = t(np.asarray(inp["L0_ffn_s_up"]).reshape(64, 128).T)
    w["sdown_c"] = t(np.asarray(inp["L0_ffn_s_down"]).reshape(16, 128).T)
    w["wkT1"] = chunk_T(t(np.asarray(inp["L1_attn_Wk"]).T))
    w["wvT1"] = chunk_T(t(np.asarray(inp["L1_attn_Wv"]).T))
    w["wrT1"] = chunk_T(t(np.asarray(inp["L1_attn_Wr"]).T))
    w["wbT"] = chunk_T(t(np.asarray(inp["L1_ffn_W_base"]).T))
    w["td_c"] = t(np.asarray(inp["L1_attn_time_decay"]).reshape(16, 128).T)
    w["bb_c"] = t(np.asarray(inp["L1_ffn_b_base"]).reshape(16, 128).T)
    w["rwT"] = chunk_T(t(np.asarray(inp["L1_ffn_router_W"]).T))
    w["AfT"] = chunk_T(t(np.asarray(inp["L1_ffn_A"]).reshape(E * R, D).T))
    w["BwF"] = t(np.asarray(inp["L1_ffn_Bw"]).transpose(0, 2, 1).reshape(E * R, D))
    sel = np.zeros((8, 128), np.float32)
    for e in range(E):
        sel[e, e * R:(e + 1) * R] = 2.0
    w["selm"] = sel
    return w


def _per_core(inp, b, h):
    x = np.asarray(inp["L0_attn_x"])
    own = x[b, h * TOWN:(h + 1) * TOWN]
    oth = x[b, (1 - h) * TOWN:(2 - h) * TOWN]
    m = {"x_in": np.ascontiguousarray(np.concatenate([own, oth], axis=0))}
    cb = np.zeros((128, 128), np.float32)
    for qi in range(4):
        for j in range(32):
            if j < 16:
                v = SHIFT if j <= 4 * qi + 3 else NEG
            else:
                v = SHIFT if h == 1 else NEG
            cb[:, qi * 32 + j] = v
    m["cb"] = cb
    m["flag"] = np.full((128, 1), float(h), np.float32)
    ws = np.zeros((128, 4), np.float32)
    ws[:, b] = 1.0 - h
    m["wsel"] = ws
    rs = np.zeros((128, 4), np.float32)
    rs[:, b] = 1.0
    m["rsel"] = rs
    return m


def kernel(**inputs):
    w = _prep_weights(inputs)
    in_maps = []
    for c in range(8):
        b, h = c // 2, c % 2
        m = dict(w)
        m.update(_per_core(inputs, b, h))
        in_maps.append(m)
    nc = build(phase_hi=PHASE_HI, dbg=DBG, part=PART)
    res = run_bass_kernel_spmd(nc, in_maps, core_ids=list(range(8)))
    out = np.empty((B, T, D), np.float32)
    for c in range(8):
        b, h = c // 2, c % 2
        out[b, h * TOWN:(h + 1) * TOWN] = np.asarray(res.results[c]["out_x2"])
    return out


# version 04
# speedup vs baseline: 1.0542x; 1.0518x over previous
import numpy as np
import ml_dtypes
from contextlib import ExitStack

import concourse.bass as bass
import concourse.mybir as mybir
from concourse.bass_utils import run_bass_kernel_spmd

F32 = mybir.dt.float32
BF16 = mybir.dt.bfloat16
I8 = mybir.dt.int8
AF = mybir.ActivationFunctionType
ALU = mybir.AluOpType
AXX = mybir.AxisListType.X

B, T, D = 4, 4096, 2048
TOWN = 2048
NDC = 16
NH = 16
INTER = 8192
NIC = 64
E, R = 8, 16
RSQ = float(1.0 / np.sqrt(128.0))
SHIFT = -20.0
NEG = -30000.0

PHASE_HI = 99
DBG = ()
PART = "all"


class _EndL0(Exception):
    pass


def build(phase_hi=99, dbg=(), part="all", sim1=False):
    nc = bass.Bass("TRN2", target_bir_lowering=False, debug=False)
    emit_l0 = part in ("all", "l0")
    emit_l1 = part in ("all", "l1")

    def dram(name, shape, dt):
        kind = "ExternalOutput" if name in dbg else "Internal"
        return nc.dram_tensor(name, shape, dt, kind=kind).ap()

    ein = lambda n, s, d: nc.dram_tensor(n, s, d, kind="ExternalInput").ap()

    if emit_l0:
        d_x = ein("x_in", [T, D], F32)
        d_wqT = ein("wqT", [128, NDC * D], F32)
        d_wkT = ein("wkT", [128, NDC * D], F32)
        d_wvT = ein("wvT", [128, NDC * D], F32)
        d_woT = ein("woT", [128, NDC * D], F32)
        d_wg = ein("w_gate", [128, 16 * 16 * 512], I8)
        d_wu = ein("w_up", [128, 16 * 16 * 512], I8)
        d_wd = ein("w_down", [128, 16 * 64 * 128], I8)
        d_sg = ein("sgate_c", [128, NIC], F32)
        d_su = ein("sup_c", [128, NIC], F32)
        d_sd = ein("sdown_c", [128, NDC], F32)
        d_cb = ein("cb", [128, 128], F32)
    if emit_l1:
        d_wkT1 = ein("wkT1", [128, NDC * D], F32)
        d_wvT1 = ein("wvT1", [128, NDC * D], F32)
        d_wrT1 = ein("wrT1", [128, NDC * D], F32)
        d_wbT = ein("wbT", [128, NDC * D], F32)
        d_td = ein("td_c", [128, NDC], F32)
        d_bb = ein("bb_c", [128, NDC], F32)
        d_rw = ein("rwT", [128, NDC * 8], F32)
        d_Af = ein("AfT", [128, NDC * 128], F32)
        d_Bw = ein("BwF", [128, D], F32)
        d_sel = ein("selm", [8, 128], F32)
    d_flag = ein("flag", [128, 1], F32)
    d_fi = ein("flagi", [1, 1], mybir.dt.int32)
    d_wsel = ein("wsel", [128, 4], F32)
    d_rsel = ein("rsel", [128, 4], F32)

    if part == "all":
        x1_tm = dram("x1_tm", [TOWN, D], F32)
        xn1_tm = dram("xn1_tm", [TOWN + 1, D], BF16)
    elif part == "l0":
        x1_tm = nc.dram_tensor("x1_tm", [TOWN, D], F32, kind="ExternalOutput").ap()
        xn1_tm = nc.dram_tensor("xn1_tm", [TOWN + 1, D], BF16,
                                kind="ExternalOutput").ap()
    else:
        x1_tm = ein("x1_tm", [TOWN, D], F32)
        xn1_tm = ein("xn1_tm", [TOWN + 1, D], BF16)
    if emit_l1:
        out_x2 = nc.dram_tensor("out_x2", [TOWN, D], F32, kind="ExternalOutput").ap()

    if emit_l0:
        xn_tm = dram("xn_tm", [T, D], BF16)
        kT_sc = dram("kT_sc", [D, T], BF16)
        vT_sc = dram("vT_sc", [D, T], BF16)
        qT_sc = dram("qT_sc", [D, TOWN], BF16)
        sT_sc = dram("sT_sc", [D, TOWN], BF16)
        cc1_in = dram("cc1_in", [128, 64], F32)
        cc1_out = dram("cc1_out", [128, 64], F32)
    dT_sc = dram("dT_sc", [D, TOWN], BF16)
    if emit_l1:
        kT1_sc = dram("kT1_sc", [D, TOWN], BF16)
        vT1_sc = dram("vT1_sc", [D, TOWN], BF16)
        rT1_sc = dram("rT1_sc", [D, TOWN], BF16)
        g1T_sc = dram("g1T_sc", [D, TOWN], BF16)
        mk_sc = dram("mk_sc", [TOWN, 128], BF16)
        cc2_in = dram("cc2_in", [128, 128], F32)
        cc2_out = dram("cc2_out", [128, 128], F32)

    ctx = ExitStack()
    with ctx:
        sb = lambda n, s, d: ctx.enter_context(nc.sbuf_tensor(n, s, d))
        psm = lambda n, s, d=F32: ctx.enter_context(nc.psum_tensor(n, s, d))

        class Sem:
            def __init__(self, name):
                self.h = ctx.enter_context(nc.semaphore(name))
                self.n = 0

            def inc(self, instr, k=1):
                instr.then_inc(self.h, k)
                self.n += k
                return self.n

        A = sb("arenaA", [128, NDC * D], BF16)
        Bb = sb("arenaB", [128, NDC * D], BF16)
        SM = sb("smalls", [128, 2048], F32)
        SMB = sb("smallsb", [128, 2708], BF16)
        EV = sb("evac", [128, 2048], BF16)
        R8 = sb("row8", [8, 256], BF16)
        RWB = sb("rwb", [128, NDC * 8], BF16)
        FI = sb("flagi_sb", [1, 1], mybir.dt.int32)

        p512 = [psm(f"ps{i}", [128, 512]) for i in range(6)]
        pD = psm("psD", [128, 512])
        pB = psm("psB", [128, 512])

        LSM = Sem("LSM")
        LX = [Sem("LX0"), Sem("LX1")]
        GX = [Sem("GX0"), Sem("GX1")]
        GS = [Sem("GS0"), Sem("GS1")]
        LR = [Sem("LR0"), Sem("LR1")]
        KH = [Sem("KH0"), Sem("KH1")]
        LS = [Sem("LS0"), Sem("LS1")]
        LMK = Sem("LMK")
        L8 = Sem("L8")
        WSM = Sem("WSM")
        WA = Sem("WA")
        WB = Sem("WB")
        WG = [Sem("WG0"), Sem("WG1")]
        WU = [Sem("WU0"), Sem("WU1")]
        SX = [Sem("SX0"), Sem("SX1")]
        SE = [Sem(f"SE{i}") for i in range(4)]
        SD = [Sem("SD0"), Sem("SD1")]
        SG = [Sem("SG0"), Sem("SG1")]
        SMK = Sem("SMK")
        SO = [Sem("SO0"), Sem("SO1")]
        OS = [Sem("OS0"), Sem("OS1")]
        STORE_SEMS = SX + SE + SD + SG + [SMK] + SO
        ALL_DMA = ([LSM, LMK, L8, WSM, WA, WB] + LX + GX + LR + KH + LS + WG + WU +
                   STORE_SEMS + GS + OS)

        def wait_stores(eng):
            for sm in STORE_SEMS + GS:
                if sm.n:
                    eng.wait_ge(sm.h, sm.n)

        TC = Sem("TC")
        VC = Sem("VC")
        SC = Sem("SC")
        GC = Sem("GC")
        CCA = Sem("CCA")
        CCB = Sem("CCB")
        CCD = Sem("CCD")
        VSs = Sem("VSs")
        SSs = Sem("SSs")

        def vedge(v, instr):
            VSs.inc(instr)
            v.wait_ge(VSs.h, VSs.n)

        def sedge(s, instr):
            SSs.inc(instr)
            s.wait_ge(SSs.h, SSs.n)

        block = ctx.enter_context(nc.Block())

        (SUM_C, SQ_C, MEAN_C, NMEAN_C, VAR_C, RSTD_C, NMUR_C, T1_C, T2_C) = range(9)
        FLAG_C, WSEL_C, RSEL_C = 10, 12, 16
        EPS1_C, EPS2_C = 20, 21
        CB_C = 24
        IOTA_C = 160
        KPCOL_C = 676
        MCOL_C = 680
        ECOL_C = 690
        NECOL_C = 1970
        WCOL_C = 710
        BND_C = 730
        CCS_C = 750
        RX_C = 820
        ACC_C = 890
        DEN_C = 910
        STN_C = 1430
        STD_C = 1450
        RN_C = 1470
        RD_C = 1490
        SG_C = 1510
        SU_C = 1580
        SD_C = 1650
        BB_C = 1670
        TD_C = 1690
        CC2S_C = 1710
        RX2_C = 1840
        MK0_C = 0
        ONES_C = 2048
        ONESR_C = 2052
        RB_C = 2192

        @block.sync
        def _(sy):
            i = sy.dma_start(out=FI[:], in_=d_fi)
            LSM.inc(i, 16)
            i = sy.dma_start(out=SM[:, FLAG_C:FLAG_C + 1], in_=d_flag)
            LSM.inc(i, 16)
            i = sy.dma_start(out=SM[:, WSEL_C:WSEL_C + 4], in_=d_wsel)
            LSM.inc(i, 16)
            i = sy.dma_start(out=SM[:, RSEL_C:RSEL_C + 4], in_=d_rsel)
            LSM.inc(i, 16)
            if emit_l0:
                i = sy.dma_start(out=SM[:, CB_C:CB_C + 128], in_=d_cb)
                LSM.inc(i, 16)
                i = sy.dma_start(out=SM[:, SG_C:SG_C + 64], in_=d_sg)
                LSM.inc(i, 16)
                i = sy.dma_start(out=SM[:, SU_C:SU_C + 64], in_=d_su)
                LSM.inc(i, 16)
                i = sy.dma_start(out=SM[:, SD_C:SD_C + 16], in_=d_sd)
                LSM.inc(i, 16)
            if emit_l1:
                i = sy.dma_start(out=SM[:, TD_C:TD_C + 16], in_=d_td)
                LSM.inc(i, 16)
                i = sy.dma_start(out=SM[:, BB_C:BB_C + 16], in_=d_bb)
                LSM.inc(i, 16)

        @block.gpsimd
        def _(g):
            g.iota(SM[:, IOTA_C:IOTA_C + 512], pattern=[[1, 512]], base=0,
                   channel_multiplier=0, allow_small_or_imprecise_dtypes=True)
            i = g.iota(SM[:, KPCOL_C:KPCOL_C + 1], pattern=[[1, 1]], base=0,
                       channel_multiplier=1, allow_small_or_imprecise_dtypes=True)
            GC.inc(i)
            if emit_l1:
                i = g.dma_start(out=R8[0:8, 0:128], in_=d_sel)
                WSM.inc(i, 16)
                i = g.dma_start(out=RWB[:], in_=d_rw)
                WSM.inc(i, 16)
        p0_gc, p0_ld, p0_wt = GC.n, LSM.n, WSM.n

        @block.vector
        def _(v):
            v.memset(SMB[:, ONES_C:ONES_C + 1], 1.0)
            v.memset(SMB[0:1, ONESR_C:ONESR_C + 128], 1.0)
            v.memset(SM[:, EPS1_C:EPS1_C + 1], 1e-5)
            v.memset(SM[:, EPS2_C:EPS2_C + 1], 1e-8)
            v.wait_ge(GC.h, p0_gc)
            v.wait_ge(LSM.h, p0_ld)
            i = None
            for m in range(4):
                i = v.tensor_scalar_add(SM[:, MCOL_C + m:MCOL_C + m + 1],
                                        SM[:, KPCOL_C:KPCOL_C + 1], float(128 * m))
            vedge(v, i)
            for m in range(4):
                i = v.tensor_single_scalar(
                    SMB[:, MK0_C + m * 512:MK0_C + (m + 1) * 512],
                    SM[:, IOTA_C:IOTA_C + 512],
                    SM[:, MCOL_C + m:MCOL_C + m + 1], ALU.is_ge)
            VC.inc(i)
        p0_vc = VC.n

        ROLE = {}

        def _mk_role(nm, dec):
            @dec
            def _(eng):
                r = eng.alloc_register()
                eng.wait_ge(LSM.h, p0_ld)
                eng.reg_load(r, FI[0:1, 0:1])
                ROLE[nm] = r

        if emit_l0 and phase_hi >= 3:
            _mk_role("t", block.tensor)
            _mk_role("s", block.scalar)
            _mk_role("v", block.vector)
            _mk_role("sy", block.sync)

        def load_weight(dst, src, wsem):
            tc_req = TC.n

            @block.gpsimd
            def _(g):
                if tc_req:
                    g.wait_ge(TC.h, tc_req)
                for gi in range(4):
                    i = g.dma_start(out=dst[:, gi * 4 * D:(gi + 1) * 4 * D],
                                    in_=src[:, gi * 4 * D:(gi + 1) * 4 * D])
                    wsem.inc(i, 16)
            return wsem.n

        def emit_ln(n_tiles, load_src, dst_tm, dst_row0, xf, xn16, dum16,
                    pre_vec=None, store_xrow=None, dma_dec=None):
            LXq = LX if dma_dec is None else GX
            SXq = SX if dma_dec is None else GS
            if dma_dec is None:
                dma_dec = block.sync

            pv = 1 if pre_vec is not None else 0
            sc0, vc0 = SC.n, VC.n
            per_st = 16 * (2 if store_xrow is not None else 1)
            sx0 = [SXq[0].n, SXq[1].n]
            st_at = {tt: sx0[tt % 2] + per_st * (tt // 2 + 1)
                     for tt in range(n_tiles)}
            sc_sums = lambda tt: sc0 + 4 * tt + 1
            sc_rstd = lambda tt: sc0 + 4 * tt + 4
            vc_pre = lambda tt: vc0 + (2 + pv) * tt + 1
            vc_varA = lambda tt: vc0 + (2 + pv) * tt + pv + 1
            vc_xn = lambda tt: vc0 + (2 + pv) * tt + pv + 2
            ld_at = {}

            if load_src is not None:
                @dma_dec
                def _(sy):
                    for tt in range(n_tiles + 1):
                        if tt < n_tiles:
                            if tt >= 2:
                                sy.wait_ge(VC.h, vc_xn(tt - 2))
                            i = sy.dma_start(out=xf[tt % 2][:], in_=load_src(tt))
                            ld_at[tt] = (LXq[tt % 2], LXq[tt % 2].inc(i, 16))
                        if tt >= 1:
                            pt = tt - 1
                            sy.wait_ge(VC.h, vc_xn(pt))
                            if store_xrow is not None:
                                i = sy.dma_start(out=store_xrow(pt),
                                                 in_=xf[pt % 2][:])
                                SXq[pt % 2].inc(i, 16)
                            i = sy.dma_start(
                                out=dst_tm[dst_row0 + pt * 128:
                                           dst_row0 + (pt + 1) * 128, :],
                                in_=xn16[pt % 2][:])
                            SXq[pt % 2].inc(i, 16)

            @block.scalar
            def _(s):
                for tt in range(n_tiles):
                    if load_src is not None:
                        s.wait_ge(ld_at[tt][0].h, ld_at[tt][1])
                    else:
                        s.wait_ge(VC.h, vc_pre(tt))
                    if tt >= 1:
                        s.wait_ge(VC.h, vc_varA(tt - 1))
                    i = s.activation(dum16[0][:], xf[tt % 2][:], AF.Copy,
                                     accum_out=SM[:, SUM_C:SUM_C + 1])
                    sedge(s, i)
                    i = s.activation(dum16[0][:], xf[tt % 2][:], AF.Square,
                                     accum_out=SM[:, SQ_C:SQ_C + 1])
                    SC.inc(i)
                    s.wait_ge(VC.h, vc_varA(tt))
                    i = s.activation(SM[:, T1_C:T1_C + 1], SM[:, VAR_C:VAR_C + 1],
                                     AF.Ln, bias=SM[:, EPS1_C:EPS1_C + 1])
                    sedge(s, i)
                    i = s.activation(SM[:, RSTD_C:RSTD_C + 1],
                                     SM[:, T1_C:T1_C + 1], AF.Exp, scale=-0.5)
                    SC.inc(i, 3)

            @block.vector
            def _(v):
                for tt in range(n_tiles):
                    if pre_vec is not None:
                        if tt >= 2:
                            v.wait_ge(SXq[tt % 2].h, st_at[tt - 2])
                        pre_vec(v, tt)
                    v.wait_ge(SC.h, sc_sums(tt))
                    i = v.tensor_scalar_mul(SM[:, MEAN_C:MEAN_C + 1],
                                            SM[:, SUM_C:SUM_C + 1], 1.0 / D)
                    vedge(v, i)
                    i = v.tensor_scalar_mul(SM[:, NMEAN_C:NMEAN_C + 1],
                                            SM[:, MEAN_C:MEAN_C + 1], -1.0)
                    i = v.tensor_scalar_mul(SM[:, T2_C:T2_C + 1],
                                            SM[:, SQ_C:SQ_C + 1], 1.0 / D)
                    vedge(v, i)
                    i = v.scalar_tensor_tensor(
                        SM[:, VAR_C:VAR_C + 1], SM[:, MEAN_C:MEAN_C + 1],
                        SM[:, NMEAN_C:NMEAN_C + 1], SM[:, T2_C:T2_C + 1],
                        ALU.mult, ALU.add)
                    VC.inc(i)
                    v.wait_ge(SC.h, sc_rstd(tt))
                    i = v.tensor_scalar(SM[:, NMUR_C:NMUR_C + 1],
                                        SM[:, NMEAN_C:NMEAN_C + 1],
                                        SM[:, RSTD_C:RSTD_C + 1], None, ALU.mult)
                    vedge(v, i)
                    if tt >= 2 and pre_vec is None:
                        v.wait_ge(SXq[tt % 2].h, st_at[tt - 2])
                    i = v.tensor_scalar(xn16[tt % 2][:], xf[tt % 2][:],
                                        SM[:, RSTD_C:RSTD_C + 1],
                                        SM[:, NMUR_C:NMUR_C + 1],
                                        ALU.mult, ALU.add)
                    VC.inc(i)

            emit_ln_last_st_at = st_at

            def store_tile(sy, tt):
                sy.wait_ge(VC.h, vc_xn(tt))
                if store_xrow is not None:
                    i = sy.dma_start(out=store_xrow(tt), in_=xf[tt % 2][:])
                    SXq[tt % 2].inc(i, 16)
                i = sy.dma_start(
                    out=dst_tm[dst_row0 + tt * 128:
                               dst_row0 + (tt + 1) * 128, :],
                    in_=xn16[tt % 2][:])
                SXq[tt % 2].inc(i, 16)
            store_tile.st_at = st_at
            store_tile.slots = SXq
            return store_tile

        def emit_proj(warena, wsem, wt_ready, rhs_src, t_blocks, dstf, rbufs,
                      rhs_gate=None):
            tc0, vc0 = TC.n, VC.n
            lr0 = [LR[0].n, LR[1].n]
            ld_at = {tb: lr0[tb % 2] + 256 * (tb // 2 + 1)
                     for tb in range(t_blocks)}
            se0 = [sm.n for sm in SE]

            @block.sync
            def _(sy):
                for tb in range(t_blocks):
                    rb = rbufs[tb % 2]
                    if tb >= 2:
                        sy.wait_ge(TC.h, tc0 + 16 * (tb - 1))
                    elif tc0:
                        sy.wait_ge(TC.h, tc0)
                    if rhs_gate is not None:
                        for (gs, gt) in rhs_gate(tb):
                            sy.wait_ge(gs.h, gt)
                    for dc in range(NDC):
                        i = sy.dma_start(out=rb[:, dc * 512:(dc + 1) * 512],
                                         in_=rhs_src(tb, dc), transpose=True)
                        LR[tb % 2].inc(i, 16)

            @block.tensor
            def _(t):
                t.wait_ge(wsem.h, wt_ready)
                for tb in range(t_blocks):
                    rb = rbufs[tb % 2]
                    t.wait_ge(LR[tb % 2].h, ld_at[tb])
                    for oc in range(NDC):
                        k = tb * 16 + oc
                        pp = p512[k % 4]
                        if k >= 4:
                            t.wait_ge(VC.h, vc0 + k - 3)
                        for dc in range(NDC):
                            lh = warena[:, (dc * NDC + oc) * 128:
                                        (dc * NDC + oc) * 128 + 128]
                            t.ldweights(lh)
                            i = t.matmul(pp[:], lh, rb[:, dc * 512:(dc + 1) * 512],
                                         start=(dc == 0), stop=(dc == NDC - 1))
                        TC.inc(i)

            @block.vector
            def _(v):
                for k in range(t_blocks * 16):
                    v.wait_ge(TC.h, tc0 + k + 1)
                    tgt = se0[k % 4] + 16 * (k // 4)
                    if tgt:
                        v.wait_ge(SE[k % 4].h, tgt)
                    i = v.tensor_copy(EV[:, (k % 4) * 512:(k % 4 + 1) * 512],
                                      p512[k % 4][:])
                    VC.inc(i)

            @block.gpsimd
            def _(g):
                for tb in range(t_blocks):
                    for oc in range(NDC):
                        k = tb * 16 + oc
                        g.wait_ge(VC.h, vc0 + k + 1)
                        i = g.dma_start(out=dstf(oc, tb),
                                        in_=EV[:, (k % 4) * 512:(k % 4 + 1) * 512])
                        SE[k % 4].inc(i, 16)

        wq_rdy = load_weight(A, d_wqT, WA) if (emit_l0 and phase_hi >= 2) else 0
        if emit_l0 and phase_hi >= 1:
            with (
                nc.sbuf_tensor("xfa", [128, D], F32) as xfa,
                nc.sbuf_tensor("xfb", [128, D], F32) as xfb,
                nc.sbuf_tensor("xna", [128, D], BF16) as xna,
                nc.sbuf_tensor("xnb", [128, D], BF16) as xnb,
                nc.sbuf_tensor("dum", [128, D], BF16) as dum,
                nc.sbuf_tensor("rhsC", [128, NDC * 512], BF16) as C,
                nc.sbuf_tensor("rhsD", [128, NDC * 512], BF16) as Dd,
            ):
                p1_store = emit_ln(
                    T // 128, lambda tt: d_x[tt * 128:(tt + 1) * 128, :],
                    xn_tm, 0, [xfa, xfb], [xna, xnb], (dum, dum),
                    dma_dec=block.gpsimd)
                p1_st_at = p1_store.st_at

                rhs_src = lambda tb, dc: xn_tm[tb * 512:(tb + 1) * 512,
                                               dc * 128:(dc + 1) * 128]

                p1_slots = p1_store.slots

                def p1_gate(tb):
                    tt = tb * 4 + 3
                    return [(p1_slots[tt % 2], p1_st_at[tt]),
                            (p1_slots[(tt - 1) % 2], p1_st_at[tt - 1])]

                if phase_hi >= 2:
                    emit_proj(A, WA, wq_rdy, rhs_src, 4,
                              lambda oc, tb: qT_sc[oc * 128:(oc + 1) * 128,
                                                   tb * 512:(tb + 1) * 512],
                              (C, Dd), rhs_gate=p1_gate)
                    wk_rdy = load_weight(Bb, d_wkT, WB)
                    emit_proj(Bb, WB, wk_rdy, rhs_src, 8,
                              lambda oc, tb: kT_sc[oc * 128:(oc + 1) * 128,
                                                   tb * 512:(tb + 1) * 512],
                              (C, Dd), rhs_gate=p1_gate)
                    wv_rdy = load_weight(A, d_wvT, WA)
                    emit_proj(A, WA, wv_rdy, rhs_src, 8,
                              lambda oc, tb: vT_sc[oc * 128:(oc + 1) * 128,
                                                   tb * 512:(tb + 1) * 512],
                              (C, Dd), rhs_gate=p1_gate)

        if emit_l0 and phase_hi >= 3:
            wo_rdy = load_weight(Bb, d_woT, WB)
            with (
                nc.sbuf_tensor("kth0", [128, T], BF16) as kth0,
                nc.sbuf_tensor("kth1", [128, T], BF16) as kth1,
                nc.sbuf_tensor("vh0", [128, T], BF16) as vh0,
                nc.sbuf_tensor("vh1", [128, T], BF16) as vh1,
                nc.sbuf_tensor("qh0", [128, TOWN], BF16) as qh0,
                nc.sbuf_tensor("qh1", [128, TOWN], BF16) as qh1,
                nc.sbuf_tensor("ptil", [128, 4 * 512], BF16) as ptil,
                nc.sbuf_tensor("bb16", [128, 512], F32) as bb16t,
                nc.sbuf_tensor("rden", [1, 512], F32) as rden,
            ):
                kth, vhh, qhh = [kth0, kth1], [vh0, vh1], [qh0, qh1]
                tc0, sc0, vc0 = TC.n, SC.n, VC.n
                kh0 = [KH[0].n, KH[1].n]
                ld_head = {h: kh0[h % 2] + 16 * 34 * (h // 2 + 1)
                           for h in range(NH)}

                VIS = [{qi: list(range(4 * qi + 4)) + list(range(16, 32))
                        for qi in range(4)},
                       {qi: list(range(4 * qi + 4)) for qi in range(4)}]
                N0 = {qi: 4 * qi + 20 for qi in range(4)}
                qk_t, pv_t, bc_t, sc_e, sc_rb = {}, {}, {}, {}, {}
                tcc, scc = tc0, sc0
                for hq in range(64):
                    qi = hq % 4
                    n = N0[qi]
                    for p in range(n):
                        tcc += 1
                        qk_t[(hq, p)] = tcc
                        if p >= 1:
                            tcc += 1
                            pv_t[(hq, p - 1)] = tcc
                    tcc += 1
                    pv_t[(hq, n - 1)] = tcc
                    tcc += 1
                    bc_t[hq] = tcc
                    for p in range(n):
                        scc += 1
                        sc_e[(hq, p)] = scc
                    sc_rb[hq] = scc + 2
                    scc += 2
                p3_tc_end, p3_sc_end = tcc, scc
                SCH = [dict(vis=VIS[r],
                            nvis={qi: len(VIS[r][qi]) for qi in range(4)},
                            qk=qk_t, pv=pv_t, bc=bc_t, se=sc_e, srb=sc_rb,
                            tc_end=p3_tc_end, sc_end=p3_sc_end)
                       for r in range(2)]
                vc_mask = lambda hq, m: vc0 + hq * 7 + m + 1
                vc_den = lambda hq: vc0 + hq * 7 + 5
                vc_bb = lambda hq: vc0 + hq * 7 + 6
                vc_ao = lambda hq: vc0 + hq * 7 + 7
                vc_end = vc0 + 64 * 7

                @block.sync
                def _(sy):
                    wait_stores(sy)
                    for case in sy.Switch(index=sy.snap(ROLE["sy"]), n=2):
                        S = SCH[case]
                        for h in range(NH):
                            if h >= 2:
                                sy.wait_ge(TC.h, S["bc"][(h - 1) * 4 - 1])
                            i = sy.dma_start(out=kth[h % 2][:],
                                             in_=kT_sc[h * 128:(h + 1) * 128, :])
                            kk = KH[h % 2].inc(i, 16)
                            for j in range(32):
                                i = sy.dma_start(
                                    out=vhh[h % 2][:, j * 128:(j + 1) * 128],
                                    in_=vT_sc[h * 128:(h + 1) * 128,
                                              j * 128:(j + 1) * 128],
                                    transpose=True)
                                kk = KH[h % 2].inc(i, 16)
                            i = sy.dma_start(out=qhh[h % 2][:],
                                             in_=qT_sc[h * 128:(h + 1) * 128, :])
                            kk = KH[h % 2].inc(i, 16)
                        if case == 0:
                            kh_fin = [KH[0].n, KH[1].n]
                        else:
                            KH[0].n, KH[1].n = kh_fin

                def emit_pv(t, S, h, qi, p, hq):
                    j = S["vis"][qi][p]
                    n = S["nvis"][qi]
                    m = j - 4 * qi
                    if 0 <= m <= 3:
                        t.wait_ge(VC.h, vc_mask(hq, m))
                    else:
                        t.wait_ge(SC.h, S["se"][(hq, p)])
                    if p == 0:
                        if hq >= 1:
                            t.wait_ge(VC.h, vc_den(hq - 1))
                        if hq >= 2:
                            t.wait_ge(VC.h, vc_ao(hq - 2))
                    pt = ptil[:, (p % 4) * 512:(p % 4 + 1) * 512]
                    po = p512[2 + (qi % 2)]
                    lhv = vhh[h % 2][:, j * 128:(j + 1) * 128]
                    t.ldweights(lhv)
                    t.matmul(po[:], lhv, pt, start=(p == 0), stop=(p == n - 1))
                    t.ldweights(SMB[:, ONES_C:ONES_C + 1])
                    i = t.matmul(pD[0:1, :], SMB[:, ONES_C:ONES_C + 1], pt,
                                 start=(p == 0), stop=(p == n - 1))
                    TC.inc(i)

                @block.tensor
                def _(t):
                    tc_save = TC.n
                    for case in t.Switch(index=t.snap(ROLE["t"]), n=2):
                        S = SCH[case]
                        TC.n = tc_save
                        for h in range(NH):
                            t.wait_ge(KH[h % 2].h, ld_head[h])
                            for qi in range(4):
                                hq = h * 4 + qi
                                n = S["nvis"][qi]
                                npr = S["nvis"][(qi - 1) % 4] if hq else 0
                                for p in range(n):
                                    j = S["vis"][qi][p]
                                    sp = p512[p % 2]
                                    if p >= 2:
                                        t.wait_ge(SC.h, S["se"][(hq, p - 2)])
                                    elif hq > 0:
                                        t.wait_ge(SC.h,
                                                  S["se"][(hq - 1, npr - 2 + p)])
                                    lhk = kth[h % 2][:, j * 128:(j + 1) * 128]
                                    t.ldweights(lhk)
                                    i = t.matmul(
                                        sp[:], lhk,
                                        qhh[h % 2][:, qi * 512:(qi + 1) * 512],
                                        start=True, stop=True)
                                    TC.inc(i)
                                    assert TC.n == S["qk"][(hq, p)]
                                    if p >= 1:
                                        emit_pv(t, S, h, qi, p - 1, hq)
                                emit_pv(t, S, h, qi, n - 1, hq)
                                if n < N0[qi]:
                                    t.wait_ge(TC.h, TC.n)
                                    t.nop().then_inc(TC.h, 2 * (N0[qi] - n))
                                    TC.n += 2 * (N0[qi] - n)
                                t.wait_ge(SC.h, S["srb"][hq])
                                if hq >= 1:
                                    t.wait_ge(VC.h, vc_bb(hq - 1))
                                t.ldweights(SMB[0:1, ONESR_C:ONESR_C + 128])
                                i = t.matmul(pB[:],
                                             SMB[0:1, ONESR_C:ONESR_C + 128],
                                             SMB[0:1, RB_C:RB_C + 512],
                                             start=True, stop=True)
                                TC.inc(i)
                                assert TC.n == S["bc"][hq]
                        assert TC.n == p3_tc_end

                @block.scalar
                def _(s):
                    sc_save = SC.n
                    ss_save = SSs.n
                    for case in s.Switch(index=s.snap(ROLE["s"]), n=2):
                        S = SCH[case]
                        SC.n = sc_save
                        SSs.n = ss_save
                        for h in range(NH):
                            for qi in range(4):
                                hq = h * 4 + qi
                                n = S["nvis"][qi]
                                npr = S["nvis"][(qi - 1) % 4] if hq else 0
                                for p in range(n):
                                    j = S["vis"][qi][p]
                                    s.wait_ge(TC.h, S["qk"][(hq, p)])
                                    if p >= 4:
                                        s.wait_ge(TC.h, S["pv"][(hq, p - 4)])
                                    elif hq > 0:
                                        s.wait_ge(TC.h,
                                                  S["pv"][(hq - 1, npr - 4 + p)])
                                    i = s.activation(
                                        ptil[:, (p % 4) * 512:(p % 4 + 1) * 512],
                                        p512[p % 2][:], AF.Exp,
                                        bias=SM[:, CB_C + qi * 32 + j:
                                                CB_C + qi * 32 + j + 1],
                                        scale=RSQ)
                                    SC.inc(i)
                                if n < N0[qi]:
                                    s.wait_ge(SC.h, SC.n)
                                    s.nop().then_inc(SC.h, N0[qi] - n)
                                    SC.n += N0[qi] - n
                                s.wait_ge(VC.h, vc_den(hq))
                                i = s.activation(rden[0:1, :],
                                                 SM[0:1, DEN_C:DEN_C + 512],
                                                 AF.Ln)
                                SSs.inc(i)
                                s.wait_ge(SSs.h, SSs.n)
                                i = s.activation(rden[0:1, :], rden[0:1, :],
                                                 AF.Exp, scale=-1.0)
                                SC.inc(i)
                                s.wait_ge(SC.h, SC.n)
                                i = s.copy(SMB[0:1, RB_C:RB_C + 512],
                                           rden[0:1, :])
                                SC.inc(i)
                        assert SC.n == p3_sc_end
                        assert SSs.n == ss_save + 64

                @block.vector
                def _(v):
                    vc_save = VC.n
                    for case in v.Switch(index=v.snap(ROLE["v"]), n=2):
                        S = SCH[case]
                        VC.n = vc_save
                        for h in range(NH):
                            for qi in range(4):
                                hq = h * 4 + qi
                                n = S["nvis"][qi]
                                for m in range(4):
                                    p = 4 * qi + m
                                    v.wait_ge(SC.h, S["se"][(hq, p)])
                                    pt = ptil[:, (p % 4) * 512:(p % 4 + 1) * 512]
                                    i = v.tensor_mul(pt, pt,
                                                     SMB[:, MK0_C + m * 512:
                                                         MK0_C + (m + 1) * 512])
                                    VC.inc(i)
                                v.wait_ge(TC.h, S["pv"][(hq, n - 1)])
                                i = v.tensor_copy(SM[0:1, DEN_C:DEN_C + 512],
                                                  pD[0:1, :])
                                VC.inc(i)
                                v.wait_ge(TC.h, S["bc"][hq])
                                i = v.tensor_copy(bb16t[:], pB[:])
                                VC.inc(i)
                                v.wait_ge(VC.h, VC.n)
                                i = v.tensor_tensor(
                                    A[:, h * TOWN + qi * 512:
                                      h * TOWN + (qi + 1) * 512],
                                    p512[2 + (qi % 2)][:], bb16t[:], ALU.mult)
                                VC.inc(i)
                        VC.n = vc_end
        p3_vc = VC.n
        p3_wt = WB.n

        if emit_l0 and phase_hi >= 4:
            tc0, vc0 = TC.n, VC.n
            se0 = [sm.n for sm in SE]

            @block.tensor
            def _(t):
                t.wait_ge(WB.h, p3_wt)
                t.wait_ge(VC.h, p3_vc)
                for tb in range(4):
                    for oc in range(NDC):
                        k = tb * 16 + oc
                        pp = p512[k % 4]
                        if k >= 4:
                            t.wait_ge(VC.h, vc0 + k - 3)
                        for dc in range(NDC):
                            lh = Bb[:, (dc * NDC + oc) * 128:
                                    (dc * NDC + oc) * 128 + 128]
                            t.ldweights(lh)
                            i = t.matmul(pp[:], lh,
                                         A[:, dc * TOWN + tb * 512:
                                           dc * TOWN + (tb + 1) * 512],
                                         start=(dc == 0), stop=(dc == NDC - 1))
                        TC.inc(i)

            @block.vector
            def _(v):
                for k in range(64):
                    v.wait_ge(TC.h, tc0 + k + 1)
                    tgt = se0[k % 4] + 16 * (k // 4)
                    if tgt:
                        v.wait_ge(SE[k % 4].h, tgt)
                    i = v.tensor_copy(EV[:, (k % 4) * 512:(k % 4 + 1) * 512],
                                      p512[k % 4][:])
                    VC.inc(i)

            @block.gpsimd
            def _(g):
                for tb in range(4):
                    for oc in range(NDC):
                        k = tb * 16 + oc
                        g.wait_ge(VC.h, vc0 + k + 1)
                        i = g.dma_start(out=sT_sc[oc * 128:(oc + 1) * 128,
                                                  tb * 512:(tb + 1) * 512],
                                        in_=EV[:, (k % 4) * 512:(k % 4 + 1) * 512])
                        SE[k % 4].inc(i, 16)

        if emit_l0 and phase_hi >= 5:
            wg = [Bb[:, 0:8192], Bb[:, 8192:16384]]
            wu = [Bb[:, 16384:24576], Bb[:, 24576:32768]]
            wdl = [Bb[:, 0:8192], Bb[:, 8192:16384]]
            _c56 = ExitStack()
            with _c56:
                C = _c56.enter_context(nc.sbuf_tensor("rhsC5", [128, NDC * 512], BF16))
                gt = _c56.enter_context(nc.sbuf_tensor("gt5", [128, 2 * 512], BF16))
                ut = _c56.enter_context(nc.sbuf_tensor("ut5", [128, 2 * 512], BF16))
                stl = _c56.enter_context(nc.sbuf_tensor("stl5", [128, 2 * 512], BF16))
                dtl = _c56.enter_context(nc.sbuf_tensor("dtl5", [128, 2 * 512], BF16))
                p6_sd_at = []
                for tq in range(4):
                    tc0, sc0, vc0 = TC.n, SC.n, VC.n
                    lr00 = LR[0].n
                    ld_rhs = lr00 + 16 * NDC
                    ls0 = [LS[0].n, LS[1].n]
                    ld_st = lambda oc: ls0[oc % 2] + 16 * (oc // 2 + 1)
                    wg0 = [WG[0].n, WG[1].n]
                    wu0 = [WU[0].n, WU[1].n]
                    wt_ib_g = lambda ib: wg0[ib % 2] + 16 * (ib // 2 + 1)
                    wt_ib_u = lambda ib: wu0[ib % 2] + 16 * (ib // 2 + 1)
                    wt_oc = lambda oc: wg0[oc % 2] + 16 * (8 + oc // 2 + 1)
                    tc_gu = lambda ic, gu: tc0 + ic * 2 + gu + 1
                    tc_oc = lambda oc: tc0 + 128 + oc + 1
                    sc_gu = lambda ic, gu: sc0 + ic * 2 + gu + 1
                    sc_oc = lambda oc: sc0 + 128 + oc + 1
                    vc_f = lambda ic: vc0 + ic + 1
                    vc_oc = lambda oc: vc0 + 64 + oc + 1
                    sd0 = [SD[0].n, SD[1].n]
                    st_oc = lambda oc: sd0[oc % 2] + 16 * (oc // 2 + 1)

                    @block.sync
                    def _(sy, tq=tq, tc0=tc0, vc0=vc0):
                        if tq == 0:
                            wait_stores(sy)
                        else:
                            sy.wait_ge(TC.h, tc0)
                        for dc in range(NDC):
                            i = sy.dma_start(
                                out=C[:, dc * 512:(dc + 1) * 512],
                                in_=xn_tm[tq * 512:(tq + 1) * 512,
                                          dc * 128:(dc + 1) * 128],
                                transpose=True)
                            LR[0].inc(i, 16)
                        for oc in range(NDC):
                            if oc >= 2:
                                sy.wait_ge(VC.h, vc0 + 64 + oc - 1)
                            i = sy.dma_start(
                                out=stl[:, (oc % 2) * 512:(oc % 2) * 512 + 512],
                                in_=sT_sc[oc * 128:(oc + 1) * 128,
                                          tq * 512:(tq + 1) * 512])
                            LS[oc % 2].inc(i, 16)
                            if oc >= 2:
                                po = oc - 2
                                i = sy.dma_start(
                                    out=dT_sc[po * 128:(po + 1) * 128,
                                              tq * 512:(tq + 1) * 512],
                                    in_=dtl[:, (po % 2) * 512:(po % 2) * 512 + 512])
                                SD[po % 2].inc(i, 16)
                        for po in (14, 15):
                            sy.wait_ge(VC.h, vc0 + 64 + po + 1)
                            i = sy.dma_start(
                                out=dT_sc[po * 128:(po + 1) * 128,
                                          tq * 512:(tq + 1) * 512],
                                in_=dtl[:, (po % 2) * 512:(po % 2) * 512 + 512])
                            SD[po % 2].inc(i, 16)
                    p6_sd_at.append((SD[0].n, SD[1].n))

                    @block.gpsimd
                    def _(g, tq=tq, tc0=tc0):
                        for ib in range(16):
                            if ib >= 2:
                                g.wait_ge(TC.h, tc0 + (ib - 1) * 8)
                            elif tc0:
                                g.wait_ge(TC.h, tc0)
                            i = g.dma_start(out=wg[ib % 2],
                                            in_=d_wg[:, ib * 8192:(ib + 1) * 8192])
                            WG[ib % 2].inc(i, 16)
                            i = g.dma_start(out=wu[ib % 2],
                                            in_=d_wu[:, ib * 8192:(ib + 1) * 8192])
                            WU[ib % 2].inc(i, 16)
                        for oc in range(16):
                            if oc < 2:
                                g.wait_ge(TC.h, tc0 + 128)
                            else:
                                g.wait_ge(TC.h, tc0 + 128 + oc - 1)
                            i = g.dma_start(out=wdl[oc % 2],
                                            in_=d_wd[:, oc * 8192:(oc + 1) * 8192])
                            WG[oc % 2].inc(i, 16)

                    @block.tensor
                    def _(t, tq=tq, ld_rhs=ld_rhs, wt_ib_g=wt_ib_g,
                          wt_ib_u=wt_ib_u, wt_oc=wt_oc,
                          sc_gu=sc_gu, sc_oc=sc_oc, vc_f=vc_f):
                        t.wait_ge(LR[0].h, ld_rhs)
                        for ib in range(16):
                            t.wait_ge(WG[ib % 2].h, wt_ib_g(ib))
                            t.wait_ge(WU[ib % 2].h, wt_ib_u(ib))
                            for icc in range(4):
                                ic = ib * 4 + icc
                                pg = p512[ic % 2]
                                pu = p512[2 + ic % 2]
                                if ic >= 2:
                                    t.wait_ge(SC.h, sc_gu(ic - 2, 1))
                                for dc in range(NDC):
                                    lh = wg[ib % 2][:, dc * 512 + icc * 128:
                                                    dc * 512 + icc * 128 + 128]
                                    t.ldweights(lh)
                                    i = t.matmul(pg[:], lh,
                                                 C[:, dc * 512:(dc + 1) * 512],
                                                 start=(dc == 0),
                                                 stop=(dc == NDC - 1))
                                TC.inc(i)
                                for dc in range(NDC):
                                    lh = wu[ib % 2][:, dc * 512 + icc * 128:
                                                    dc * 512 + icc * 128 + 128]
                                    t.ldweights(lh)
                                    i = t.matmul(pu[:], lh,
                                                 C[:, dc * 512:(dc + 1) * 512],
                                                 start=(dc == 0),
                                                 stop=(dc == NDC - 1))
                                TC.inc(i)
                        for oc in range(16):
                            t.wait_ge(WG[oc % 2].h, wt_oc(oc))
                            if oc == 0:
                                t.wait_ge(VC.h, vc_f(63))
                            py = p512[4 + oc % 2]
                            if oc >= 2:
                                t.wait_ge(SC.h, sc_oc(oc - 2))
                            for ic in range(NIC):
                                lh = wdl[oc % 2][:, ic * 128:(ic + 1) * 128]
                                t.ldweights(lh)
                                i = t.matmul(py[:], lh, A[:, ic * 512:(ic + 1) * 512],
                                             start=(ic == 0), stop=(ic == NIC - 1))
                            TC.inc(i)

                    @block.scalar
                    def _(s, tq=tq, tc_gu=tc_gu, tc_oc=tc_oc, vc_f=vc_f,
                          vc_oc=vc_oc):
                        for ic in range(NIC):
                            s.wait_ge(TC.h, tc_gu(ic, 0))
                            s.wait_ge(VC.h, vc_f(ic - 2) if ic >= 2 else vc0)
                            i = s.activation(gt[:, (ic % 2) * 512:(ic % 2) * 512 + 512],
                                             p512[ic % 2][:], AF.Silu,
                                             scale=SM[:, SG_C + ic:SG_C + ic + 1])
                            SC.inc(i)
                            s.wait_ge(TC.h, tc_gu(ic, 1))
                            i = s.activation(ut[:, (ic % 2) * 512:(ic % 2) * 512 + 512],
                                             p512[2 + ic % 2][:], AF.Copy,
                                             scale=SM[:, SU_C + ic:SU_C + ic + 1])
                            SC.inc(i)
                        for oc in range(16):
                            s.wait_ge(TC.h, tc_oc(oc))
                            if oc >= 2:
                                s.wait_ge(VC.h, vc_oc(oc - 2))
                            i = s.activation(gt[:, (oc % 2) * 512:(oc % 2) * 512 + 512],
                                             p512[4 + oc % 2][:], AF.Silu,
                                             scale=SM[:, SD_C + oc:SD_C + oc + 1])
                            SC.inc(i)

                    @block.vector
                    def _(v, tq=tq, sc_gu=sc_gu, sc_oc=sc_oc, ld_st=ld_st,
                          st_oc=st_oc):
                        for ic in range(NIC):
                            v.wait_ge(SC.h, sc_gu(ic, 1))
                            i = v.tensor_mul(A[:, ic * 512:(ic + 1) * 512],
                                             gt[:, (ic % 2) * 512:(ic % 2) * 512 + 512],
                                             ut[:, (ic % 2) * 512:(ic % 2) * 512 + 512])
                            VC.inc(i)
                        for oc in range(16):
                            v.wait_ge(SC.h, sc_oc(oc))
                            v.wait_ge(LS[oc % 2].h, ld_st(oc))
                            tgt = st_oc(oc - 2) if oc >= 2 else sd0[oc % 2]
                            if tgt:
                                v.wait_ge(SD[oc % 2].h, tgt)
                            i = v.tensor_mul(dtl[:, (oc % 2) * 512:(oc % 2) * 512 + 512],
                                             gt[:, (oc % 2) * 512:(oc % 2) * 512 + 512],
                                             stl[:, (oc % 2) * 512:(oc % 2) * 512 + 512])
                            VC.inc(i)


                if phase_hi >= 6:
                    xf6a = _c56.enter_context(nc.sbuf_tensor("xf6a", [128, D], F32))
                    xf6b = _c56.enter_context(nc.sbuf_tensor("xf6b", [128, D], F32))
                    dl6a = _c56.enter_context(nc.sbuf_tensor("dl6a", [128, D], BF16))
                    dl6b = _c56.enter_context(nc.sbuf_tensor("dl6b", [128, D], BF16))
                    xn6a = _c56.enter_context(nc.sbuf_tensor("xn6a", [128, D], BF16))
                    xn6b = _c56.enter_context(nc.sbuf_tensor("xn6b", [128, D], BF16))
                    dum6 = _c56.enter_context(nc.sbuf_tensor("dum6", [128, D], BF16))
                    bnd = _c56.enter_context(nc.sbuf_tensor("bnd", [128, 16], BF16))
                    xf6, dl6 = [xf6a, xf6b], [dl6a, dl6b]
                    vc0 = VC.n
                    lx0 = [LX[0].n, LX[1].n]
                    ld_at6 = {tt: lx0[tt % 2] + 80 * (tt // 2 + 1)
                              for tt in range(16)}
                    vc_pre6 = lambda tt: vc0 + 3 * tt + 1

                    def pre_vec(v, tt):
                        v.wait_ge(LX[tt % 2].h, ld_at6[tt])
                        i = v.tensor_tensor(xf6[tt % 2][:], xf6[tt % 2][:],
                                            dl6[tt % 2][:], ALU.add)
                        VC.inc(i)

                    store_tile = emit_ln(
                        16, None, xn1_tm, 1, xf6, [xn6a, xn6b], (dum6, dum6),
                        pre_vec=pre_vec,
                        store_xrow=lambda tt: x1_tm[tt * 128:(tt + 1) * 128, :])

                    @block.sync
                    def _(sy):
                        for tt in range(17):
                            if tt < 16:
                                tq = tt // 4
                                sy.wait_ge(SD[0].h, p6_sd_at[tq][0])
                                sy.wait_ge(SD[1].h, p6_sd_at[tq][1])
                                if tt >= 2:
                                    sy.wait_ge(VC.h, vc_pre6(tt - 2))
                                    sy.wait_ge(store_tile.slots[tt % 2].h,
                                               store_tile.st_at[tt - 2])
                                i = sy.dma_start(out=xf6[tt % 2][:],
                                                 in_=d_x[tt * 128:(tt + 1) * 128, :])
                                LX[tt % 2].inc(i, 16)
                                for gq in range(4):
                                    i = sy.dma_start(
                                        out=dl6[tt % 2][:, gq * 512:(gq + 1) * 512],
                                        in_=dT_sc[gq * 512:(gq + 1) * 512,
                                                  tt * 128:(tt + 1) * 128],
                                        transpose=True)
                                    LX[tt % 2].inc(i, 16)
                            if tt >= 1:
                                store_tile(sy, tt - 1)

                    @block.sync
                    def _(sy):
                        wait_stores(sy)
                        i = sy.dma_start(
                            out=bnd[:],
                            in_=xn1_tm[TOWN:TOWN + 1, :].rearrange(
                                "one (p c) -> (one p) c", p=128))
                        LSM.inc(i, 16)
                    ld_bnd = LSM.n
                    vc0b = VC.n

                    @block.vector
                    def _(v):
                        v.wait_ge(LSM.h, ld_bnd)
                        i = v.tensor_copy(SM[:, BND_C:BND_C + 16], bnd[:])
                        vedge(v, i)
                        for bp in range(4):
                            i = v.tensor_scalar(
                                SM[:, CCS_C + bp * 16:CCS_C + (bp + 1) * 16],
                                SM[:, BND_C:BND_C + 16],
                                SM[:, WSEL_C + bp:WSEL_C + bp + 1], None, ALU.mult)
                        VC.inc(i)
                    vc_ccs = VC.n

                    @block.gpsimd
                    def _(g):
                        g.wait_ge(VC.h, vc_ccs)
                        i = g.dma_start(out=cc1_in, in_=SM[:, CCS_C:CCS_C + 64])
                        CCD.inc(i, 16)
                        g.wait_ge(CCD.h, CCD.n)
                        i = g.collective_compute(
                            "AllReduce", ALU.add,
                            replica_groups=[[0]] if sim1 else [list(range(8))],
                            ins=[cc1_in.opt()], outs=[cc1_out.opt()])
                        CCA.inc(i, 1)
                        g.wait_ge(CCA.h, 1)
                        i = g.dma_start(out=SM[:, RX_C:RX_C + 64], in_=cc1_out)
                        CCD.inc(i, 16)
                    gc_rx1 = CCD.n

                    @block.vector
                    def _(v):
                        v.wait_ge(CCD.h, gc_rx1)
                        i = v.tensor_scalar(SM[:, ACC_C:ACC_C + 16],
                                            SM[:, RX_C:RX_C + 16],
                                            SM[:, RSEL_C:RSEL_C + 1], None, ALU.mult)
                        vedge(v, i)
                        for bp in range(1, 4):
                            i = v.scalar_tensor_tensor(
                                SM[:, ACC_C:ACC_C + 16],
                                SM[:, RX_C + bp * 16:RX_C + (bp + 1) * 16],
                                SM[:, RSEL_C + bp:RSEL_C + bp + 1],
                                SM[:, ACC_C:ACC_C + 16], ALU.mult, ALU.add)
                            vedge(v, i)
                        i = v.tensor_scalar(bnd[:], SM[:, ACC_C:ACC_C + 16],
                                            SM[:, FLAG_C:FLAG_C + 1], None, ALU.mult)
                        VC.inc(i)
                    vc_bnd = VC.n

                    @block.sync
                    def _(sy):
                        sy.wait_ge(VC.h, vc_bnd)
                        i = sy.dma_start(
                            out=xn1_tm[0:1, :].rearrange("one (p c) -> (one p) c",
                                                         p=128),
                            in_=bnd[:])
                        SX[0].inc(i, 16)
        p6_bnd_st = SX[0].n if (emit_l0 and phase_hi >= 6) else 0

        if emit_l1 and phase_hi >= 7:
            with (
                nc.sbuf_tensor("rhsC7", [128, NDC * 512], BF16) as C,
                nc.sbuf_tensor("rhsD7", [128, NDC * 512], BF16) as Dd,
                nc.sbuf_tensor("abuf", [128, NDC * 128], BF16) as ABUF,
                nc.sbuf_tensor("bwb", [128, D], BF16) as BWB,
                nc.sbuf_tensor("hT", [128, TOWN], BF16) as hT,
                nc.sbuf_tensor("lgt", [128, 8], F32) as lgt,
                nc.sbuf_tensor("mxc", [128, 1], F32) as mxc,
                nc.sbuf_tensor("mk8", [128, 8], BF16) as mk8,
                nc.sbuf_tensor("mkT", [128, 512], BF16) as mkT,
                nc.sbuf_tensor("g1t", [128, 2 * 512], BF16) as g1t,
            ):
                @block.gpsimd
                def _(g):
                    tcn = TC.n
                    if tcn:
                        g.wait_ge(TC.h, tcn)
                    i = g.dma_start(out=ABUF[:], in_=d_Af)
                    WSM.inc(i, 16)
                    i = g.dma_start(out=BWB[:], in_=d_Bw)
                    WSM.inc(i, 16)
                wsm_rdy = WSM.n

                rhs_shift = lambda tb, dc: xn1_tm[tb * 512:(tb + 1) * 512,
                                                  dc * 128:(dc + 1) * 128]
                rhs_plain = lambda tb, dc: xn1_tm[1 + tb * 512:1 + (tb + 1) * 512,
                                                  dc * 128:(dc + 1) * 128]

                p6_gate = lambda tb: [(SX[0], p6_bnd_st)]
                wk1_rdy = load_weight(A, d_wkT1, WA)
                emit_proj(A, WA, wk1_rdy, rhs_shift, 4,
                          lambda oc, tb: kT1_sc[oc * 128:(oc + 1) * 128,
                                                tb * 512:(tb + 1) * 512], (C, Dd),
                          rhs_gate=p6_gate)
                wv1_rdy = load_weight(Bb, d_wvT1, WB)
                emit_proj(Bb, WB, wv1_rdy, rhs_shift, 4,
                          lambda oc, tb: vT1_sc[oc * 128:(oc + 1) * 128,
                                                tb * 512:(tb + 1) * 512], (C, Dd),
                          rhs_gate=p6_gate)
                wr1_rdy = load_weight(A, d_wrT1, WA)
                emit_proj(A, WA, wr1_rdy, rhs_plain, 4,
                          lambda oc, tb: rT1_sc[oc * 128:(oc + 1) * 128,
                                                tb * 512:(tb + 1) * 512], (C, Dd))
                wb_rdy = load_weight(Bb, d_wbT, WB)

                @block.vector
                def _(v):
                    for sm in SE:
                        if sm.n:
                            v.wait_ge(sm.h, sm.n)
                    i = v.memset(EV[:], 0.0)
                    VC.inc(i)
                vc_z = VC.n

                @block.gpsimd
                def _(g):
                    g.wait_ge(VC.h, vc_z)
                    i = g.dma_start(
                        out=mk_sc.rearrange("(a p) c -> p a c", p=128),
                        in_=EV.rearrange("p (a c) -> p a c", c=128))
                    SMK.inc(i, 16)

                prev_vh = None
                prev_vhm = None
                prev_tse = None
                for tb in range(4):
                    tc0, vc0, sc0 = TC.n, VC.n, SC.n
                    rb = C if tb % 2 == 0 else Dd
                    t_h = tc0 + 1
                    t_lg = lambda tc4: tc0 + 2 + tc4
                    t_se = tc0 + 6
                    t_base = lambda oc: tc0 + 6 + oc + 1
                    v_h = vc0 + 1
                    v_mk = lambda tc4: vc0 + 2 + tc4
                    v_hm = vc0 + 6
                    smk0 = SMK.n
                    st_mk = lambda tc4: smk0 + 16 * (tc4 + 1)
                    sg0 = [SG[0].n, SG[1].n]
                    st_g1 = lambda oc: sg0[oc % 2] + 16 * (oc // 2 + 1)
                    sc_g1 = lambda oc: sc0 + oc + 1

                    @block.sync
                    def _(sy, tb=tb, rb=rb, tc0=tc0):
                        sy.wait_ge(TC.h, tc0)
                        for dc in range(NDC):
                            i = sy.dma_start(out=rb[:, dc * 512:(dc + 1) * 512],
                                             in_=rhs_plain(tb, dc), transpose=True)
                            LR[tb % 2].inc(i, 16)
                    ld_rhs = LR[tb % 2].n

                    @block.sync
                    def _(sy, tb=tb, st_mk=st_mk, prev_tse=prev_tse):
                        sy.wait_ge(SMK.h, st_mk(3))
                        if prev_tse is not None:
                            sy.wait_ge(TC.h, prev_tse)
                        i = sy.dma_start(out=mkT[:],
                                         in_=mk_sc[tb * 512:(tb + 1) * 512, :],
                                         transpose=True)
                        LMK.inc(i, 16)
                    ld_mk = LMK.n

                    @block.tensor
                    def _(t, tb=tb, rb=rb, ld_rhs=ld_rhs, ld_mk=ld_mk, tc0=tc0,
                          v_mk=v_mk, v_hm=v_hm, sc_g1=sc_g1,
                          prev_vh=prev_vh, prev_vhm=prev_vhm):
                        t.wait_ge(LR[tb % 2].h, ld_rhs)
                        t.wait_ge(WB.h, wb_rdy)
                        t.wait_ge(WSM.h, wsm_rdy)
                        if prev_vh is not None:
                            t.wait_ge(VC.h, prev_vh)
                        for dc in range(NDC):
                            lh = ABUF[:, dc * 128:(dc + 1) * 128]
                            t.ldweights(lh)
                            i = t.matmul(p512[0][:], lh,
                                         rb[:, dc * 512:(dc + 1) * 512],
                                         start=(dc == 0), stop=(dc == NDC - 1))
                        TC.inc(i)
                        for tc4 in range(4):
                            if tc4 >= 1:
                                t.wait_ge(VC.h, v_mk(tc4 - 1))
                            for dc in range(NDC):
                                lh = rb[:, dc * 512 + tc4 * 128:
                                        dc * 512 + tc4 * 128 + 128]
                                t.ldweights(lh)
                                i = t.matmul(p512[1][0:128, 0:8], lh,
                                             RWB[:, dc * 8:(dc + 1) * 8],
                                             start=(dc == 0), stop=(dc == NDC - 1))
                            TC.inc(i)
                        t.wait_ge(LMK.h, ld_mk)
                        if prev_vhm is not None:
                            t.wait_ge(VC.h, prev_vhm)
                        t.ldweights(R8[0:8, 0:128])
                        i = t.matmul(p512[2][:], R8[0:8, 0:128], mkT[0:8, :],
                                     start=True, stop=True)
                        TC.inc(i)
                        t.wait_ge(VC.h, v_hm)
                        for oc in range(NDC):
                            po = p512[3 + oc % 2]
                            if oc >= 2:
                                t.wait_ge(SC.h, sc_g1(oc - 2))
                            for dc in range(NDC):
                                lh = Bb[:, (dc * NDC + oc) * 128:
                                        (dc * NDC + oc) * 128 + 128]
                                t.ldweights(lh)
                                t.matmul(po[:], lh, rb[:, dc * 512:(dc + 1) * 512],
                                         start=(dc == 0), stop=False)
                            lh = BWB[:, oc * 128:(oc + 1) * 128]
                            t.ldweights(lh)
                            i = t.matmul(po[:], lh, hT[:, tb * 512:(tb + 1) * 512],
                                         start=False, stop=True)
                            TC.inc(i)

                    @block.vector
                    def _(v, tb=tb, tc0=tc0, t_h=t_h, t_lg=t_lg, t_se=t_se,
                          st_mk=st_mk):
                        v.wait_ge(TC.h, t_h)
                        i = v.tensor_copy(hT[:, tb * 512:(tb + 1) * 512],
                                          p512[0][:])
                        VC.inc(i)
                        for tc4 in range(4):
                            v.wait_ge(TC.h, t_lg(tc4))
                            tgt = st_mk(tc4 - 1) if tc4 >= 1 else smk0
                            if tgt:
                                v.wait_ge(SMK.h, tgt)
                            i = v.tensor_copy(lgt[:], p512[1][0:128, 0:8])
                            vedge(v, i)
                            i = v.reduce_max(mxc[:], lgt[:], axis=AXX)
                            vedge(v, i)
                            i = v.tensor_single_scalar(mk8[:], lgt[:],
                                                       mxc[:, 0:1], ALU.is_ge)
                            VC.inc(i)
                        v.wait_ge(TC.h, t_se)
                        i = v.tensor_tensor(hT[:, tb * 512:(tb + 1) * 512],
                                            hT[:, tb * 512:(tb + 1) * 512],
                                            p512[2][:], ALU.mult)
                        VC.inc(i)

                    @block.scalar
                    def _(s, tb=tb, t_base=t_base, st_g1=st_g1):
                        for oc in range(NDC):
                            s.wait_ge(TC.h, t_base(oc))
                            tgt = st_g1(oc - 2) if oc >= 2 else sg0[oc % 2]
                            if tgt:
                                s.wait_ge(SG[oc % 2].h, tgt)
                            i = s.activation(
                                g1t[:, (oc % 2) * 512:(oc % 2) * 512 + 512],
                                p512[3 + oc % 2][:], AF.Silu,
                                bias=SM[:, BB_C + oc:BB_C + oc + 1])
                            SC.inc(i)

                    @block.gpsimd
                    def _(g, tb=tb, v_mk=v_mk, sc_g1=sc_g1):
                        for tc4 in range(4):
                            g.wait_ge(VC.h, v_mk(tc4))
                            i = g.dma_start(
                                out=mk_sc[tb * 512 + tc4 * 128:
                                          tb * 512 + (tc4 + 1) * 128, 0:8],
                                in_=mk8[:])
                            SMK.inc(i, 16)
                        for oc in range(NDC):
                            g.wait_ge(SC.h, sc_g1(oc))
                            i = g.dma_start(
                                out=g1T_sc[oc * 128:(oc + 1) * 128,
                                           tb * 512:(tb + 1) * 512],
                                in_=g1t[:, (oc % 2) * 512:(oc % 2) * 512 + 512])
                            SG[oc % 2].inc(i, 16)

                    prev_vh = v_h
                    prev_vhm = v_hm
                    prev_tse = t_se

        if emit_l1 and phase_hi >= 8:
            HT = TOWN // 2
            with (
                nc.sbuf_tensor("ktl", [128, TOWN], BF16) as ktl,
                nc.sbuf_tensor("vtl", [128, TOWN], BF16) as vtl,
                nc.sbuf_tensor("rtl", [128, TOWN], BF16) as rtl,
                nc.sbuf_tensor("gtl", [128, TOWN], BF16) as gtl,
                nc.sbuf_tensor("ekf", [128, HT], F32) as ekf,
                nc.sbuf_tensor("avf", [128, HT], F32) as avf,
                nc.sbuf_tensor("n0f", [128, HT], F32) as n0f,
                nc.sbuf_tensor("d0f", [128, HT], F32) as d0f,
                nc.sbuf_tensor("wrp", [128, HT], F32) as wrp,
                nc.sbuf_tensor("sth", [128, 4], F32) as sth,
                nc.sbuf_tensor("dta", [128, TOWN], BF16) as dta,
                nc.sbuf_tensor("dtb", [128, TOWN], BF16) as dtb,
            ):
                douts = [dta, dtb]

                @block.scalar
                def _(s):
                    s.wait_ge(LSM.h, p0_ld)
                    i = s.activation(SM[:, ECOL_C:ECOL_C + 16],
                                      SM[:, TD_C:TD_C + 16], AF.Exp)
                    sedge(s, i)
                    i = s.activation(SM[:, NECOL_C:NECOL_C + 16],
                                     SM[:, ECOL_C:ECOL_C + 16], AF.Copy,
                                     scale=-1.0)
                    sedge(s, i)
                    i = s.activation(SM[:, WCOL_C:WCOL_C + 16],
                                     SM[:, ECOL_C:ECOL_C + 16], AF.Exp, scale=-1.0)
                    SC.inc(i)
                sc_w = SC.n

                ld0, sc0, vc0 = L8.n, SC.n, VC.n
                ldA = lambda dc: ld0 + 32 * (dc + 1)
                scA = lambda dc, k: sc0 + 3 * dc + k
                vcA = lambda dc, k: vc0 + 6 * dc + k

                @block.sync
                def _(sy):
                    wait_stores(sy)
                    for dc in range(NDC):
                        if dc >= 1:
                            sy.wait_ge(VC.h, vcA(dc - 1, 6))
                        i = sy.dma_start(out=ktl[:],
                                         in_=kT1_sc[dc * 128:(dc + 1) * 128, :])
                        L8.inc(i, 16)
                        i = sy.dma_start(out=vtl[:],
                                         in_=vT1_sc[dc * 128:(dc + 1) * 128, :])
                        L8.inc(i, 16)

                @block.scalar
                def _(s):
                    s.wait_ge(SC.h, sc_w)
                    for dc in range(NDC):
                        s.wait_ge(L8.h, ldA(dc))
                        if dc >= 1:
                            s.wait_ge(VC.h, vcA(dc - 1, 3))
                        i = s.activation(ekf[:], ktl[:, 0:HT], AF.Exp)
                        SC.inc(i)
                        i = s.activation(wrp[:], ktl[:, 0:HT], AF.Exp, scale=0.0,
                                         bias=SM[:, NECOL_C + dc:NECOL_C + dc + 1])
                        SC.inc(i)
                        s.wait_ge(VC.h, vcA(dc, 3))
                        i = s.activation(ekf[:], ktl[:, HT:TOWN], AF.Exp)
                        SC.inc(i)

                @block.vector
                def _(v):
                    for dc in range(NDC):
                        v.wait_ge(SC.h, scA(dc, 1))
                        v.wait_ge(L8.h, ldA(dc))
                        i = v.tensor_tensor(avf[:], ekf[:], vtl[:, 0:HT], ALU.mult)
                        VC.inc(i)
                        v.wait_ge(VC.h, VC.n)
                        v.wait_ge(SC.h, scA(dc, 2))
                        i = v.tensor_tensor_scan(n0f[:], wrp[:], avf[:], 0.0,
                                                 ALU.mult, ALU.add)
                        VC.inc(i)
                        i = v.tensor_tensor_scan(d0f[:], wrp[:], ekf[:], 0.0,
                                                 ALU.mult, ALU.add)
                        vedge(v, i)
                        v.wait_ge(VC.h, VC.n)
                        v.tensor_copy(sth[:, 0:1], n0f[:, HT - 1:HT])
                        i = v.tensor_copy(sth[:, 1:2], d0f[:, HT - 1:HT])
                        VC.inc(i)
                        v.wait_ge(VC.h, VC.n)
                        v.wait_ge(SC.h, scA(dc, 3))
                        i = v.tensor_tensor(avf[:], ekf[:], vtl[:, HT:TOWN],
                                            ALU.mult)
                        VC.inc(i)
                        v.wait_ge(VC.h, VC.n)
                        i = v.tensor_tensor_scan(n0f[:], wrp[:], avf[:],
                                                 sth[:, 0:1], ALU.mult, ALU.add)
                        VC.inc(i)
                        v.wait_ge(VC.h, VC.n)
                        i = v.tensor_tensor_scan(d0f[:], wrp[:], ekf[:],
                                                 sth[:, 1:2], ALU.mult, ALU.add)
                        vedge(v, i)
                        v.tensor_copy(SM[:, STN_C + dc:STN_C + dc + 1],
                                      n0f[:, HT - 1:HT])
                        i = v.tensor_copy(SM[:, STD_C + dc:STD_C + dc + 1],
                                          d0f[:, HT - 1:HT])
                        VC.inc(i)

                @block.vector
                def _(v):
                    v.wait_ge(VC.h, VC.n)
                    i = None
                    for bp in range(4):
                        i = v.tensor_scalar(
                            SM[:, CC2S_C + bp * 16:CC2S_C + (bp + 1) * 16],
                            SM[:, STN_C:STN_C + 16],
                            SM[:, WSEL_C + bp:WSEL_C + bp + 1], None, ALU.mult)
                        i = v.tensor_scalar(
                            SM[:, CC2S_C + 64 + bp * 16:CC2S_C + 64 + (bp + 1) * 16],
                            SM[:, STD_C:STD_C + 16],
                            SM[:, WSEL_C + bp:WSEL_C + bp + 1], None, ALU.mult)
                    VC.inc(i)
                vc_cc2 = VC.n

                @block.gpsimd
                def _(g):
                    g.wait_ge(VC.h, vc_cc2)
                    i = g.dma_start(out=cc2_in, in_=SM[:, CC2S_C:CC2S_C + 128])
                    CCD.inc(i, 16)
                    g.wait_ge(CCD.h, CCD.n)
                    i = g.collective_compute(
                        "AllReduce", ALU.add,
                        replica_groups=[[0]] if sim1 else [list(range(8))],
                        ins=[cc2_in.opt()], outs=[cc2_out.opt()])
                    CCB.inc(i, 1)
                    g.wait_ge(CCB.h, 1)
                    i = g.dma_start(out=SM[:, RX2_C:RX2_C + 128], in_=cc2_out)
                    CCD.inc(i, 16)
                gc_rx2 = CCD.n

                @block.vector
                def _(v):
                    v.wait_ge(CCD.h, gc_rx2)
                    for (dstc, base) in ((RN_C, RX2_C), (RD_C, RX2_C + 64)):
                        i = v.tensor_scalar(SM[:, dstc:dstc + 16],
                                            SM[:, base:base + 16],
                                            SM[:, RSEL_C:RSEL_C + 1], None,
                                            ALU.mult)
                        vedge(v, i)
                        for bp in range(1, 4):
                            i = v.scalar_tensor_tensor(
                                SM[:, dstc:dstc + 16],
                                SM[:, base + bp * 16:base + (bp + 1) * 16],
                                SM[:, RSEL_C + bp:RSEL_C + bp + 1],
                                SM[:, dstc:dstc + 16], ALU.mult, ALU.add)
                            vedge(v, i)
                        i = v.tensor_scalar(SM[:, dstc:dstc + 16],
                                            SM[:, dstc:dstc + 16],
                                            SM[:, FLAG_C:FLAG_C + 1], None,
                                            ALU.mult)
                        VC.inc(i)
                        v.wait_ge(VC.h, VC.n)

                ld1, sc1, vc1 = L8.n, SC.n, VC.n
                so0 = [SO[0].n, SO[1].n]
                ldB = lambda dc: ld1 + 64 * (dc + 1)
                scB = lambda dc, hf, k: sc1 + 6 * dc + 3 * hf + k
                vcB = lambda dc, hf, k: vc1 + 8 * dc + 4 * hf + k
                stB = lambda dc: so0[dc % 2] + 16 * (dc // 2 + 1)

                @block.sync
                def _(sy):
                    sy.wait_ge(VC.h, vc1)
                    for dc in range(NDC):
                        if dc >= 1:
                            sy.wait_ge(VC.h, vcB(dc - 1, 1, 4))
                        i = sy.dma_start(out=ktl[:],
                                         in_=kT1_sc[dc * 128:(dc + 1) * 128, :])
                        L8.inc(i, 16)
                        i = sy.dma_start(out=vtl[:],
                                         in_=vT1_sc[dc * 128:(dc + 1) * 128, :])
                        L8.inc(i, 16)
                        i = sy.dma_start(out=rtl[:],
                                         in_=rT1_sc[dc * 128:(dc + 1) * 128, :])
                        L8.inc(i, 16)
                        i = sy.dma_start(out=gtl[:],
                                         in_=g1T_sc[dc * 128:(dc + 1) * 128, :])
                        L8.inc(i, 16)

                @block.scalar
                def _(s):
                    for dc in range(NDC):
                        s.wait_ge(L8.h, ldB(dc))
                        for hf in range(2):
                            sl = slice(hf * HT, (hf + 1) * HT)
                            if dc + hf > 0:
                                prev = vcB(dc, 0, 3) if hf else vcB(dc - 1, 1, 3)
                                s.wait_ge(VC.h, prev)
                            i = s.activation(ekf[:], ktl[:, sl], AF.Exp)
                            SC.inc(i)
                            i = s.activation(wrp[:], ktl[:, sl], AF.Exp,
                                             scale=0.0,
                                             bias=SM[:, NECOL_C + dc:NECOL_C + dc + 1])
                            SC.inc(i)
                            s.wait_ge(VC.h, vcB(dc, hf, 3))
                            i = s.activation(d0f[:], d0f[:], AF.Ln,
                                             bias=SM[0:128, EPS2_C:EPS2_C + 1])
                            sedge(s, i)
                            i = s.activation(d0f[:], d0f[:], AF.Exp, scale=-1.0)
                            SC.inc(i)

                @block.vector
                def _(v):
                    for dc in range(NDC):
                        for hf in range(2):
                            sl = slice(hf * HT, (hf + 1) * HT)
                            v.wait_ge(SC.h, scB(dc, hf, 1))
                            i = v.tensor_tensor(avf[:], ekf[:], vtl[:, sl],
                                                ALU.mult)
                            VC.inc(i)
                            v.wait_ge(VC.h, VC.n)
                            v.wait_ge(SC.h, scB(dc, hf, 2))
                            ini_n = (SM[:, RN_C + dc:RN_C + dc + 1] if hf == 0
                                     else sth[:, 2:3])
                            ini_d = (SM[:, RD_C + dc:RD_C + dc + 1] if hf == 0
                                     else sth[:, 3:4])
                            i = v.tensor_tensor_scan(n0f[:], wrp[:], avf[:],
                                                     ini_n, ALU.mult, ALU.add)
                            VC.inc(i)
                            v.wait_ge(VC.h, VC.n)
                            if hf == 0:
                                i = v.tensor_copy(sth[:, 2:3], n0f[:, HT - 1:HT])
                                vedge(v, i)
                            i = v.tensor_tensor_scan(d0f[:], wrp[:], ekf[:],
                                                     ini_d, ALU.mult, ALU.add)
                            if hf == 0:
                                vedge(v, i)
                                i = v.tensor_copy(sth[:, 3:4], d0f[:, HT - 1:HT])
                            VC.inc(i)
                            v.wait_ge(VC.h, VC.n)
                            v.wait_ge(SC.h, scB(dc, hf, 3))
                            i = v.tensor_tensor(n0f[:], n0f[:], d0f[:], ALU.mult)
                            vedge(v, i)
                            i = v.tensor_tensor(n0f[:], n0f[:], rtl[:, sl],
                                                ALU.mult)
                            vedge(v, i)
                            if dc >= 2 and hf == 0:
                                v.wait_ge(SO[dc % 2].h, stB(dc - 2))
                            i = v.tensor_tensor(douts[dc % 2][:, sl], n0f[:],
                                                gtl[:, sl], ALU.mult)
                            VC.inc(i)

                @block.gpsimd
                def _(g):
                    for dc in range(NDC):
                        g.wait_ge(VC.h, vcB(dc, 1, 4))
                        i = g.dma_start(out=dT_sc[dc * 128:(dc + 1) * 128, :],
                                        in_=douts[dc % 2][:])
                        SO[dc % 2].inc(i, 16)

        if emit_l1 and phase_hi >= 9:
            with (
                nc.sbuf_tensor("xr9a", [128, D], F32) as xr9a,
                nc.sbuf_tensor("xr9b", [128, D], F32) as xr9b,
                nc.sbuf_tensor("dl9a", [128, D], BF16) as dl9a,
                nc.sbuf_tensor("dl9b", [128, D], BF16) as dl9b,
                nc.sbuf_tensor("o9a", [128, D], F32) as o9a,
                nc.sbuf_tensor("o9b", [128, D], F32) as o9b,
            ):
                xr9, dl9, o9 = [xr9a, xr9b], [dl9a, dl9b], [o9a, o9b]
                vc0 = VC.n
                lx0 = [LX[0].n, LX[1].n]
                ld9 = lambda tt: lx0[tt % 2] + 80 * (tt // 2 + 1)
                os0 = [OS[0].n, OS[1].n]
                os_at = lambda tt: os0[tt % 2] + 16 * (tt // 2 + 1)

                @block.sync
                def _(sy):
                    wait_stores(sy)
                    for tt in range(16):
                        if tt >= 2:
                            sy.wait_ge(VC.h, vc0 + tt - 1)
                        i = sy.dma_start(out=xr9[tt % 2][:],
                                         in_=x1_tm[tt * 128:(tt + 1) * 128, :])
                        LX[tt % 2].inc(i, 16)
                        for gq in range(4):
                            i = sy.dma_start(
                                out=dl9[tt % 2][:, gq * 512:(gq + 1) * 512],
                                in_=dT_sc[gq * 512:(gq + 1) * 512,
                                          tt * 128:(tt + 1) * 128],
                                transpose=True)
                            LX[tt % 2].inc(i, 16)

                @block.vector
                def _(v):
                    for tt in range(16):
                        v.wait_ge(LX[tt % 2].h, ld9(tt))
                        if tt >= 2:
                            v.wait_ge(OS[tt % 2].h, os_at(tt - 2))
                        i = v.tensor_tensor(o9[tt % 2][:], xr9[tt % 2][:],
                                            dl9[tt % 2][:], ALU.add)
                        VC.inc(i)

                @block.gpsimd
                def _(g):
                    for tt in range(16):
                        g.wait_ge(VC.h, vc0 + tt + 1)
                        i = g.dma_start(out=out_x2[tt * 128:(tt + 1) * 128, :],
                                        in_=o9[tt % 2][:])
                        OS[tt % 2].inc(i, 16)

        @block.sync
        def _(sy):
            for sm in ALL_DMA + [TC, VC, SC, GC, CCA, CCB, CCD, VSs, SSs]:
                if sm.n:
                    sy.wait_ge(sm.h, sm.n)

    return nc


def _prep_weights(inp):
    t = lambda a: np.ascontiguousarray(np.asarray(a))
    w = {}

    def chunk_T(m):
        Dm, O = m.shape
        return t(m.reshape(Dm // 128, 128, O).transpose(1, 0, 2).reshape(128, -1))

    w["wqT"] = chunk_T(t(np.asarray(inp["L0_attn_wq"]).T))
    w["wkT"] = chunk_T(t(np.asarray(inp["L0_attn_wk"]).T))
    w["wvT"] = chunk_T(t(np.asarray(inp["L0_attn_wv"]).T))
    w["woT"] = chunk_T(t(np.asarray(inp["L0_attn_wo"]).T))
    for nm, key in (("w_gate", "L0_ffn_w_gate"), ("w_up", "L0_ffn_w_up")):
        m = np.asarray(inp[key])
        w[nm] = t(m.reshape(16, 128, 16, 512).transpose(1, 2, 0, 3).reshape(128, -1))
    md = np.asarray(inp["L0_ffn_w_down"])
    w["w_down"] = t(md.reshape(64, 128, 16, 128).transpose(1, 2, 0, 3).reshape(128, -1))
    w["sgate_c"] = t(np.asarray(inp["L0_ffn_s_gate"]).reshape(64, 128).T)
    w["sup_c"] = t(np.asarray(inp["L0_ffn_s_up"]).reshape(64, 128).T)
    w["sdown_c"] = t(np.asarray(inp["L0_ffn_s_down"]).reshape(16, 128).T)
    w["wkT1"] = chunk_T(t(np.asarray(inp["L1_attn_Wk"]).T))
    w["wvT1"] = chunk_T(t(np.asarray(inp["L1_attn_Wv"]).T))
    w["wrT1"] = chunk_T(t(np.asarray(inp["L1_attn_Wr"]).T))
    w["wbT"] = chunk_T(t(np.asarray(inp["L1_ffn_W_base"]).T))
    w["td_c"] = t(np.asarray(inp["L1_attn_time_decay"]).reshape(16, 128).T)
    w["bb_c"] = t(np.asarray(inp["L1_ffn_b_base"]).reshape(16, 128).T)
    w["rwT"] = chunk_T(t(np.asarray(inp["L1_ffn_router_W"]).T))
    w["AfT"] = chunk_T(t(np.asarray(inp["L1_ffn_A"]).reshape(E * R, D).T))
    w["BwF"] = t(np.asarray(inp["L1_ffn_Bw"]).transpose(0, 2, 1).reshape(E * R, D))
    sel = np.zeros((8, 128), np.float32)
    for e in range(E):
        sel[e, e * R:(e + 1) * R] = 2.0
    w["selm"] = sel
    return w


def _per_core(inp, b, h):
    x = np.asarray(inp["L0_attn_x"])
    own = x[b, h * TOWN:(h + 1) * TOWN]
    oth = x[b, (1 - h) * TOWN:(2 - h) * TOWN]
    m = {"x_in": np.ascontiguousarray(np.concatenate([own, oth], axis=0)),
         "flagi": np.array([[1 - h]], np.int32)}
    cb = np.zeros((128, 128), np.float32)
    for qi in range(4):
        for j in range(32):
            if j < 16:
                v = SHIFT if j <= 4 * qi + 3 else NEG
            else:
                v = SHIFT if h == 1 else NEG
            cb[:, qi * 32 + j] = v
    m["cb"] = cb
    m["flag"] = np.full((128, 1), float(h), np.float32)
    ws = np.zeros((128, 4), np.float32)
    ws[:, b] = 1.0 - h
    m["wsel"] = ws
    rs = np.zeros((128, 4), np.float32)
    rs[:, b] = 1.0
    m["rsel"] = rs
    return m


def kernel(**inputs):
    w = _prep_weights(inputs)
    in_maps = []
    for c in range(8):
        b, h = c // 2, c % 2
        m = dict(w)
        m.update(_per_core(inputs, b, h))
        in_maps.append(m)
    nc = build(phase_hi=PHASE_HI, dbg=DBG, part=PART)
    res = run_bass_kernel_spmd(nc, in_maps, core_ids=list(range(8)))
    out = np.empty((B, T, D), np.float32)
    for c in range(8):
        b, h = c // 2, c % 2
        out[b, h * TOWN:(h + 1) * TOWN] = np.asarray(res.results[c]["out_x2"])
    return out


# version 05
# speedup vs baseline: 1.0657x; 1.0109x over previous
import numpy as np
import ml_dtypes
from contextlib import ExitStack

import concourse.bass as bass
import concourse.mybir as mybir
from concourse.bass_utils import run_bass_kernel_spmd

F32 = mybir.dt.float32
BF16 = mybir.dt.bfloat16
I8 = mybir.dt.int8
AF = mybir.ActivationFunctionType
ALU = mybir.AluOpType
AXX = mybir.AxisListType.X

B, T, D = 4, 4096, 2048
TOWN = 2048
NDC = 16
NH = 16
INTER = 8192
NIC = 64
E, R = 8, 16
RSQ = float(1.0 / np.sqrt(128.0))
SHIFT = -20.0
NEG = -30000.0

PHASE_HI = 99
DBG = ()
PART = "all"


class _EndL0(Exception):
    pass


def build(phase_hi=99, dbg=(), part="all", sim1=False):
    nc = bass.Bass("TRN2", target_bir_lowering=False, debug=False)
    emit_l0 = part in ("all", "l0")
    emit_l1 = part in ("all", "l1")

    def dram(name, shape, dt):
        kind = "ExternalOutput" if name in dbg else "Internal"
        return nc.dram_tensor(name, shape, dt, kind=kind).ap()

    ein = lambda n, s, d: nc.dram_tensor(n, s, d, kind="ExternalInput").ap()

    if emit_l0:
        d_x = ein("x_in", [T, D], F32)
        d_wqT = ein("wqT", [128, NDC * D], F32)
        d_wkT = ein("wkT", [128, NDC * D], F32)
        d_wvT = ein("wvT", [128, NDC * D], F32)
        d_woT = ein("woT", [128, NDC * D], F32)
        d_wg = ein("w_gate", [128, 16 * 16 * 512], I8)
        d_wu = ein("w_up", [128, 16 * 16 * 512], I8)
        d_wd = ein("w_down", [128, 16 * 64 * 128], I8)
        d_sg = ein("sgate_c", [128, NIC], F32)
        d_su = ein("sup_c", [128, NIC], F32)
        d_sd = ein("sdown_c", [128, NDC], F32)
        d_cb = ein("cb", [128, 128], F32)
    if emit_l1:
        d_wkT1 = ein("wkT1", [128, NDC * D], F32)
        d_wvT1 = ein("wvT1", [128, NDC * D], F32)
        d_wrT1 = ein("wrT1", [128, NDC * D], F32)
        d_wbT = ein("wbT", [128, NDC * D], F32)
        d_td = ein("td_c", [128, NDC], F32)
        d_bb = ein("bb_c", [128, NDC], F32)
        d_rw = ein("rwT", [128, NDC * 8], F32)
        d_Af = ein("AfT", [128, NDC * 128], F32)
        d_Bw = ein("BwF", [128, D], F32)
        d_sel = ein("selm", [8, 128], F32)
    d_flag = ein("flag", [128, 1], F32)
    d_fi = ein("flagi", [1, 1], mybir.dt.int32)
    d_wsel = ein("wsel", [128, 4], F32)
    d_rsel = ein("rsel", [128, 4], F32)

    if part == "all":
        x1_tm = dram("x1_tm", [TOWN, D], F32)
        xn1_tm = dram("xn1_tm", [TOWN + 1, D], BF16)
    elif part == "l0":
        x1_tm = nc.dram_tensor("x1_tm", [TOWN, D], F32, kind="ExternalOutput").ap()
        xn1_tm = nc.dram_tensor("xn1_tm", [TOWN + 1, D], BF16,
                                kind="ExternalOutput").ap()
    else:
        x1_tm = ein("x1_tm", [TOWN, D], F32)
        xn1_tm = ein("xn1_tm", [TOWN + 1, D], BF16)
    if emit_l1:
        out_x2 = nc.dram_tensor("out_x2", [TOWN, D], F32, kind="ExternalOutput").ap()

    if emit_l0:
        xn_tm = dram("xn_tm", [T, D], BF16)
        kT_sc = dram("kT_sc", [D, T], BF16)
        vT_sc = dram("vT_sc", [D, T], BF16)
        qT_sc = dram("qT_sc", [D, TOWN], BF16)
        sT_sc = dram("sT_sc", [D, TOWN], BF16)
        cc1_in = dram("cc1_in", [128, 64], F32)
        cc1_out = dram("cc1_out", [128, 64], F32)
    dT_sc = dram("dT_sc", [D, TOWN], BF16)
    if emit_l1:
        kT1_sc = dram("kT1_sc", [D, TOWN], BF16)
        vT1_sc = dram("vT1_sc", [D, TOWN], BF16)
        rT1_sc = dram("rT1_sc", [D, TOWN], BF16)
        g1T_sc = dram("g1T_sc", [D, TOWN], BF16)
        mk_sc = dram("mk_sc", [TOWN, 128], BF16)
        cc2_in = dram("cc2_in", [128, 128], F32)
        cc2_out = dram("cc2_out", [128, 128], F32)

    ctx = ExitStack()
    with ctx:
        sb = lambda n, s, d: ctx.enter_context(nc.sbuf_tensor(n, s, d))
        psm = lambda n, s, d=F32: ctx.enter_context(nc.psum_tensor(n, s, d))

        class Sem:
            def __init__(self, name):
                self.h = ctx.enter_context(nc.semaphore(name))
                self.n = 0

            def inc(self, instr, k=1):
                instr.then_inc(self.h, k)
                self.n += k
                return self.n

        A = sb("arenaA", [128, NDC * D], BF16)
        Bb = sb("arenaB", [128, NDC * D], BF16)
        SM = sb("smalls", [128, 2048], F32)
        SMB = sb("smallsb", [128, 2708], BF16)
        EV = sb("evac", [128, 2048], BF16)
        R8 = sb("row8", [8, 256], BF16)
        RWB = sb("rwb", [128, NDC * 8], BF16)
        FI = sb("flagi_sb", [1, 1], mybir.dt.int32)
        bnd = sb("bnd", [128, 16], BF16)

        p512 = [psm(f"ps{i}", [128, 512]) for i in range(6)]
        pD = psm("psD", [128, 512])
        pB = psm("psB", [128, 512])

        LSM = Sem("LSM")
        LX = [Sem("LX0"), Sem("LX1")]
        GX = [Sem("GX0"), Sem("GX1")]
        GS = [Sem("GS0"), Sem("GS1")]
        LR = [Sem("LR0"), Sem("LR1")]
        KH = [Sem("KH0"), Sem("KH1")]
        LS = [Sem("LS0"), Sem("LS1")]
        LMK = Sem("LMK")
        L8 = Sem("L8")
        WSM = Sem("WSM")
        WA = Sem("WA")
        WB = Sem("WB")
        WG = [Sem("WG0"), Sem("WG1")]
        WU = [Sem("WU0"), Sem("WU1")]
        SX = [Sem("SX0"), Sem("SX1")]
        SE = [Sem(f"SE{i}") for i in range(4)]
        SD = [Sem("SD0"), Sem("SD1")]
        SG = [Sem("SG0"), Sem("SG1")]
        SMK = Sem("SMK")
        SO = [Sem("SO0"), Sem("SO1")]
        OS = [Sem("OS0"), Sem("OS1")]
        STORE_SEMS = SX + SE + SD + SG + [SMK] + SO
        ALL_DMA = ([LSM, LMK, L8, WSM, WA, WB] + LX + GX + LR + KH + LS + WG + WU +
                   STORE_SEMS + GS + OS)

        def wait_stores(eng):
            for sm in STORE_SEMS + GS:
                if sm.n:
                    eng.wait_ge(sm.h, sm.n)

        TC = Sem("TC")
        VC = Sem("VC")
        SC = Sem("SC")
        GC = Sem("GC")
        CCA = Sem("CCA")
        CCB = Sem("CCB")
        CCD = Sem("CCD")
        VSs = Sem("VSs")
        SSs = Sem("SSs")

        def vedge(v, instr):
            VSs.inc(instr)
            v.wait_ge(VSs.h, VSs.n)

        def sedge(s, instr):
            SSs.inc(instr)
            s.wait_ge(SSs.h, SSs.n)

        block = ctx.enter_context(nc.Block())

        (SUM_C, SQ_C, MEAN_C, NMEAN_C, VAR_C, RSTD_C, NMUR_C, T1_C, T2_C) = range(9)
        FLAG_C, WSEL_C, RSEL_C = 10, 12, 16
        EPS1_C, EPS2_C = 20, 21
        CB_C = 24
        IOTA_C = 160
        KPCOL_C = 676
        MCOL_C = 680
        ECOL_C = 690
        NECOL_C = 1970
        WCOL_C = 710
        BND_C = 730
        CCS_C = 750
        RX_C = 820
        ACC_C = 890
        DEN_C = 910
        STN_C = 1430
        STD_C = 1450
        RN_C = 1470
        RD_C = 1490
        SG_C = 1510
        SU_C = 1580
        SD_C = 1650
        BB_C = 1670
        TD_C = 1690
        CC2S_C = 1710
        RX2_C = 1840
        MK0_C = 0
        ONES_C = 2048
        ONESR_C = 2052
        RB_C = 2192

        @block.sync
        def _(sy):
            i = sy.dma_start(out=FI[:], in_=d_fi)
            LSM.inc(i, 16)
            i = sy.dma_start(out=SM[:, FLAG_C:FLAG_C + 1], in_=d_flag)
            LSM.inc(i, 16)
            i = sy.dma_start(out=SM[:, WSEL_C:WSEL_C + 4], in_=d_wsel)
            LSM.inc(i, 16)
            i = sy.dma_start(out=SM[:, RSEL_C:RSEL_C + 4], in_=d_rsel)
            LSM.inc(i, 16)
            if emit_l0:
                i = sy.dma_start(out=SM[:, CB_C:CB_C + 128], in_=d_cb)
                LSM.inc(i, 16)
                i = sy.dma_start(out=SM[:, SG_C:SG_C + 64], in_=d_sg)
                LSM.inc(i, 16)
                i = sy.dma_start(out=SM[:, SU_C:SU_C + 64], in_=d_su)
                LSM.inc(i, 16)
                i = sy.dma_start(out=SM[:, SD_C:SD_C + 16], in_=d_sd)
                LSM.inc(i, 16)
            if emit_l1:
                i = sy.dma_start(out=SM[:, TD_C:TD_C + 16], in_=d_td)
                LSM.inc(i, 16)
                i = sy.dma_start(out=SM[:, BB_C:BB_C + 16], in_=d_bb)
                LSM.inc(i, 16)

        @block.gpsimd
        def _(g):
            g.iota(SM[:, IOTA_C:IOTA_C + 512], pattern=[[1, 512]], base=0,
                   channel_multiplier=0, allow_small_or_imprecise_dtypes=True)
            i = g.iota(SM[:, KPCOL_C:KPCOL_C + 1], pattern=[[1, 1]], base=0,
                       channel_multiplier=1, allow_small_or_imprecise_dtypes=True)
            GC.inc(i)
            if emit_l1:
                i = g.dma_start(out=R8[0:8, 0:128], in_=d_sel)
                WSM.inc(i, 16)
                i = g.dma_start(out=RWB[:], in_=d_rw)
                WSM.inc(i, 16)
        p0_gc, p0_ld, p0_wt = GC.n, LSM.n, WSM.n

        @block.vector
        def _(v):
            v.memset(SMB[:, ONES_C:ONES_C + 1], 1.0)
            v.memset(SMB[0:1, ONESR_C:ONESR_C + 128], 1.0)
            v.memset(SM[:, EPS1_C:EPS1_C + 1], 1e-5)
            v.memset(SM[:, EPS2_C:EPS2_C + 1], 1e-8)
            v.wait_ge(GC.h, p0_gc)
            v.wait_ge(LSM.h, p0_ld)
            i = None
            for m in range(4):
                i = v.tensor_scalar_add(SM[:, MCOL_C + m:MCOL_C + m + 1],
                                        SM[:, KPCOL_C:KPCOL_C + 1], float(128 * m))
            vedge(v, i)
            for m in range(4):
                i = v.tensor_single_scalar(
                    SMB[:, MK0_C + m * 512:MK0_C + (m + 1) * 512],
                    SM[:, IOTA_C:IOTA_C + 512],
                    SM[:, MCOL_C + m:MCOL_C + m + 1], ALU.is_ge)
            VC.inc(i)
        p0_vc = VC.n

        ROLE = {}

        def _mk_role(nm, dec):
            @dec
            def _(eng):
                r = eng.alloc_register()
                eng.wait_ge(LSM.h, p0_ld)
                eng.reg_load(r, FI[0:1, 0:1])
                ROLE[nm] = r

        if emit_l0 and phase_hi >= 3:
            _mk_role("t", block.tensor)
            _mk_role("s", block.scalar)
            _mk_role("v", block.vector)
            _mk_role("sy", block.sync)

        def load_weight(dst, src, wsem):
            tc_req = TC.n

            @block.gpsimd
            def _(g):
                if tc_req:
                    g.wait_ge(TC.h, tc_req)
                for gi in range(4):
                    i = g.dma_start(out=dst[:, gi * 4 * D:(gi + 1) * 4 * D],
                                    in_=src[:, gi * 4 * D:(gi + 1) * 4 * D])
                    wsem.inc(i, 16)
            return wsem.n

        def emit_ln(n_tiles, load_src, dst_tm, dst_row0, xf, xn16, dum16,
                    pre_vec=None, store_xrow=None, dma_dec=None):
            LXq = LX if dma_dec is None else GX
            SXq = SX if dma_dec is None else GS
            if dma_dec is None:
                dma_dec = block.sync

            pv = 1 if pre_vec is not None else 0
            sc0, vc0 = SC.n, VC.n
            per_st = 16 * (2 if store_xrow is not None else 1)
            sx0 = [SXq[0].n, SXq[1].n]
            st_at = {tt: sx0[tt % 2] + per_st * (tt // 2 + 1)
                     for tt in range(n_tiles)}
            sc_sums = lambda tt: sc0 + 4 * tt + 1
            sc_rstd = lambda tt: sc0 + 4 * tt + 4
            vc_pre = lambda tt: vc0 + (2 + pv) * tt + 1
            vc_varA = lambda tt: vc0 + (2 + pv) * tt + pv + 1
            vc_xn = lambda tt: vc0 + (2 + pv) * tt + pv + 2
            ld_at = {}

            if load_src is not None:
                @dma_dec
                def _(sy):
                    for tt in range(n_tiles + 1):
                        if tt < n_tiles:
                            if tt >= 2:
                                sy.wait_ge(VC.h, vc_xn(tt - 2))
                            i = sy.dma_start(out=xf[tt % 2][:], in_=load_src(tt))
                            ld_at[tt] = (LXq[tt % 2], LXq[tt % 2].inc(i, 16))
                        if tt >= 1:
                            pt = tt - 1
                            sy.wait_ge(VC.h, vc_xn(pt))
                            if store_xrow is not None:
                                i = sy.dma_start(out=store_xrow(pt),
                                                 in_=xf[pt % 2][:])
                                SXq[pt % 2].inc(i, 16)
                            i = sy.dma_start(
                                out=dst_tm[dst_row0 + pt * 128:
                                           dst_row0 + (pt + 1) * 128, :],
                                in_=xn16[pt % 2][:])
                            SXq[pt % 2].inc(i, 16)

            @block.scalar
            def _(s):
                for tt in range(n_tiles):
                    if load_src is not None:
                        s.wait_ge(ld_at[tt][0].h, ld_at[tt][1])
                    else:
                        s.wait_ge(VC.h, vc_pre(tt))
                    if tt >= 1:
                        s.wait_ge(VC.h, vc_varA(tt - 1))
                    i = s.activation(dum16[0][:], xf[tt % 2][:], AF.Copy,
                                     accum_out=SM[:, SUM_C:SUM_C + 1])
                    sedge(s, i)
                    i = s.activation(dum16[0][:], xf[tt % 2][:], AF.Square,
                                     accum_out=SM[:, SQ_C:SQ_C + 1])
                    SC.inc(i)
                    s.wait_ge(VC.h, vc_varA(tt))
                    i = s.activation(SM[:, T1_C:T1_C + 1], SM[:, VAR_C:VAR_C + 1],
                                     AF.Ln, bias=SM[:, EPS1_C:EPS1_C + 1])
                    sedge(s, i)
                    i = s.activation(SM[:, RSTD_C:RSTD_C + 1],
                                     SM[:, T1_C:T1_C + 1], AF.Exp, scale=-0.5)
                    SC.inc(i, 3)

            @block.vector
            def _(v):
                for tt in range(n_tiles):
                    if pre_vec is not None:
                        if tt >= 2:
                            v.wait_ge(SXq[tt % 2].h, st_at[tt - 2])
                        pre_vec(v, tt)
                    v.wait_ge(SC.h, sc_sums(tt))
                    i = v.tensor_scalar_mul(SM[:, MEAN_C:MEAN_C + 1],
                                            SM[:, SUM_C:SUM_C + 1], 1.0 / D)
                    vedge(v, i)
                    i = v.tensor_scalar_mul(SM[:, NMEAN_C:NMEAN_C + 1],
                                            SM[:, MEAN_C:MEAN_C + 1], -1.0)
                    i = v.tensor_scalar_mul(SM[:, T2_C:T2_C + 1],
                                            SM[:, SQ_C:SQ_C + 1], 1.0 / D)
                    vedge(v, i)
                    i = v.scalar_tensor_tensor(
                        SM[:, VAR_C:VAR_C + 1], SM[:, MEAN_C:MEAN_C + 1],
                        SM[:, NMEAN_C:NMEAN_C + 1], SM[:, T2_C:T2_C + 1],
                        ALU.mult, ALU.add)
                    VC.inc(i)
                    v.wait_ge(SC.h, sc_rstd(tt))
                    i = v.tensor_scalar(SM[:, NMUR_C:NMUR_C + 1],
                                        SM[:, NMEAN_C:NMEAN_C + 1],
                                        SM[:, RSTD_C:RSTD_C + 1], None, ALU.mult)
                    vedge(v, i)
                    if tt >= 2 and pre_vec is None:
                        v.wait_ge(SXq[tt % 2].h, st_at[tt - 2])
                    i = v.tensor_scalar(xn16[tt % 2][:], xf[tt % 2][:],
                                        SM[:, RSTD_C:RSTD_C + 1],
                                        SM[:, NMUR_C:NMUR_C + 1],
                                        ALU.mult, ALU.add)
                    VC.inc(i)

            emit_ln_last_st_at = st_at

            def store_tile(sy, tt):
                sy.wait_ge(VC.h, vc_xn(tt))
                if store_xrow is not None:
                    i = sy.dma_start(out=store_xrow(tt), in_=xf[tt % 2][:])
                    SXq[tt % 2].inc(i, 16)
                i = sy.dma_start(
                    out=dst_tm[dst_row0 + tt * 128:
                               dst_row0 + (tt + 1) * 128, :],
                    in_=xn16[tt % 2][:])
                SXq[tt % 2].inc(i, 16)
            store_tile.st_at = st_at
            store_tile.slots = SXq
            return store_tile

        def emit_proj(warena, wsem, wt_ready, rhs_src, t_blocks, dstf, rbufs,
                      rhs_gate=None, tb_order=None, prev_proj=False):
            order = list(tb_order) if tb_order is not None else list(range(t_blocks))
            tc0, vc0 = TC.n, VC.n
            lr0 = [LR[0].n, LR[1].n]
            ld_at = {pos: lr0[pos % 2] + 256 * (pos // 2 + 1)
                     for pos in range(t_blocks)}
            se0 = [sm.n for sm in SE]

            @block.sync
            def _(sy):
                for pos, tb in enumerate(order):
                    rb = rbufs[pos % 2]
                    if pos >= 2:
                        sy.wait_ge(TC.h, tc0 + 16 * (pos - 1))
                    elif tc0:
                        req = tc0 - 16 if (prev_proj and pos == 0) else tc0
                        if req > 0:
                            sy.wait_ge(TC.h, req)
                    if rhs_gate is not None:
                        for (gs, gt) in rhs_gate(tb):
                            sy.wait_ge(gs.h, gt)
                    for dc in range(NDC):
                        i = sy.dma_start(out=rb[:, dc * 512:(dc + 1) * 512],
                                         in_=rhs_src(tb, dc), transpose=True)
                        LR[pos % 2].inc(i, 16)

            @block.tensor
            def _(t):
                t.wait_ge(wsem.h, wt_ready)
                for pos, tb in enumerate(order):
                    rb = rbufs[pos % 2]
                    t.wait_ge(LR[pos % 2].h, ld_at[pos])
                    for oc in range(NDC):
                        k = pos * 16 + oc
                        pp = p512[k % 4]
                        if k >= 4:
                            t.wait_ge(VC.h, vc0 + k - 3)
                        for dc in range(NDC):
                            lh = warena[:, (dc * NDC + oc) * 128:
                                        (dc * NDC + oc) * 128 + 128]
                            t.ldweights(lh)
                            i = t.matmul(pp[:], lh, rb[:, dc * 512:(dc + 1) * 512],
                                         start=(dc == 0), stop=(dc == NDC - 1))
                        TC.inc(i)

            @block.vector
            def _(v):
                for k in range(t_blocks * 16):
                    v.wait_ge(TC.h, tc0 + k + 1)
                    tgt = se0[k % 4] + 16 * (k // 4)
                    if tgt:
                        v.wait_ge(SE[k % 4].h, tgt)
                    i = v.tensor_copy(EV[:, (k % 4) * 512:(k % 4 + 1) * 512],
                                      p512[k % 4][:])
                    VC.inc(i)

            @block.gpsimd
            def _(g):
                for pos, tb in enumerate(order):
                    for oc in range(NDC):
                        k = pos * 16 + oc
                        g.wait_ge(VC.h, vc0 + k + 1)
                        i = g.dma_start(out=dstf(oc, tb),
                                        in_=EV[:, (k % 4) * 512:(k % 4 + 1) * 512])
                        SE[k % 4].inc(i, 16)

        wq_rdy = load_weight(A, d_wqT, WA) if (emit_l0 and phase_hi >= 2) else 0
        if emit_l0 and phase_hi >= 1:
            with (
                nc.sbuf_tensor("xfa", [128, D], F32) as xfa,
                nc.sbuf_tensor("xfb", [128, D], F32) as xfb,
                nc.sbuf_tensor("xna", [128, D], BF16) as xna,
                nc.sbuf_tensor("xnb", [128, D], BF16) as xnb,
                nc.sbuf_tensor("dum", [128, D], BF16) as dum,
                nc.sbuf_tensor("rhsC", [128, NDC * 512], BF16) as C,
                nc.sbuf_tensor("rhsD", [128, NDC * 512], BF16) as Dd,
            ):
                p1_store = emit_ln(
                    T // 128, lambda tt: d_x[tt * 128:(tt + 1) * 128, :],
                    xn_tm, 0, [xfa, xfb], [xna, xnb], (dum, dum),
                    dma_dec=block.gpsimd)
                p1_st_at = p1_store.st_at

                rhs_src = lambda tb, dc: xn_tm[tb * 512:(tb + 1) * 512,
                                               dc * 128:(dc + 1) * 128]

                p1_slots = p1_store.slots

                def p1_gate(tb):
                    tt = tb * 4 + 3
                    return [(p1_slots[tt % 2], p1_st_at[tt]),
                            (p1_slots[(tt - 1) % 2], p1_st_at[tt - 1])]

                if phase_hi >= 2:
                    emit_proj(A, WA, wq_rdy, rhs_src, 4,
                              lambda oc, tb: qT_sc[oc * 128:(oc + 1) * 128,
                                                   tb * 512:(tb + 1) * 512],
                              (C, Dd), rhs_gate=p1_gate)
                    wk_rdy = load_weight(Bb, d_wkT, WB)
                    emit_proj(Bb, WB, wk_rdy, rhs_src, 8,
                              lambda oc, tb: kT_sc[oc * 128:(oc + 1) * 128,
                                                   tb * 512:(tb + 1) * 512],
                              (C, Dd), rhs_gate=p1_gate, prev_proj=True)
                    wv_rdy = load_weight(A, d_wvT, WA)
                    emit_proj(A, WA, wv_rdy, rhs_src, 8,
                              lambda oc, tb: vT_sc[oc * 128:(oc + 1) * 128,
                                                   tb * 512:(tb + 1) * 512],
                              (C, Dd), rhs_gate=p1_gate, prev_proj=True)

        if emit_l0 and phase_hi >= 3:
            wo_rdy = load_weight(Bb, d_woT, WB)
            with (
                nc.sbuf_tensor("kth0", [128, T], BF16) as kth0,
                nc.sbuf_tensor("kth1", [128, T], BF16) as kth1,
                nc.sbuf_tensor("vh0", [128, T], BF16) as vh0,
                nc.sbuf_tensor("vh1", [128, T], BF16) as vh1,
                nc.sbuf_tensor("qh0", [128, TOWN], BF16) as qh0,
                nc.sbuf_tensor("qh1", [128, TOWN], BF16) as qh1,
                nc.sbuf_tensor("ptil", [128, 4 * 512], BF16) as ptil,
                nc.sbuf_tensor("bb16", [128, 512], F32) as bb16t,
                nc.sbuf_tensor("rden", [1, 512], F32) as rden,
            ):
                kth, vhh, qhh = [kth0, kth1], [vh0, vh1], [qh0, qh1]
                tc0, sc0, vc0 = TC.n, SC.n, VC.n
                kh0 = [KH[0].n, KH[1].n]
                ld_head = {h: kh0[h % 2] + 16 * 34 * (h // 2 + 1)
                           for h in range(NH)}

                VIS = [{qi: list(range(4 * qi + 4)) + list(range(16, 32))
                        for qi in range(4)},
                       {qi: list(range(4 * qi + 4)) for qi in range(4)}]
                N0 = {qi: 4 * qi + 20 for qi in range(4)}
                qk_t, pv_t, bc_t, sc_e, sc_rb = {}, {}, {}, {}, {}
                tcc, scc = tc0, sc0
                for hq in range(64):
                    qi = hq % 4
                    n = N0[qi]
                    for p in range(n):
                        tcc += 1
                        qk_t[(hq, p)] = tcc
                        if p >= 1:
                            tcc += 1
                            pv_t[(hq, p - 1)] = tcc
                    tcc += 1
                    pv_t[(hq, n - 1)] = tcc
                    tcc += 1
                    bc_t[hq] = tcc
                    for p in range(n):
                        scc += 1
                        sc_e[(hq, p)] = scc
                    sc_rb[hq] = scc + 2
                    scc += 2
                p3_tc_end, p3_sc_end = tcc, scc
                SCH = [dict(vis=VIS[r],
                            nvis={qi: len(VIS[r][qi]) for qi in range(4)},
                            qk=qk_t, pv=pv_t, bc=bc_t, se=sc_e, srb=sc_rb,
                            tc_end=p3_tc_end, sc_end=p3_sc_end)
                       for r in range(2)]
                vc_mask = lambda hq, m: vc0 + hq * 7 + m + 1
                vc_den = lambda hq: vc0 + hq * 7 + 5
                vc_bb = lambda hq: vc0 + hq * 7 + 6
                vc_ao = lambda hq: vc0 + hq * 7 + 7
                vc_end = vc0 + 64 * 7

                @block.sync
                def _(sy):
                    wait_stores(sy)
                    for case in sy.Switch(index=sy.snap(ROLE["sy"]), n=2):
                        S = SCH[case]
                        for h in range(NH):
                            if h >= 2:
                                sy.wait_ge(TC.h, S["bc"][(h - 1) * 4 - 1])
                            i = sy.dma_start(out=kth[h % 2][:],
                                             in_=kT_sc[h * 128:(h + 1) * 128, :])
                            kk = KH[h % 2].inc(i, 16)
                            for j in range(32):
                                i = sy.dma_start(
                                    out=vhh[h % 2][:, j * 128:(j + 1) * 128],
                                    in_=vT_sc[h * 128:(h + 1) * 128,
                                              j * 128:(j + 1) * 128],
                                    transpose=True)
                                kk = KH[h % 2].inc(i, 16)
                            i = sy.dma_start(out=qhh[h % 2][:],
                                             in_=qT_sc[h * 128:(h + 1) * 128, :])
                            kk = KH[h % 2].inc(i, 16)
                        if case == 0:
                            kh_fin = [KH[0].n, KH[1].n]
                        else:
                            KH[0].n, KH[1].n = kh_fin

                def emit_pv(t, S, h, qi, p, hq):
                    j = S["vis"][qi][p]
                    n = S["nvis"][qi]
                    m = j - 4 * qi
                    if 0 <= m <= 3:
                        t.wait_ge(VC.h, vc_mask(hq, m))
                    else:
                        t.wait_ge(SC.h, S["se"][(hq, p)])
                    if p == 0:
                        if hq >= 1:
                            t.wait_ge(VC.h, vc_den(hq - 1))
                        if hq >= 2:
                            t.wait_ge(VC.h, vc_ao(hq - 2))
                    pt = ptil[:, (p % 4) * 512:(p % 4 + 1) * 512]
                    po = p512[2 + (qi % 2)]
                    lhv = vhh[h % 2][:, j * 128:(j + 1) * 128]
                    t.ldweights(lhv)
                    t.matmul(po[:], lhv, pt, start=(p == 0), stop=(p == n - 1))
                    t.ldweights(SMB[:, ONES_C:ONES_C + 1])
                    i = t.matmul(pD[0:1, :], SMB[:, ONES_C:ONES_C + 1], pt,
                                 start=(p == 0), stop=(p == n - 1))
                    TC.inc(i)

                @block.tensor
                def _(t):
                    tc_save = TC.n
                    for case in t.Switch(index=t.snap(ROLE["t"]), n=2):
                        S = SCH[case]
                        TC.n = tc_save
                        for h in range(NH):
                            t.wait_ge(KH[h % 2].h, ld_head[h])
                            for qi in range(4):
                                hq = h * 4 + qi
                                n = S["nvis"][qi]
                                npr = S["nvis"][(qi - 1) % 4] if hq else 0
                                for p in range(n):
                                    j = S["vis"][qi][p]
                                    sp = p512[p % 2]
                                    if p >= 2:
                                        t.wait_ge(SC.h, S["se"][(hq, p - 2)])
                                    elif hq > 0:
                                        t.wait_ge(SC.h,
                                                  S["se"][(hq - 1, npr - 2 + p)])
                                    lhk = kth[h % 2][:, j * 128:(j + 1) * 128]
                                    t.ldweights(lhk)
                                    i = t.matmul(
                                        sp[:], lhk,
                                        qhh[h % 2][:, qi * 512:(qi + 1) * 512],
                                        start=True, stop=True)
                                    TC.inc(i)
                                    assert TC.n == S["qk"][(hq, p)]
                                    if p >= 1:
                                        emit_pv(t, S, h, qi, p - 1, hq)
                                emit_pv(t, S, h, qi, n - 1, hq)
                                if n < N0[qi]:
                                    t.wait_ge(TC.h, TC.n)
                                    t.nop().then_inc(TC.h, 2 * (N0[qi] - n))
                                    TC.n += 2 * (N0[qi] - n)
                                t.wait_ge(SC.h, S["srb"][hq])
                                if hq >= 1:
                                    t.wait_ge(VC.h, vc_bb(hq - 1))
                                t.ldweights(SMB[0:1, ONESR_C:ONESR_C + 128])
                                i = t.matmul(pB[:],
                                             SMB[0:1, ONESR_C:ONESR_C + 128],
                                             SMB[0:1, RB_C:RB_C + 512],
                                             start=True, stop=True)
                                TC.inc(i)
                                assert TC.n == S["bc"][hq]
                        assert TC.n == p3_tc_end

                @block.scalar
                def _(s):
                    sc_save = SC.n
                    ss_save = SSs.n
                    for case in s.Switch(index=s.snap(ROLE["s"]), n=2):
                        S = SCH[case]
                        SC.n = sc_save
                        SSs.n = ss_save
                        for h in range(NH):
                            for qi in range(4):
                                hq = h * 4 + qi
                                n = S["nvis"][qi]
                                npr = S["nvis"][(qi - 1) % 4] if hq else 0
                                for p in range(n):
                                    j = S["vis"][qi][p]
                                    s.wait_ge(TC.h, S["qk"][(hq, p)])
                                    if p >= 4:
                                        s.wait_ge(TC.h, S["pv"][(hq, p - 4)])
                                    elif hq > 0:
                                        s.wait_ge(TC.h,
                                                  S["pv"][(hq - 1, npr - 4 + p)])
                                    i = s.activation(
                                        ptil[:, (p % 4) * 512:(p % 4 + 1) * 512],
                                        p512[p % 2][:], AF.Exp,
                                        bias=SM[:, CB_C + qi * 32 + j:
                                                CB_C + qi * 32 + j + 1],
                                        scale=RSQ)
                                    SC.inc(i)
                                if n < N0[qi]:
                                    s.wait_ge(SC.h, SC.n)
                                    s.nop().then_inc(SC.h, N0[qi] - n)
                                    SC.n += N0[qi] - n
                                s.wait_ge(VC.h, vc_den(hq))
                                i = s.activation(rden[0:1, :],
                                                 SM[0:1, DEN_C:DEN_C + 512],
                                                 AF.Ln)
                                SSs.inc(i)
                                s.wait_ge(SSs.h, SSs.n)
                                i = s.activation(rden[0:1, :], rden[0:1, :],
                                                 AF.Exp, scale=-1.0)
                                SC.inc(i)
                                s.wait_ge(SC.h, SC.n)
                                i = s.copy(SMB[0:1, RB_C:RB_C + 512],
                                           rden[0:1, :])
                                SC.inc(i)
                        assert SC.n == p3_sc_end
                        assert SSs.n == ss_save + 64

                @block.vector
                def _(v):
                    vc_save = VC.n
                    for case in v.Switch(index=v.snap(ROLE["v"]), n=2):
                        S = SCH[case]
                        VC.n = vc_save
                        for h in range(NH):
                            for qi in range(4):
                                hq = h * 4 + qi
                                n = S["nvis"][qi]
                                for m in range(4):
                                    p = 4 * qi + m
                                    v.wait_ge(SC.h, S["se"][(hq, p)])
                                    pt = ptil[:, (p % 4) * 512:(p % 4 + 1) * 512]
                                    i = v.tensor_mul(pt, pt,
                                                     SMB[:, MK0_C + m * 512:
                                                         MK0_C + (m + 1) * 512])
                                    VC.inc(i)
                                v.wait_ge(TC.h, S["pv"][(hq, n - 1)])
                                i = v.tensor_copy(SM[0:1, DEN_C:DEN_C + 512],
                                                  pD[0:1, :])
                                VC.inc(i)
                                v.wait_ge(TC.h, S["bc"][hq])
                                i = v.tensor_copy(bb16t[:], pB[:])
                                VC.inc(i)
                                v.wait_ge(VC.h, VC.n)
                                i = v.tensor_tensor(
                                    A[:, h * TOWN + qi * 512:
                                      h * TOWN + (qi + 1) * 512],
                                    p512[2 + (qi % 2)][:], bb16t[:], ALU.mult)
                                VC.inc(i)
                        VC.n = vc_end
        p3_vc = VC.n
        p3_wt = WB.n

        if emit_l0 and phase_hi >= 4:
            tc0, vc0 = TC.n, VC.n
            se0 = [sm.n for sm in SE]

            @block.tensor
            def _(t):
                t.wait_ge(WB.h, p3_wt)
                t.wait_ge(VC.h, p3_vc)
                for tb in range(4):
                    for oc in range(NDC):
                        k = tb * 16 + oc
                        pp = p512[k % 4]
                        if k >= 4:
                            t.wait_ge(VC.h, vc0 + k - 3)
                        for dc in range(NDC):
                            lh = Bb[:, (dc * NDC + oc) * 128:
                                    (dc * NDC + oc) * 128 + 128]
                            t.ldweights(lh)
                            i = t.matmul(pp[:], lh,
                                         A[:, dc * TOWN + tb * 512:
                                           dc * TOWN + (tb + 1) * 512],
                                         start=(dc == 0), stop=(dc == NDC - 1))
                        TC.inc(i)

            @block.vector
            def _(v):
                for k in range(64):
                    v.wait_ge(TC.h, tc0 + k + 1)
                    tgt = se0[k % 4] + 16 * (k // 4)
                    if tgt:
                        v.wait_ge(SE[k % 4].h, tgt)
                    i = v.tensor_copy(EV[:, (k % 4) * 512:(k % 4 + 1) * 512],
                                      p512[k % 4][:])
                    VC.inc(i)

            @block.gpsimd
            def _(g):
                for tb in range(4):
                    for oc in range(NDC):
                        k = tb * 16 + oc
                        g.wait_ge(VC.h, vc0 + k + 1)
                        i = g.dma_start(out=sT_sc[oc * 128:(oc + 1) * 128,
                                                  tb * 512:(tb + 1) * 512],
                                        in_=EV[:, (k % 4) * 512:(k % 4 + 1) * 512])
                        SE[k % 4].inc(i, 16)

        if emit_l0 and phase_hi >= 5:
            wg = [Bb[:, 0:8192], Bb[:, 8192:16384]]
            wu = [Bb[:, 16384:24576], Bb[:, 24576:32768]]
            wdl = [Bb[:, 0:8192], Bb[:, 8192:16384]]
            _c56 = ExitStack()
            with _c56:
                C = _c56.enter_context(nc.sbuf_tensor("rhsC5", [128, NDC * 512], BF16))
                gt = _c56.enter_context(nc.sbuf_tensor("gt5", [128, 2 * 512], BF16))
                ut = _c56.enter_context(nc.sbuf_tensor("ut5", [128, 2 * 512], BF16))
                stl = _c56.enter_context(nc.sbuf_tensor("stl5", [128, 2 * 512], BF16))
                dtl = _c56.enter_context(nc.sbuf_tensor("dtl5", [128, 2 * 512], BF16))
                p6_sd_at = []
                for tq in range(4):
                    tc0, sc0, vc0 = TC.n, SC.n, VC.n
                    lr00 = LR[0].n
                    ld_rhs = lr00 + 16 * NDC
                    ls0 = [LS[0].n, LS[1].n]
                    ld_st = lambda oc: ls0[oc % 2] + 16 * (oc // 2 + 1)
                    wg0 = [WG[0].n, WG[1].n]
                    wu0 = [WU[0].n, WU[1].n]
                    wt_ib_g = lambda ib: wg0[ib % 2] + 16 * (ib // 2 + 1)
                    wt_ib_u = lambda ib: wu0[ib % 2] + 16 * (ib // 2 + 1)
                    wt_oc = lambda oc: wg0[oc % 2] + 16 * (8 + oc // 2 + 1)
                    tc_gu = lambda ic, gu: tc0 + ic * 2 + gu + 1
                    tc_oc = lambda oc: tc0 + 128 + oc + 1
                    sc_gu = lambda ic, gu: sc0 + ic * 2 + gu + 1
                    sc_oc = lambda oc: sc0 + 128 + oc + 1
                    vc_f = lambda ic: vc0 + ic + 1
                    vc_oc = lambda oc: vc0 + 64 + oc + 1
                    sd0 = [SD[0].n, SD[1].n]
                    st_oc = lambda oc: sd0[oc % 2] + 16 * (oc // 2 + 1)

                    @block.sync
                    def _(sy, tq=tq, tc0=tc0, vc0=vc0):
                        if tq == 0:
                            wait_stores(sy)
                        else:
                            sy.wait_ge(TC.h, tc0)
                        for dc in range(NDC):
                            i = sy.dma_start(
                                out=C[:, dc * 512:(dc + 1) * 512],
                                in_=xn_tm[tq * 512:(tq + 1) * 512,
                                          dc * 128:(dc + 1) * 128],
                                transpose=True)
                            LR[0].inc(i, 16)
                        for oc in range(NDC):
                            if oc >= 2:
                                sy.wait_ge(VC.h, vc0 + 64 + oc - 1)
                            i = sy.dma_start(
                                out=stl[:, (oc % 2) * 512:(oc % 2) * 512 + 512],
                                in_=sT_sc[oc * 128:(oc + 1) * 128,
                                          tq * 512:(tq + 1) * 512])
                            LS[oc % 2].inc(i, 16)
                            if oc >= 2:
                                po = oc - 2
                                i = sy.dma_start(
                                    out=dT_sc[po * 128:(po + 1) * 128,
                                              tq * 512:(tq + 1) * 512],
                                    in_=dtl[:, (po % 2) * 512:(po % 2) * 512 + 512])
                                SD[po % 2].inc(i, 16)
                        for po in (14, 15):
                            sy.wait_ge(VC.h, vc0 + 64 + po + 1)
                            i = sy.dma_start(
                                out=dT_sc[po * 128:(po + 1) * 128,
                                          tq * 512:(tq + 1) * 512],
                                in_=dtl[:, (po % 2) * 512:(po % 2) * 512 + 512])
                            SD[po % 2].inc(i, 16)
                    p6_sd_at.append((SD[0].n, SD[1].n))

                    @block.gpsimd
                    def _(g, tq=tq, tc0=tc0):
                        for ib in range(16):
                            if ib >= 2:
                                g.wait_ge(TC.h, tc0 + (ib - 1) * 8)
                            elif tc0:
                                g.wait_ge(TC.h, tc0)
                            i = g.dma_start(out=wg[ib % 2],
                                            in_=d_wg[:, ib * 8192:(ib + 1) * 8192])
                            WG[ib % 2].inc(i, 16)
                            i = g.dma_start(out=wu[ib % 2],
                                            in_=d_wu[:, ib * 8192:(ib + 1) * 8192])
                            WU[ib % 2].inc(i, 16)
                        for oc in range(16):
                            if oc < 2:
                                g.wait_ge(TC.h, tc0 + 128)
                            else:
                                g.wait_ge(TC.h, tc0 + 128 + oc - 1)
                            i = g.dma_start(out=wdl[oc % 2],
                                            in_=d_wd[:, oc * 8192:(oc + 1) * 8192])
                            WG[oc % 2].inc(i, 16)

                    @block.tensor
                    def _(t, tq=tq, ld_rhs=ld_rhs, wt_ib_g=wt_ib_g,
                          wt_ib_u=wt_ib_u, wt_oc=wt_oc,
                          sc_gu=sc_gu, sc_oc=sc_oc, vc_f=vc_f):
                        t.wait_ge(LR[0].h, ld_rhs)
                        for ib in range(16):
                            t.wait_ge(WG[ib % 2].h, wt_ib_g(ib))
                            t.wait_ge(WU[ib % 2].h, wt_ib_u(ib))
                            for icc in range(4):
                                ic = ib * 4 + icc
                                pg = p512[ic % 2]
                                pu = p512[2 + ic % 2]
                                if ic >= 2:
                                    t.wait_ge(SC.h, sc_gu(ic - 2, 1))
                                for dc in range(NDC):
                                    lh = wg[ib % 2][:, dc * 512 + icc * 128:
                                                    dc * 512 + icc * 128 + 128]
                                    t.ldweights(lh)
                                    i = t.matmul(pg[:], lh,
                                                 C[:, dc * 512:(dc + 1) * 512],
                                                 start=(dc == 0),
                                                 stop=(dc == NDC - 1))
                                TC.inc(i)
                                for dc in range(NDC):
                                    lh = wu[ib % 2][:, dc * 512 + icc * 128:
                                                    dc * 512 + icc * 128 + 128]
                                    t.ldweights(lh)
                                    i = t.matmul(pu[:], lh,
                                                 C[:, dc * 512:(dc + 1) * 512],
                                                 start=(dc == 0),
                                                 stop=(dc == NDC - 1))
                                TC.inc(i)
                        for oc in range(16):
                            t.wait_ge(WG[oc % 2].h, wt_oc(oc))
                            if oc == 0:
                                t.wait_ge(VC.h, vc_f(63))
                            py = p512[4 + oc % 2]
                            if oc >= 2:
                                t.wait_ge(SC.h, sc_oc(oc - 2))
                            for ic in range(NIC):
                                lh = wdl[oc % 2][:, ic * 128:(ic + 1) * 128]
                                t.ldweights(lh)
                                i = t.matmul(py[:], lh, A[:, ic * 512:(ic + 1) * 512],
                                             start=(ic == 0), stop=(ic == NIC - 1))
                            TC.inc(i)

                    @block.scalar
                    def _(s, tq=tq, tc_gu=tc_gu, tc_oc=tc_oc, vc_f=vc_f,
                          vc_oc=vc_oc):
                        for ic in range(NIC):
                            s.wait_ge(TC.h, tc_gu(ic, 0))
                            s.wait_ge(VC.h, vc_f(ic - 2) if ic >= 2 else vc0)
                            i = s.activation(gt[:, (ic % 2) * 512:(ic % 2) * 512 + 512],
                                             p512[ic % 2][:], AF.Silu,
                                             scale=SM[:, SG_C + ic:SG_C + ic + 1])
                            SC.inc(i)
                            s.wait_ge(TC.h, tc_gu(ic, 1))
                            i = s.activation(ut[:, (ic % 2) * 512:(ic % 2) * 512 + 512],
                                             p512[2 + ic % 2][:], AF.Copy,
                                             scale=SM[:, SU_C + ic:SU_C + ic + 1])
                            SC.inc(i)
                        for oc in range(16):
                            s.wait_ge(TC.h, tc_oc(oc))
                            if oc >= 2:
                                s.wait_ge(VC.h, vc_oc(oc - 2))
                            i = s.activation(gt[:, (oc % 2) * 512:(oc % 2) * 512 + 512],
                                             p512[4 + oc % 2][:], AF.Silu,
                                             scale=SM[:, SD_C + oc:SD_C + oc + 1])
                            SC.inc(i)

                    @block.vector
                    def _(v, tq=tq, sc_gu=sc_gu, sc_oc=sc_oc, ld_st=ld_st,
                          st_oc=st_oc):
                        for ic in range(NIC):
                            v.wait_ge(SC.h, sc_gu(ic, 1))
                            i = v.tensor_mul(A[:, ic * 512:(ic + 1) * 512],
                                             gt[:, (ic % 2) * 512:(ic % 2) * 512 + 512],
                                             ut[:, (ic % 2) * 512:(ic % 2) * 512 + 512])
                            VC.inc(i)
                        for oc in range(16):
                            v.wait_ge(SC.h, sc_oc(oc))
                            v.wait_ge(LS[oc % 2].h, ld_st(oc))
                            tgt = st_oc(oc - 2) if oc >= 2 else sd0[oc % 2]
                            if tgt:
                                v.wait_ge(SD[oc % 2].h, tgt)
                            i = v.tensor_mul(dtl[:, (oc % 2) * 512:(oc % 2) * 512 + 512],
                                             gt[:, (oc % 2) * 512:(oc % 2) * 512 + 512],
                                             stl[:, (oc % 2) * 512:(oc % 2) * 512 + 512])
                            VC.inc(i)


                if phase_hi >= 6:
                    xf6a = _c56.enter_context(nc.sbuf_tensor("xf6a", [128, D], F32))
                    xf6b = _c56.enter_context(nc.sbuf_tensor("xf6b", [128, D], F32))
                    dl6a = _c56.enter_context(nc.sbuf_tensor("dl6a", [128, D], BF16))
                    dl6b = _c56.enter_context(nc.sbuf_tensor("dl6b", [128, D], BF16))
                    xn6a = _c56.enter_context(nc.sbuf_tensor("xn6a", [128, D], BF16))
                    xn6b = _c56.enter_context(nc.sbuf_tensor("xn6b", [128, D], BF16))
                    dum6 = _c56.enter_context(nc.sbuf_tensor("dum6", [128, D], BF16))
                    xf6, dl6 = [xf6a, xf6b], [dl6a, dl6b]
                    vc0 = VC.n
                    lx0 = [LX[0].n, LX[1].n]
                    ld_at6 = {tt: lx0[tt % 2] + 80 * (tt // 2 + 1)
                              for tt in range(16)}
                    vc_pre6 = lambda tt: vc0 + 3 * tt + 1

                    def pre_vec(v, tt):
                        v.wait_ge(LX[tt % 2].h, ld_at6[tt])
                        i = v.tensor_tensor(xf6[tt % 2][:], xf6[tt % 2][:],
                                            dl6[tt % 2][:], ALU.add)
                        VC.inc(i)

                    store_tile = emit_ln(
                        16, None, xn1_tm, 1, xf6, [xn6a, xn6b], (dum6, dum6),
                        pre_vec=pre_vec,
                        store_xrow=lambda tt: x1_tm[tt * 128:(tt + 1) * 128, :])

                    @block.sync
                    def _(sy):
                        for tt in range(17):
                            if tt < 16:
                                tq = tt // 4
                                sy.wait_ge(SD[0].h, p6_sd_at[tq][0])
                                sy.wait_ge(SD[1].h, p6_sd_at[tq][1])
                                if tt >= 2:
                                    sy.wait_ge(VC.h, vc_pre6(tt - 2))
                                    sy.wait_ge(store_tile.slots[tt % 2].h,
                                               store_tile.st_at[tt - 2])
                                i = sy.dma_start(out=xf6[tt % 2][:],
                                                 in_=d_x[tt * 128:(tt + 1) * 128, :])
                                LX[tt % 2].inc(i, 16)
                                for gq in range(4):
                                    i = sy.dma_start(
                                        out=dl6[tt % 2][:, gq * 512:(gq + 1) * 512],
                                        in_=dT_sc[gq * 512:(gq + 1) * 512,
                                                  tt * 128:(tt + 1) * 128],
                                        transpose=True)
                                    LX[tt % 2].inc(i, 16)
                            if tt >= 1:
                                store_tile(sy, tt - 1)

                    @block.sync
                    def _(sy):
                        wait_stores(sy)
                        i = sy.dma_start(
                            out=bnd[:],
                            in_=xn1_tm[TOWN:TOWN + 1, :].rearrange(
                                "one (p c) -> (one p) c", p=128))
                        LSM.inc(i, 16)
                    ld_bnd = LSM.n
                    vc0b = VC.n

                    @block.vector
                    def _(v):
                        v.wait_ge(LSM.h, ld_bnd)
                        i = v.tensor_copy(SM[:, BND_C:BND_C + 16], bnd[:])
                        vedge(v, i)
                        for bp in range(4):
                            i = v.tensor_scalar(
                                SM[:, CCS_C + bp * 16:CCS_C + (bp + 1) * 16],
                                SM[:, BND_C:BND_C + 16],
                                SM[:, WSEL_C + bp:WSEL_C + bp + 1], None, ALU.mult)
                        VC.inc(i)
                    vc_ccs = VC.n

                    @block.gpsimd
                    def _(g):
                        g.wait_ge(VC.h, vc_ccs)
                        i = g.dma_start(out=cc1_in, in_=SM[:, CCS_C:CCS_C + 64])
                        CCD.inc(i, 16)
                        g.wait_ge(CCD.h, CCD.n)
                        i = g.collective_compute(
                            "AllReduce", ALU.add,
                            replica_groups=[[0]] if sim1 else [list(range(8))],
                            ins=[cc1_in.opt()], outs=[cc1_out.opt()])
                        CCA.inc(i, 1)
                        g.wait_ge(CCA.h, 1)
                        i = g.dma_start(out=SM[:, RX_C:RX_C + 64], in_=cc1_out)
                        CCD.inc(i, 16)
                    gc_rx1 = CCD.n

                    @block.vector
                    def _(v):
                        v.wait_ge(CCD.h, gc_rx1)
                        i = v.tensor_scalar(SM[:, ACC_C:ACC_C + 16],
                                            SM[:, RX_C:RX_C + 16],
                                            SM[:, RSEL_C:RSEL_C + 1], None, ALU.mult)
                        vedge(v, i)
                        for bp in range(1, 4):
                            i = v.scalar_tensor_tensor(
                                SM[:, ACC_C:ACC_C + 16],
                                SM[:, RX_C + bp * 16:RX_C + (bp + 1) * 16],
                                SM[:, RSEL_C + bp:RSEL_C + bp + 1],
                                SM[:, ACC_C:ACC_C + 16], ALU.mult, ALU.add)
                            vedge(v, i)
                        i = v.tensor_scalar(bnd[:], SM[:, ACC_C:ACC_C + 16],
                                            SM[:, FLAG_C:FLAG_C + 1], None, ALU.mult)
                        VC.inc(i)
                    vc_bnd = VC.n

                    @block.sync
                    def _(sy):
                        sy.wait_ge(VC.h, vc_bnd)
                        i = sy.dma_start(
                            out=xn1_tm[0:1, :].rearrange("one (p c) -> (one p) c",
                                                         p=128),
                            in_=bnd[:])
                        SX[0].inc(i, 16)
        p6_bnd_st = SX[0].n if (emit_l0 and phase_hi >= 6) else 0

        if emit_l1 and phase_hi >= 7:
            with (
                nc.sbuf_tensor("rhsC7", [128, NDC * 512], BF16) as C,
                nc.sbuf_tensor("rhsD7", [128, NDC * 512], BF16) as Dd,
                nc.sbuf_tensor("abuf", [128, NDC * 128], BF16) as ABUF,
                nc.sbuf_tensor("bwb", [128, D], BF16) as BWB,
                nc.sbuf_tensor("hT", [128, TOWN], BF16) as hT,
                nc.sbuf_tensor("lgt", [128, 8], F32) as lgt,
                nc.sbuf_tensor("mxc", [128, 1], F32) as mxc,
                nc.sbuf_tensor("mk8", [128, 8], BF16) as mk8,
                nc.sbuf_tensor("mkT", [128, 512], BF16) as mkT,
                nc.sbuf_tensor("g1t", [128, 2 * 512], BF16) as g1t,
            ):
                @block.gpsimd
                def _(g):
                    tcn = TC.n
                    if tcn:
                        g.wait_ge(TC.h, tcn)
                    i = g.dma_start(out=ABUF[:], in_=d_Af)
                    WSM.inc(i, 16)
                    i = g.dma_start(out=BWB[:], in_=d_Bw)
                    WSM.inc(i, 16)
                wsm_rdy = WSM.n

                rhs_shift = lambda tb, dc: xn1_tm[tb * 512:(tb + 1) * 512,
                                                  dc * 128:(dc + 1) * 128]
                rhs_plain = lambda tb, dc: xn1_tm[1 + tb * 512:1 + (tb + 1) * 512,
                                                  dc * 128:(dc + 1) * 128]

                def p6_gate(tb):
                    if not (emit_l0 and phase_hi >= 6):
                        return []
                    if tb == 0:
                        return [(SX[0], p6_bnd_st)]
                    return [(store_tile.slots[1], store_tile.st_at[15]),
                            (store_tile.slots[0], store_tile.st_at[14])]

                wk1_rdy = load_weight(A, d_wkT1, WA)
                emit_proj(A, WA, wk1_rdy, rhs_shift, 4,
                          lambda oc, tb: kT1_sc[oc * 128:(oc + 1) * 128,
                                                tb * 512:(tb + 1) * 512], (C, Dd),
                          rhs_gate=p6_gate, tb_order=[1, 2, 3, 0])
                wv1_rdy = load_weight(Bb, d_wvT1, WB)
                emit_proj(Bb, WB, wv1_rdy, rhs_shift, 4,
                          lambda oc, tb: vT1_sc[oc * 128:(oc + 1) * 128,
                                                tb * 512:(tb + 1) * 512], (C, Dd),
                          rhs_gate=p6_gate, tb_order=[1, 2, 3, 0],
                          prev_proj=True)
                wr1_rdy = load_weight(A, d_wrT1, WA)
                emit_proj(A, WA, wr1_rdy, rhs_plain, 4,
                          lambda oc, tb: rT1_sc[oc * 128:(oc + 1) * 128,
                                                tb * 512:(tb + 1) * 512], (C, Dd),
                          prev_proj=True)
                wb_rdy = load_weight(Bb, d_wbT, WB)

                @block.vector
                def _(v):
                    for sm in SE:
                        if sm.n:
                            v.wait_ge(sm.h, sm.n)
                    i = v.memset(EV[:], 0.0)
                    VC.inc(i)
                vc_z = VC.n

                @block.gpsimd
                def _(g):
                    g.wait_ge(VC.h, vc_z)
                    i = g.dma_start(
                        out=mk_sc.rearrange("(a p) c -> p a c", p=128),
                        in_=EV.rearrange("p (a c) -> p a c", c=128))
                    SMK.inc(i, 16)

                prev_vh = None
                prev_vhm = None
                prev_tse = None
                for tb in range(4):
                    tc0, vc0, sc0 = TC.n, VC.n, SC.n
                    rb = C if tb % 2 == 0 else Dd
                    t_h = tc0 + 1
                    t_lg = lambda tc4: tc0 + 2 + tc4
                    t_se = tc0 + 6
                    t_base = lambda oc: tc0 + 6 + oc + 1
                    v_h = vc0 + 1
                    v_mk = lambda tc4: vc0 + 2 + tc4
                    v_hm = vc0 + 6
                    smk0 = SMK.n
                    st_mk = lambda tc4: smk0 + 16 * (tc4 + 1)
                    sg0 = [SG[0].n, SG[1].n]
                    st_g1 = lambda oc: sg0[oc % 2] + 16 * (oc // 2 + 1)
                    sc_g1 = lambda oc: sc0 + oc + 1

                    @block.sync
                    def _(sy, tb=tb, rb=rb, tc0=tc0):
                        sy.wait_ge(TC.h, tc0 - 16 if tb == 0 else tc0)
                        for dc in range(NDC):
                            i = sy.dma_start(out=rb[:, dc * 512:(dc + 1) * 512],
                                             in_=rhs_plain(tb, dc), transpose=True)
                            LR[tb % 2].inc(i, 16)
                    ld_rhs = LR[tb % 2].n

                    @block.sync
                    def _(sy, tb=tb, st_mk=st_mk, prev_tse=prev_tse):
                        sy.wait_ge(SMK.h, st_mk(3))
                        if prev_tse is not None:
                            sy.wait_ge(TC.h, prev_tse)
                        i = sy.dma_start(out=mkT[:],
                                         in_=mk_sc[tb * 512:(tb + 1) * 512, :],
                                         transpose=True)
                        LMK.inc(i, 16)
                    ld_mk = LMK.n

                    @block.tensor
                    def _(t, tb=tb, rb=rb, ld_rhs=ld_rhs, ld_mk=ld_mk, tc0=tc0,
                          v_mk=v_mk, v_hm=v_hm, sc_g1=sc_g1,
                          prev_vh=prev_vh, prev_vhm=prev_vhm):
                        t.wait_ge(LR[tb % 2].h, ld_rhs)
                        t.wait_ge(WB.h, wb_rdy)
                        t.wait_ge(WSM.h, wsm_rdy)
                        if prev_vh is not None:
                            t.wait_ge(VC.h, prev_vh)
                        for dc in range(NDC):
                            lh = ABUF[:, dc * 128:(dc + 1) * 128]
                            t.ldweights(lh)
                            i = t.matmul(p512[0][:], lh,
                                         rb[:, dc * 512:(dc + 1) * 512],
                                         start=(dc == 0), stop=(dc == NDC - 1))
                        TC.inc(i)
                        for tc4 in range(4):
                            if tc4 >= 1:
                                t.wait_ge(VC.h, v_mk(tc4 - 1))
                            for dc in range(NDC):
                                lh = rb[:, dc * 512 + tc4 * 128:
                                        dc * 512 + tc4 * 128 + 128]
                                t.ldweights(lh)
                                i = t.matmul(p512[1][0:128, 0:8], lh,
                                             RWB[:, dc * 8:(dc + 1) * 8],
                                             start=(dc == 0), stop=(dc == NDC - 1))
                            TC.inc(i)
                        t.wait_ge(LMK.h, ld_mk)
                        if prev_vhm is not None:
                            t.wait_ge(VC.h, prev_vhm)
                        t.ldweights(R8[0:8, 0:128])
                        i = t.matmul(p512[2][:], R8[0:8, 0:128], mkT[0:8, :],
                                     start=True, stop=True)
                        TC.inc(i)
                        t.wait_ge(VC.h, v_hm)
                        for oc in range(NDC):
                            po = p512[3 + oc % 2]
                            if oc >= 2:
                                t.wait_ge(SC.h, sc_g1(oc - 2))
                            for dc in range(NDC):
                                lh = Bb[:, (dc * NDC + oc) * 128:
                                        (dc * NDC + oc) * 128 + 128]
                                t.ldweights(lh)
                                t.matmul(po[:], lh, rb[:, dc * 512:(dc + 1) * 512],
                                         start=(dc == 0), stop=False)
                            lh = BWB[:, oc * 128:(oc + 1) * 128]
                            t.ldweights(lh)
                            i = t.matmul(po[:], lh, hT[:, tb * 512:(tb + 1) * 512],
                                         start=False, stop=True)
                            TC.inc(i)

                    @block.vector
                    def _(v, tb=tb, tc0=tc0, t_h=t_h, t_lg=t_lg, t_se=t_se,
                          st_mk=st_mk):
                        v.wait_ge(TC.h, t_h)
                        i = v.tensor_copy(hT[:, tb * 512:(tb + 1) * 512],
                                          p512[0][:])
                        VC.inc(i)
                        for tc4 in range(4):
                            v.wait_ge(TC.h, t_lg(tc4))
                            tgt = st_mk(tc4 - 1) if tc4 >= 1 else smk0
                            if tgt:
                                v.wait_ge(SMK.h, tgt)
                            i = v.tensor_copy(lgt[:], p512[1][0:128, 0:8])
                            vedge(v, i)
                            i = v.reduce_max(mxc[:], lgt[:], axis=AXX)
                            vedge(v, i)
                            i = v.tensor_single_scalar(mk8[:], lgt[:],
                                                       mxc[:, 0:1], ALU.is_ge)
                            VC.inc(i)
                        v.wait_ge(TC.h, t_se)
                        i = v.tensor_tensor(hT[:, tb * 512:(tb + 1) * 512],
                                            hT[:, tb * 512:(tb + 1) * 512],
                                            p512[2][:], ALU.mult)
                        VC.inc(i)

                    @block.scalar
                    def _(s, tb=tb, t_base=t_base, st_g1=st_g1):
                        for oc in range(NDC):
                            s.wait_ge(TC.h, t_base(oc))
                            tgt = st_g1(oc - 2) if oc >= 2 else sg0[oc % 2]
                            if tgt:
                                s.wait_ge(SG[oc % 2].h, tgt)
                            i = s.activation(
                                g1t[:, (oc % 2) * 512:(oc % 2) * 512 + 512],
                                p512[3 + oc % 2][:], AF.Silu,
                                bias=SM[:, BB_C + oc:BB_C + oc + 1])
                            SC.inc(i)

                    @block.gpsimd
                    def _(g, tb=tb, v_mk=v_mk, sc_g1=sc_g1):
                        for tc4 in range(4):
                            g.wait_ge(VC.h, v_mk(tc4))
                            i = g.dma_start(
                                out=mk_sc[tb * 512 + tc4 * 128:
                                          tb * 512 + (tc4 + 1) * 128, 0:8],
                                in_=mk8[:])
                            SMK.inc(i, 16)
                        for oc in range(NDC):
                            g.wait_ge(SC.h, sc_g1(oc))
                            i = g.dma_start(
                                out=g1T_sc[oc * 128:(oc + 1) * 128,
                                           tb * 512:(tb + 1) * 512],
                                in_=g1t[:, (oc % 2) * 512:(oc % 2) * 512 + 512])
                            SG[oc % 2].inc(i, 16)

                    prev_vh = v_h
                    prev_vhm = v_hm
                    prev_tse = t_se

        if emit_l1 and phase_hi >= 8:
            HT = TOWN // 2
            with (
                nc.sbuf_tensor("ktl", [128, TOWN], BF16) as ktl,
                nc.sbuf_tensor("vtl", [128, TOWN], BF16) as vtl,
                nc.sbuf_tensor("rtl", [128, TOWN], BF16) as rtl,
                nc.sbuf_tensor("gtl", [128, TOWN], BF16) as gtl,
                nc.sbuf_tensor("ekf", [128, HT], F32) as ekf,
                nc.sbuf_tensor("avf", [128, HT], F32) as avf,
                nc.sbuf_tensor("n0f", [128, HT], F32) as n0f,
                nc.sbuf_tensor("d0f", [128, HT], F32) as d0f,
                nc.sbuf_tensor("wrp", [128, HT], F32) as wrp,
                nc.sbuf_tensor("sth", [128, 4], F32) as sth,
                nc.sbuf_tensor("dta", [128, TOWN], BF16) as dta,
                nc.sbuf_tensor("dtb", [128, TOWN], BF16) as dtb,
            ):
                douts = [dta, dtb]

                @block.scalar
                def _(s):
                    s.wait_ge(LSM.h, p0_ld)
                    i = s.activation(SM[:, ECOL_C:ECOL_C + 16],
                                      SM[:, TD_C:TD_C + 16], AF.Exp)
                    sedge(s, i)
                    i = s.activation(SM[:, NECOL_C:NECOL_C + 16],
                                     SM[:, ECOL_C:ECOL_C + 16], AF.Copy,
                                     scale=-1.0)
                    sedge(s, i)
                    i = s.activation(SM[:, WCOL_C:WCOL_C + 16],
                                     SM[:, ECOL_C:ECOL_C + 16], AF.Exp, scale=-1.0)
                    SC.inc(i)
                sc_w = SC.n

                ld0, sc0, vc0 = L8.n, SC.n, VC.n
                ldA = lambda dc: ld0 + 32 * (dc + 1)
                scA = lambda dc, k: sc0 + 3 * dc + k
                vcA = lambda dc, k: vc0 + 6 * dc + k

                @block.sync
                def _(sy):
                    wait_stores(sy)
                    for dc in range(NDC):
                        if dc >= 1:
                            sy.wait_ge(VC.h, vcA(dc - 1, 6))
                        i = sy.dma_start(out=ktl[:],
                                         in_=kT1_sc[dc * 128:(dc + 1) * 128, :])
                        L8.inc(i, 16)
                        i = sy.dma_start(out=vtl[:],
                                         in_=vT1_sc[dc * 128:(dc + 1) * 128, :])
                        L8.inc(i, 16)

                @block.scalar
                def _(s):
                    s.wait_ge(SC.h, sc_w)
                    for dc in range(NDC):
                        s.wait_ge(L8.h, ldA(dc))
                        if dc >= 1:
                            s.wait_ge(VC.h, vcA(dc - 1, 3))
                        i = s.activation(ekf[:], ktl[:, 0:HT], AF.Exp)
                        SC.inc(i)
                        i = s.activation(wrp[:], ktl[:, 0:HT], AF.Exp, scale=0.0,
                                         bias=SM[:, NECOL_C + dc:NECOL_C + dc + 1])
                        SC.inc(i)
                        s.wait_ge(VC.h, vcA(dc, 3))
                        i = s.activation(ekf[:], ktl[:, HT:TOWN], AF.Exp)
                        SC.inc(i)

                @block.vector
                def _(v):
                    for dc in range(NDC):
                        v.wait_ge(SC.h, scA(dc, 1))
                        v.wait_ge(L8.h, ldA(dc))
                        i = v.tensor_tensor(avf[:], ekf[:], vtl[:, 0:HT], ALU.mult)
                        VC.inc(i)
                        v.wait_ge(VC.h, VC.n)
                        v.wait_ge(SC.h, scA(dc, 2))
                        i = v.tensor_tensor_scan(n0f[:], wrp[:], avf[:], 0.0,
                                                 ALU.mult, ALU.add)
                        VC.inc(i)
                        i = v.tensor_tensor_scan(d0f[:], wrp[:], ekf[:], 0.0,
                                                 ALU.mult, ALU.add)
                        vedge(v, i)
                        v.wait_ge(VC.h, VC.n)
                        v.tensor_copy(sth[:, 0:1], n0f[:, HT - 1:HT])
                        i = v.tensor_copy(sth[:, 1:2], d0f[:, HT - 1:HT])
                        VC.inc(i)
                        v.wait_ge(VC.h, VC.n)
                        v.wait_ge(SC.h, scA(dc, 3))
                        i = v.tensor_tensor(avf[:], ekf[:], vtl[:, HT:TOWN],
                                            ALU.mult)
                        VC.inc(i)
                        v.wait_ge(VC.h, VC.n)
                        i = v.tensor_tensor_scan(n0f[:], wrp[:], avf[:],
                                                 sth[:, 0:1], ALU.mult, ALU.add)
                        VC.inc(i)
                        v.wait_ge(VC.h, VC.n)
                        i = v.tensor_tensor_scan(d0f[:], wrp[:], ekf[:],
                                                 sth[:, 1:2], ALU.mult, ALU.add)
                        vedge(v, i)
                        v.tensor_copy(SM[:, STN_C + dc:STN_C + dc + 1],
                                      n0f[:, HT - 1:HT])
                        i = v.tensor_copy(SM[:, STD_C + dc:STD_C + dc + 1],
                                          d0f[:, HT - 1:HT])
                        VC.inc(i)

                @block.vector
                def _(v):
                    v.wait_ge(VC.h, VC.n)
                    i = None
                    for bp in range(4):
                        i = v.tensor_scalar(
                            SM[:, CC2S_C + bp * 16:CC2S_C + (bp + 1) * 16],
                            SM[:, STN_C:STN_C + 16],
                            SM[:, WSEL_C + bp:WSEL_C + bp + 1], None, ALU.mult)
                        i = v.tensor_scalar(
                            SM[:, CC2S_C + 64 + bp * 16:CC2S_C + 64 + (bp + 1) * 16],
                            SM[:, STD_C:STD_C + 16],
                            SM[:, WSEL_C + bp:WSEL_C + bp + 1], None, ALU.mult)
                    VC.inc(i)
                vc_cc2 = VC.n

                @block.gpsimd
                def _(g):
                    g.wait_ge(VC.h, vc_cc2)
                    i = g.dma_start(out=cc2_in, in_=SM[:, CC2S_C:CC2S_C + 128])
                    CCD.inc(i, 16)
                    g.wait_ge(CCD.h, CCD.n)
                    i = g.collective_compute(
                        "AllReduce", ALU.add,
                        replica_groups=[[0]] if sim1 else [list(range(8))],
                        ins=[cc2_in.opt()], outs=[cc2_out.opt()])
                    CCB.inc(i, 1)
                    g.wait_ge(CCB.h, 1)
                    i = g.dma_start(out=SM[:, RX2_C:RX2_C + 128], in_=cc2_out)
                    CCD.inc(i, 16)
                gc_rx2 = CCD.n

                @block.vector
                def _(v):
                    v.wait_ge(CCD.h, gc_rx2)
                    for (dstc, base) in ((RN_C, RX2_C), (RD_C, RX2_C + 64)):
                        i = v.tensor_scalar(SM[:, dstc:dstc + 16],
                                            SM[:, base:base + 16],
                                            SM[:, RSEL_C:RSEL_C + 1], None,
                                            ALU.mult)
                        vedge(v, i)
                        for bp in range(1, 4):
                            i = v.scalar_tensor_tensor(
                                SM[:, dstc:dstc + 16],
                                SM[:, base + bp * 16:base + (bp + 1) * 16],
                                SM[:, RSEL_C + bp:RSEL_C + bp + 1],
                                SM[:, dstc:dstc + 16], ALU.mult, ALU.add)
                            vedge(v, i)
                        i = v.tensor_scalar(SM[:, dstc:dstc + 16],
                                            SM[:, dstc:dstc + 16],
                                            SM[:, FLAG_C:FLAG_C + 1], None,
                                            ALU.mult)
                        VC.inc(i)
                        v.wait_ge(VC.h, VC.n)

                ld1, sc1, vc1 = L8.n, SC.n, VC.n
                so0 = [SO[0].n, SO[1].n]
                ldB = lambda dc: ld1 + 64 * (dc + 1)
                scB = lambda dc, hf, k: sc1 + 6 * dc + 3 * hf + k
                vcB = lambda dc, hf, k: vc1 + 8 * dc + 4 * hf + k
                stB = lambda dc: so0[dc % 2] + 16 * (dc // 2 + 1)

                @block.sync
                def _(sy):
                    sy.wait_ge(VC.h, vc1)
                    for dc in range(NDC):
                        if dc >= 1:
                            sy.wait_ge(VC.h, vcB(dc - 1, 1, 4))
                        i = sy.dma_start(out=ktl[:],
                                         in_=kT1_sc[dc * 128:(dc + 1) * 128, :])
                        L8.inc(i, 16)
                        i = sy.dma_start(out=vtl[:],
                                         in_=vT1_sc[dc * 128:(dc + 1) * 128, :])
                        L8.inc(i, 16)
                        i = sy.dma_start(out=rtl[:],
                                         in_=rT1_sc[dc * 128:(dc + 1) * 128, :])
                        L8.inc(i, 16)
                        i = sy.dma_start(out=gtl[:],
                                         in_=g1T_sc[dc * 128:(dc + 1) * 128, :])
                        L8.inc(i, 16)

                @block.scalar
                def _(s):
                    for dc in range(NDC):
                        s.wait_ge(L8.h, ldB(dc))
                        for hf in range(2):
                            sl = slice(hf * HT, (hf + 1) * HT)
                            if dc + hf > 0:
                                prev = vcB(dc, 0, 3) if hf else vcB(dc - 1, 1, 3)
                                s.wait_ge(VC.h, prev)
                            i = s.activation(ekf[:], ktl[:, sl], AF.Exp)
                            SC.inc(i)
                            i = s.activation(wrp[:], ktl[:, sl], AF.Exp,
                                             scale=0.0,
                                             bias=SM[:, NECOL_C + dc:NECOL_C + dc + 1])
                            SC.inc(i)
                            s.wait_ge(VC.h, vcB(dc, hf, 3))
                            i = s.activation(d0f[:], d0f[:], AF.Ln,
                                             bias=SM[0:128, EPS2_C:EPS2_C + 1])
                            sedge(s, i)
                            i = s.activation(d0f[:], d0f[:], AF.Exp, scale=-1.0)
                            SC.inc(i)

                @block.vector
                def _(v):
                    for dc in range(NDC):
                        for hf in range(2):
                            sl = slice(hf * HT, (hf + 1) * HT)
                            v.wait_ge(SC.h, scB(dc, hf, 1))
                            i = v.tensor_tensor(avf[:], ekf[:], vtl[:, sl],
                                                ALU.mult)
                            VC.inc(i)
                            v.wait_ge(VC.h, VC.n)
                            v.wait_ge(SC.h, scB(dc, hf, 2))
                            ini_n = (SM[:, RN_C + dc:RN_C + dc + 1] if hf == 0
                                     else sth[:, 2:3])
                            ini_d = (SM[:, RD_C + dc:RD_C + dc + 1] if hf == 0
                                     else sth[:, 3:4])
                            i = v.tensor_tensor_scan(n0f[:], wrp[:], avf[:],
                                                     ini_n, ALU.mult, ALU.add)
                            VC.inc(i)
                            v.wait_ge(VC.h, VC.n)
                            if hf == 0:
                                i = v.tensor_copy(sth[:, 2:3], n0f[:, HT - 1:HT])
                                vedge(v, i)
                            i = v.tensor_tensor_scan(d0f[:], wrp[:], ekf[:],
                                                     ini_d, ALU.mult, ALU.add)
                            if hf == 0:
                                vedge(v, i)
                                i = v.tensor_copy(sth[:, 3:4], d0f[:, HT - 1:HT])
                            VC.inc(i)
                            v.wait_ge(VC.h, VC.n)
                            v.wait_ge(SC.h, scB(dc, hf, 3))
                            i = v.tensor_tensor(n0f[:], n0f[:], d0f[:], ALU.mult)
                            vedge(v, i)
                            i = v.tensor_tensor(n0f[:], n0f[:], rtl[:, sl],
                                                ALU.mult)
                            vedge(v, i)
                            if dc >= 2 and hf == 0:
                                v.wait_ge(SO[dc % 2].h, stB(dc - 2))
                            i = v.tensor_tensor(douts[dc % 2][:, sl], n0f[:],
                                                gtl[:, sl], ALU.mult)
                            VC.inc(i)

                @block.gpsimd
                def _(g):
                    for dc in range(NDC):
                        g.wait_ge(VC.h, vcB(dc, 1, 4))
                        i = g.dma_start(out=dT_sc[dc * 128:(dc + 1) * 128, :],
                                        in_=douts[dc % 2][:])
                        SO[dc % 2].inc(i, 16)

        if emit_l1 and phase_hi >= 9:
            with (
                nc.sbuf_tensor("xr9a", [128, D], F32) as xr9a,
                nc.sbuf_tensor("xr9b", [128, D], F32) as xr9b,
                nc.sbuf_tensor("dl9a", [128, D], BF16) as dl9a,
                nc.sbuf_tensor("dl9b", [128, D], BF16) as dl9b,
                nc.sbuf_tensor("o9a", [128, D], F32) as o9a,
                nc.sbuf_tensor("o9b", [128, D], F32) as o9b,
            ):
                xr9, dl9, o9 = [xr9a, xr9b], [dl9a, dl9b], [o9a, o9b]
                vc0 = VC.n
                lx0 = [LX[0].n, LX[1].n]
                ld9 = lambda tt: lx0[tt % 2] + 80 * (tt // 2 + 1)
                os0 = [OS[0].n, OS[1].n]
                os_at = lambda tt: os0[tt % 2] + 16 * (tt // 2 + 1)

                @block.sync
                def _(sy):
                    wait_stores(sy)
                    for tt in range(16):
                        if tt >= 2:
                            sy.wait_ge(VC.h, vc0 + tt - 1)
                        i = sy.dma_start(out=xr9[tt % 2][:],
                                         in_=x1_tm[tt * 128:(tt + 1) * 128, :])
                        LX[tt % 2].inc(i, 16)
                        for gq in range(4):
                            i = sy.dma_start(
                                out=dl9[tt % 2][:, gq * 512:(gq + 1) * 512],
                                in_=dT_sc[gq * 512:(gq + 1) * 512,
                                          tt * 128:(tt + 1) * 128],
                                transpose=True)
                            LX[tt % 2].inc(i, 16)

                @block.vector
                def _(v):
                    for tt in range(16):
                        v.wait_ge(LX[tt % 2].h, ld9(tt))
                        if tt >= 2:
                            v.wait_ge(OS[tt % 2].h, os_at(tt - 2))
                        i = v.tensor_tensor(o9[tt % 2][:], xr9[tt % 2][:],
                                            dl9[tt % 2][:], ALU.add)
                        VC.inc(i)

                @block.gpsimd
                def _(g):
                    for tt in range(16):
                        g.wait_ge(VC.h, vc0 + tt + 1)
                        i = g.dma_start(out=out_x2[tt * 128:(tt + 1) * 128, :],
                                        in_=o9[tt % 2][:])
                        OS[tt % 2].inc(i, 16)

        @block.sync
        def _(sy):
            for sm in ALL_DMA + [TC, VC, SC, GC, CCA, CCB, CCD, VSs, SSs]:
                if sm.n:
                    sy.wait_ge(sm.h, sm.n)

    return nc


def _prep_weights(inp):
    t = lambda a: np.ascontiguousarray(np.asarray(a))
    w = {}

    def chunk_T(m):
        Dm, O = m.shape
        return t(m.reshape(Dm // 128, 128, O).transpose(1, 0, 2).reshape(128, -1))

    w["wqT"] = chunk_T(t(np.asarray(inp["L0_attn_wq"]).T))
    w["wkT"] = chunk_T(t(np.asarray(inp["L0_attn_wk"]).T))
    w["wvT"] = chunk_T(t(np.asarray(inp["L0_attn_wv"]).T))
    w["woT"] = chunk_T(t(np.asarray(inp["L0_attn_wo"]).T))
    for nm, key in (("w_gate", "L0_ffn_w_gate"), ("w_up", "L0_ffn_w_up")):
        m = np.asarray(inp[key])
        w[nm] = t(m.reshape(16, 128, 16, 512).transpose(1, 2, 0, 3).reshape(128, -1))
    md = np.asarray(inp["L0_ffn_w_down"])
    w["w_down"] = t(md.reshape(64, 128, 16, 128).transpose(1, 2, 0, 3).reshape(128, -1))
    w["sgate_c"] = t(np.asarray(inp["L0_ffn_s_gate"]).reshape(64, 128).T)
    w["sup_c"] = t(np.asarray(inp["L0_ffn_s_up"]).reshape(64, 128).T)
    w["sdown_c"] = t(np.asarray(inp["L0_ffn_s_down"]).reshape(16, 128).T)
    w["wkT1"] = chunk_T(t(np.asarray(inp["L1_attn_Wk"]).T))
    w["wvT1"] = chunk_T(t(np.asarray(inp["L1_attn_Wv"]).T))
    w["wrT1"] = chunk_T(t(np.asarray(inp["L1_attn_Wr"]).T))
    w["wbT"] = chunk_T(t(np.asarray(inp["L1_ffn_W_base"]).T))
    w["td_c"] = t(np.asarray(inp["L1_attn_time_decay"]).reshape(16, 128).T)
    w["bb_c"] = t(np.asarray(inp["L1_ffn_b_base"]).reshape(16, 128).T)
    w["rwT"] = chunk_T(t(np.asarray(inp["L1_ffn_router_W"]).T))
    w["AfT"] = chunk_T(t(np.asarray(inp["L1_ffn_A"]).reshape(E * R, D).T))
    w["BwF"] = t(np.asarray(inp["L1_ffn_Bw"]).transpose(0, 2, 1).reshape(E * R, D))
    sel = np.zeros((8, 128), np.float32)
    for e in range(E):
        sel[e, e * R:(e + 1) * R] = 2.0
    w["selm"] = sel
    return w


def _per_core(inp, b, h):
    x = np.asarray(inp["L0_attn_x"])
    own = x[b, h * TOWN:(h + 1) * TOWN]
    oth = x[b, (1 - h) * TOWN:(2 - h) * TOWN]
    m = {"x_in": np.ascontiguousarray(np.concatenate([own, oth], axis=0)),
         "flagi": np.array([[1 - h]], np.int32)}
    cb = np.zeros((128, 128), np.float32)
    for qi in range(4):
        for j in range(32):
            if j < 16:
                v = SHIFT if j <= 4 * qi + 3 else NEG
            else:
                v = SHIFT if h == 1 else NEG
            cb[:, qi * 32 + j] = v
    m["cb"] = cb
    m["flag"] = np.full((128, 1), float(h), np.float32)
    ws = np.zeros((128, 4), np.float32)
    ws[:, b] = 1.0 - h
    m["wsel"] = ws
    rs = np.zeros((128, 4), np.float32)
    rs[:, b] = 1.0
    m["rsel"] = rs
    return m


def kernel(**inputs):
    w = _prep_weights(inputs)
    in_maps = []
    for c in range(8):
        b, h = c // 2, c % 2
        m = dict(w)
        m.update(_per_core(inputs, b, h))
        in_maps.append(m)
    nc = build(phase_hi=PHASE_HI, dbg=DBG, part=PART)
    res = run_bass_kernel_spmd(nc, in_maps, core_ids=list(range(8)))
    out = np.empty((B, T, D), np.float32)
    for c in range(8):
        b, h = c // 2, c % 2
        out[b, h * TOWN:(h + 1) * TOWN] = np.asarray(res.results[c]["out_x2"])
    return out
